# version 03
# baseline (speedup 1.0000x reference)
import math
from contextlib import ExitStack

import numpy as np
import ml_dtypes

import concourse.bass as bass
import concourse.mybir as mybir
from concourse.bass_utils import run_bass_kernel_spmd

F32 = mybir.dt.float32
BF16 = mybir.dt.bfloat16
AF = mybir.ActivationFunctionType
ALU = mybir.AluOpType
AX = mybir.AxisListType.X
BF = ml_dtypes.bfloat16

NCORE = 8
B, T, C = 2, 8192, 1024
G = T * B // NCORE
KCH = C // 128

ENGINES = ("sync", "tensor", "vector", "scalar", "gpsimd")


class Sem:
    __slots__ = ("h", "n", "name")

    def __init__(self, h, name):
        self.h = h
        self.n = 0
        self.name = name

    def bump(self, k):
        self.n += k
        return self.n


class Prog:
    def __init__(self, nc, stack):
        self.nc = nc
        self.stack = stack
        self.ops = {e: [] for e in ENGINES}
        self.kinds = {e: [] for e in ENGINES}
        self.meta = {e: [] for e in ENGINES}
        self.all_sems = []
        self._semcnt = 0

    def sem(self, name=None):
        name = f"{name or 's'}_{self._semcnt}"
        self._semcnt += 1
        h = self.stack.enter_context(self.nc.semaphore(name))
        s = Sem(h, name)
        self.all_sems.append(s)
        return s

    def op(self, eng, method, *args, inc=(), lowprec=False, **kw):
        inc = tuple(inc)

        def fn(e, _m=method, _a=args, _k=kw, _inc=inc, _lp=lowprec):
            if _lp:
                with self.nc.allow_low_precision(reason="bf16 out ok"):
                    ins = getattr(e, _m)(*_a, **_k)
            else:
                ins = getattr(e, _m)(*_a, **_k)
            for s, k in _inc:
                ins = ins.then_inc(s.h, k)
            return ins

        kind = "dma" if method in ("dma_start", "collective_compute") else "op"
        for s, k in inc:
            s.bump(k)
        self.ops[eng].append(fn)
        self.kinds[eng].append(kind)
        self.meta[eng].append([(s, s.n) for s, k in inc])

    def cond_dma(self, eng, *, out, in_, jval, inc=()):
        inc = tuple(inc)
        for s, k in inc:
            s.bump(k)
        et = {"sync": mybir.EngineType.SP, "gpsimd": mybir.EngineType.Pool,
              "scalar": mybir.EngineType.Activation}[eng]
        if not hasattr(self, "_pid"):
            self._pid = {}

        def fn(e, _o=out, _i=in_, _j=jval, _inc=inc, _et=et):
            if _et not in self._pid:
                self._pid[_et] = self.nc.partition_id((_et,))
            pid = self._pid[_et]
            from concourse.expressions import s_logical_or
            cond = s_logical_or(pid == _j, pid == _j + 4)
            ins = e.dma_start(out=_o, in_=_i, cond=cond)
            for s, k in _inc:
                ins = ins.then_inc(s.h, k)
            return ins

        self.ops[eng].append(fn)
        self.kinds[eng].append("dma")
        self.meta[eng].append([(s, s.n) for s, k in inc])

    def wait(self, eng, sem, val=None):
        v = sem.n if val is None else val
        self.ops[eng].append(lambda e, _s=sem.h, _v=v: e.wait_ge(_s, _v))
        self.kinds[eng].append("wait")
        self.meta[eng].append(None)

    def last_op_inc(self, eng):
        for i in range(len(self.ops[eng]) - 1, -1, -1):
            if self.kinds[eng][i] == "wait":
                continue
            if self.kinds[eng][i] != "op":
                return None
            m = self.meta[eng][i]
            return m[-1] if m else None
        return None

    def mark(self, eng, sem, k=1):
        stack = []
        while self.kinds[eng] and self.kinds[eng][-1] != "op":
            stack.append((self.ops[eng].pop(), self.kinds[eng].pop(),
                          self.meta[eng].pop()))
        assert self.kinds[eng], f"no compute op to mark on {eng}"
        fn = self.ops[eng].pop()
        meta = self.meta[eng].pop()
        self.kinds[eng].pop()

        def wrapped(e, _fn=fn, _s=sem, _k=k):
            ins = _fn(e)
            return ins.then_inc(_s.h, _k)

        sem.bump(k)
        self.ops[eng].append(wrapped)
        self.kinds[eng].append("op")
        self.meta[eng].append((meta or []) + [(sem, sem.n)])
        for f, kk, mmx in reversed(stack):
            self.ops[eng].append(f)
            self.kinds[eng].append(kk)
            self.meta[eng].append(mmx)

    def selfedge(self, eng, sem):
        last = self.last_op_inc(eng)
        if last is None:
            self.mark(eng, sem, 1)
            last = (sem, sem.n)
        self.wait(eng, last[0], last[1])

    def build(self):
        with self.nc.Block() as block:
            for e in ENGINES:
                if not self.ops[e]:
                    continue

                def section(eng, _e=e):
                    for fn in self.ops[_e]:
                        fn(eng)

                getattr(block, e)(section)


class R:

    def __init__(self, nc, stack):
        self.nc = nc
        self.stacks = [stack]
        self.tags = [""]
        self.sb_bytes = 0
        self.scope_bytes = [0]
        self.names = set()

    def push(self, tag=""):
        st = ExitStack()
        self.stacks.append(st)
        self.tags.append(tag)
        self.scope_bytes.append(0)
        return st

    def pop(self):
        self.tags.pop()
        self.sb_bytes -= self.scope_bytes.pop()
        self.stacks.pop().close()

    def sb(self, name, shape, dtype):
        name = self.tags[-1] + name
        assert name not in self.names, name
        self.names.add(name)
        nbytes = shape[1] * (2 if dtype == BF16 else 4)
        self.sb_bytes += nbytes * 128
        self.scope_bytes[-1] += nbytes * 128
        t = self.stacks[-1].enter_context(
            self.nc.sbuf_tensor(name, list(shape), dtype))
        return t

    def ps(self, name, shape, dtype=F32):
        name = self.tags[-1] + name
        assert name not in self.names, name
        self.names.add(name)
        return self.stacks[-1].enter_context(
            self.nc.psum_tensor(name, list(shape), dtype))


class PsumRing:

    def __init__(self, P, rr, name, n, shape=(128, 512), dtype=F32):
        self.P = P
        self.tiles = [rr.ps(f"{name}{i}", list(shape), dtype) for i in range(n)]
        self.free_at = [None] * n
        self.i = 0

    def grab(self):
        i = self.i
        self.i = (self.i + 1) % len(self.tiles)
        if self.free_at[i] is not None:
            s, v = self.free_at[i]
            self.P.wait("tensor", s, v)
        return i, self.tiles[i]

    def release(self, i, sem, val):
        self.free_at[i] = (sem, val)


def load_weight_fm(P, rr, name, hbm_ap, Ktot, Mtot, ldsem, queue="gpsimd"):
    kch = Ktot // 128
    sb = rr.sb(name, (128, kch * Mtot), BF16)
    for ki in range(kch):
        P.op(queue, "dma_start", out=sb[:, ki * Mtot:(ki + 1) * Mtot],
             in_=hbm_ap[ki * 128:(ki + 1) * 128, :], inc=[(ldsem, 16)])

    def w(ki, m0, mrows):
        return sb[:, ki * Mtot + m0: ki * Mtot + m0 + mrows]

    return w, ldsem.n


def fm_tiles(sb, N):

    def x(ki, n0, nw):
        return sb[:, ki * N + n0: ki * N + n0 + nw]

    return x


def mm_proj(P, pr, *, wt, xt, kch, Mtot, N, mm_sem, evac, nwin=512, rank1=(),
            wt_ready=None, xt_ready=None, mlist=None):
    first = True
    for m0 in (mlist if mlist is not None else range(0, Mtot, 128)):
        mr = min(128, Mtot - m0)
        for n0 in range(0, N, nwin):
            nw = min(nwin, N - n0)
            i, tile = pr.grab()
            if first:
                if wt_ready is not None:
                    P.wait("tensor", *wt_ready)
                if xt_ready is not None:
                    P.wait("tensor", *xt_ready)
                first = False
            nterms = kch + len(rank1)
            for ki in range(kch):
                last = ki == nterms - 1
                P.op("tensor", "matmul", tile[0:mr, 0:nw], wt(ki, m0, mr),
                     xt(ki, n0, nw), start=(ki == 0), stop=last,
                     inc=([(mm_sem, 1)] if last else []))
            for rj, (la, ra) in enumerate(rank1):
                last = kch + rj == nterms - 1
                P.op("tensor", "matmul", tile[0:mr, 0:nw], la(m0, mr), ra(n0, nw),
                     start=False, stop=last, inc=([(mm_sem, 1)] if last else []))
            free = evac(m0, mr, n0, nw, tile[0:mr, 0:nw], mm_sem.n)
            pr.release(i, *free)


def make_ln_bufs(rr, tag, x2=None):
    return {
        "x2": (x2 if x2 is not None else rr.sb(f"{tag}_x2", (128, KCH * 512), BF16)),
        "zsc": rr.sb(f"{tag}_z", (128, 512), BF16),
        "rowbuf": rr.sb(f"{tag}_rows", (1, 3 * 512), F32),
        "rowbf": rr.sb(f"{tag}_rowbf", (1, 2 * 512), BF16),
        "bc": rr.sb(f"{tag}_bc", (128, 2 * 512), BF16),
    }


class LNState:
    def __init__(self):
        self.mm_stats_prev = None
        self.ve_rows_prev = None
        self.ve_apply_prev = None


def emit_ln_win(P, pr, bufs, st8, xt, ot, nw, consts, sems, gain=None, bias=None,
                ready=None):
    mm, ve, se = sems["mm"], sems["ve"], sems["se"]
    x2, zsc = bufs["x2"], bufs["zsc"]
    rowbuf, rowbf, bc = bufs["rowbuf"], bufs["rowbf"], bufs["bc"]
    if ready is not None:
        P.wait("scalar", *ready)
    if st8.mm_stats_prev is not None:
        P.wait("scalar", mm, st8.mm_stats_prev)
    if st8.ve_apply_prev is not None:
        P.wait("scalar", ve, st8.ve_apply_prev)
    for ki in range(KCH):
        P.op("scalar", "activation", x2[:, ki * 512: ki * 512 + nw],
             xt(ki), AF.Square, inc=[(se, 1)])
    se_sq = se.n
    i1, t1 = pr.grab()
    i2, t2 = pr.grab()
    if ready is not None:
        P.wait("tensor", *ready)
    P.wait("tensor", se, se_sq)
    for ki in range(KCH):
        P.op("tensor", "matmul", t1[0:1, 0:nw], consts["ones_col"], xt(ki),
             start=(ki == 0), stop=(ki == KCH - 1),
             inc=([(mm, 1)] if ki == KCH - 1 else []))
    for ki in range(KCH):
        P.op("tensor", "matmul", t2[0:1, 0:nw], consts["ones_col"],
             x2[:, ki * 512: ki * 512 + nw],
             start=(ki == 0), stop=(ki == KCH - 1),
             inc=([(mm, 1)] if ki == KCH - 1 else []))
    mm_stats = mm.n
    st8.mm_stats_prev = mm_stats
    mrow = rowbuf[:, 0:nw]
    e2row = rowbuf[:, 512:512 + nw]
    m2row = rowbuf[:, 1024:1024 + nw]
    varrow = m2row
    lnrow = m2row
    rstd = m2row
    P.wait("scalar", mm, mm_stats)
    if st8.ve_rows_prev is not None:
        P.wait("scalar", ve, st8.ve_rows_prev)
    P.op("scalar", "activation", mrow, t1[0:1, 0:nw], AF.Copy, scale=1.0 / C)
    P.op("scalar", "activation", e2row, t2[0:1, 0:nw], AF.Copy, scale=1.0 / C,
         inc=[(se, 1)])
    se_rows = se.n
    P.wait("vector", se, se_rows)
    P.op("vector", "tensor_tensor", m2row, mrow, mrow, op=ALU.mult)
    P.selfedge("vector", ve)
    P.op("vector", "tensor_tensor", varrow, e2row, m2row, op=ALU.subtract,
         inc=[(ve, 1)])
    ve_var = ve.n
    P.wait("scalar", ve, ve_var)
    P.op("scalar", "activation", lnrow, varrow, AF.Ln,
         bias=consts["cvals"][0:1, 0:1])
    P.selfedge("scalar", se)
    P.op("scalar", "activation", rstd, lnrow, AF.Exp, scale=-0.5, inc=[(se, 1)])
    se_rstd = se.n
    P.wait("vector", se, se_rstd)
    P.op("vector", "tensor_copy", rowbf[:, 0:nw], mrow)
    P.op("vector", "tensor_copy", rowbf[:, 512:512 + nw], rstd, inc=[(ve, 1)])
    ve_rows = ve.n
    st8.ve_rows_prev = ve_rows
    i3, t3 = pr.grab()
    i4, t4 = pr.grab()
    P.wait("tensor", ve, ve_rows)
    P.op("tensor", "matmul", t3[:, 0:nw], consts["ones_row"], rowbf[:, 0:nw],
         start=True, stop=True, inc=[(mm, 1)])
    P.op("tensor", "matmul", t4[:, 0:nw], consts["ones_row"],
         rowbf[:, 512:512 + nw], start=True, stop=True, inc=[(mm, 1)])
    mm_bc = mm.n
    P.wait("vector", mm, mm_bc)
    P.op("vector", "tensor_copy", bc[:, 0:nw], t3[:, 0:nw])
    P.op("vector", "tensor_copy", bc[:, 512:512 + nw], t4[:, 0:nw], inc=[(ve, 1)])
    P.selfedge("vector", ve)
    pr.release(i1, ve, ve.n)
    pr.release(i2, ve, ve.n)
    pr.release(i3, ve, ve.n)
    pr.release(i4, ve, ve.n)
    for ki in range(KCH):
        P.op("vector", "tensor_tensor", x2[:, ki * 512: ki * 512 + nw],
             xt(ki), bc[:, 0:nw], op=ALU.subtract)
    P.selfedge("vector", ve)
    if gain is None:
        for ki in range(KCH):
            P.op("vector", "tensor_tensor", ot(ki),
                 x2[:, ki * 512: ki * 512 + nw],
                 bc[:, 512:512 + nw], op=ALU.mult, inc=[(ve, 1)])
    else:
        for ki in range(KCH):
            P.op("vector", "tensor_tensor", x2[:, ki * 512: ki * 512 + nw],
                 x2[:, ki * 512: ki * 512 + nw],
                 bc[:, 512:512 + nw], op=ALU.mult)
        P.selfedge("vector", ve)
        for ki in range(KCH):
            P.op("vector", "tensor_scalar", ot(ki),
                 x2[:, ki * 512: ki * 512 + nw],
                 gain(ki), bias(ki), op0=ALU.mult, op1=ALU.add, inc=[(ve, 1)])
    st8.ve_apply_prev = ve.n
    return ve.n


def emit_ln(P, rr, pr, tag, xb, N, consts, sems, gain=None, bias=None, out=None,
            ready=None):
    out = out if out is not None else xb
    bufs = make_ln_bufs(rr, tag)
    st8 = LNState()
    xt0 = fm_tiles(xb, N)
    ot0 = fm_tiles(out, N)
    r = None
    first = True
    for n0 in range(0, N, 512):
        nw = min(512, N - n0)
        xt = lambda ki, _n0=n0, _nw=nw: xt0(ki, _n0, _nw)
        ot = lambda ki, _n0=n0, _nw=nw: ot0(ki, _n0, _nw)
        r = emit_ln_win(P, pr, bufs, st8, xt, ot, nw, consts, sems,
                        gain=gain, bias=bias, ready=(ready if first else None))
        first = False
    return r


def make_consts(P, rr, ve, tag=""):
    ones_col = rr.sb(f"cons_col{tag}", (128, 1), BF16)
    ones_row = rr.sb(f"cons_row{tag}", (1, 128), BF16)
    cvals = rr.sb(f"cons_vals{tag}", (128, 4), F32)
    P.op("vector", "memset", ones_col[:], 1.0)
    P.op("vector", "memset", ones_row[:], 1.0)
    P.op("vector", "memset", cvals[:, 0:1], 1e-5)
    P.op("vector", "memset", cvals[:, 1:2], 1e-24)
    P.op("vector", "memset", cvals[:, 2:3], 0.0)
    P.op("vector", "memset", cvals[:, 3:4], 1.0, inc=[(ve, 1)])
    d = {"ones_col": ones_col[:], "ones_row": ones_row[:], "cvals": cvals,
         "ready": ve.n}
    for e in ("tensor", "scalar", "gpsimd"):
        P.wait(e, ve, d["ready"])
    return d


class PsumPool:

    def __init__(self, P, rr, tag):
        self.P = P
        self.tA = [rr.ps(f"{tag}A{i}", [128, 1024], F32) for i in range(2)]
        self.tB = [rr.ps(f"{tag}B{i}", [128, 512], F32) for i in range(4)]
        self.free_at = [None] * 8
        self.i = 0

    def view(self, idx):
        if idx < 4:
            return self.tA[idx // 2][:, 512 * (idx % 2):512 * (idx % 2) + 512]
        return self.tB[idx - 4][:, :]

    def wide(self, j):
        return self.tA[j][:, :]

    def gate(self, idx, eng="tensor"):
        f = self.free_at[idx]
        if f is not None:
            self.P.wait(eng, f[0], f[1])
            self.free_at[idx] = None

    def release(self, idx, sem, val):
        self.free_at[idx] = (sem, val)

    def grab(self):
        i = self.i
        self.i = (self.i + 1) % 6
        self.gate(i)
        return i, self.view(i)


def quiesce(P, sems):
    barrier(P, sems, engines=("tensor", "vector", "scalar", "gpsimd"))
    targets = [(sm, sm.n) for sm in P.all_sems if sm.n > 0]
    for e in ENGINES:
        if not any(k == "op" for k in P.kinds[e]):
            continue
        for (sm, v) in targets:
            P.wait(e, sm, v)


def barrier(P, sems, engines=("tensor", "vector", "scalar")):
    names = {"tensor": "mm", "vector": "ve", "scalar": "se", "gpsimd": "ve"}
    marks = []
    for e in engines:
        if not any(k == "op" for k in P.kinds[e]):
            continue
        last = P.last_op_inc(e)
        if last is None:
            s = sems[names[e]]
            P.mark(e, s)
            last = (s, s.n)
        marks.append((e,) + last)
    for e in engines:
        if not any(k == "op" for k in P.kinds[e]):
            continue
        for (e2, s, v) in marks:
            if e2 != e:
                P.wait(e, s, v)


def emit_ffn_inner(P, rr, pool, tag, *, xin, QLO, QHI, wq_sb, tabs, wts, consts,
                   sems, out_cb, kmask_bf, xoff, lnb=None, xin_load=None,
                   wq_load=None, xo_stage=None, OUT0=0):
    mm, ve, se = sems["mm"], sems["ve"], sems["se"]
    tab = sems["tab"]
    wst2 = sems["wst2"]
    NW = (QHI - QLO) // 512
    if DEBUG_NW is not None:
        NW = min(NW, DEBUG_NW)
    KLO = QLO - 512

    kn = rr.sb(f"{tag}_kn", (128, 2 * 4 * 512), BF16)
    vtok = rr.sb(f"{tag}_v", (128, 2 * 4 * 520), BF16)
    qn = rr.sb(f"{tag}_qn", (128, 4 * 512), BF16)
    q2s = rr.sb(f"{tag}_q2", (128, 512), BF16)
    rstdb = rr.sb(f"{tag}_rstd", (128, 512), BF16)
    tmp1 = rr.sb(f"{tag}_t1", (128, 512), BF16)
    tmp2 = rr.sb(f"{tag}_t2", (128, 512), BF16)
    tmp3 = rr.sb(f"{tag}_t3", (128, 512), BF16)
    tabst = rr.sb(f"{tag}_tabs", (128, 2 * 512), BF16)
    mf32 = rr.sb(f"{tag}_mf", (128, 8), F32)
    mbuf = rr.sb(f"{tag}_mb", (128, 8), BF16)
    mrows = rr.sb(f"{tag}_mrows", (1, 8 * 128), BF16)
    probsT = rr.sb(f"{tag}_pT", (128, 8 * 128), BF16)
    sumsr = rr.sb(f"{tag}_sums", (1, 1024), F32)
    recipr = rr.sb(f"{tag}_recip", (1, 1024), BF16)
    o_fm = rr.sb(f"{tag}_ofm", (128, 4 * 128), BF16)
    bcn = rr.sb(f"{tag}_bcn", (128, 128), BF16)
    xa = rr.sb(f"{tag}_xa", (128, 2 * KCH * 520), BF16)
    hln = rr.sb(f"{tag}_hln", (128, KCH * 512), BF16)
    ffnh = rr.sb(f"{tag}_ffnh", (128, 6 * 512), BF16)
    gts = rr.sb(f"{tag}_gt", (128, 512), BF16)
    w1st = rr.sb(f"{tag}_w1st", (128, 4 * 1024), BF16)
    w2st = rr.sb(f"{tag}_w2st", (128, 1024), BF16)
    if xo_stage is None:
        xo_stage = rr.sb(f"{tag}_xo", (128, KCH * 512), BF16)
    if lnb is None:
        lnb = make_ln_bufs(rr, f"{tag}_iln")
    lnst = LNState()
    state = {"store": None}

    P.op("vector", "memset", vtok[:], 1.0)
    P.op("vector", "memset", xa[:], 0.0, inc=[(ve, 1)])
    P.selfedge("vector", ve)

    def xsh(ki, c0, nw):
        return xin(ki, c0 - (1 if ki >= 4 else 0), nw)

    def kslot(c0):
        return ((c0 - KLO) // 512) % 2

    def kn_ap(c0, base_par, j0, nj):
        slot = kslot(c0)
        hp = None
        return kn[base_par:base_par + 64,
                  slot * 2048 + j0: slot * 2048 + j0 + nj]

    def project_qk(c0, is_q):
        if DEBUG_PROJ < 2:
            return 0
        if is_q:
            t0 = c0 - QLO
            P.wait("gpsimd", ve, ve.n)
            P.op("gpsimd", "dma_start", out=tabst[:, 0:512],
                 in_=tabs["cq"][:, t0:t0 + 512], inc=[(tab, 16)])
            P.op("gpsimd", "dma_start", out=tabst[:, 512:1024],
                 in_=tabs["sq"][:, t0:t0 + 512], inc=[(tab, 16)])
            ctab, stab = tabst[:, 0:512], tabst[:, 512:1024]
            scl, scls = wts["qs"], wts["qss"]
            mlo = 0
        else:
            t0 = c0 - KLO
            P.wait("gpsimd", ve, ve.n)
            P.op("gpsimd", "dma_start", out=tabst[:, 0:512],
                 in_=tabs["ck"][:, t0:t0 + 512], inc=[(tab, 16)])
            P.op("gpsimd", "dma_start", out=tabst[:, 512:1024],
                 in_=tabs["sk"][:, t0:t0 + 512], inc=[(tab, 16)])
            ctab, stab = tabst[:, 0:512], tabst[:, 512:1024]
            scl, scls = wts["ks"], wts["kss"]
            mlo = 512
        tab_val = tab.n
        if wq_load is not None:
            wq_load(mlo // 512)
        for hp in range(4):
            m0 = mlo + hp * 128
            ib, bank = pool.grab()
            for ki in range(KCH):
                P.op("tensor", "matmul", bank, wq_sb(ki, m0, 128),
                     xsh(ki, c0, 512), start=(ki == 0), stop=(ki == KCH - 1),
                     inc=([(mm, 1)] if ki == KCH - 1 else []))
            mmv = mm.n
            if is_q:
                raw = qn[:, hp * 512:(hp + 1) * 512]
            else:
                slot = kslot(c0)
                raw = kn[:, slot * 2048 + hp * 512: slot * 2048 + (hp + 1) * 512]
            if DEBUG_PROJ == 21:
                pool.release(ib, mm, mmv)
                continue
            P.wait("scalar", mm, mmv)
            P.op("scalar", "activation", q2s[:], bank, AF.Square, inc=[(se, 1)])
            se_sq = se.n
            if DEBUG_PROJ == 22:
                pool.release(ib, se, se_sq)
                continue
            P.wait("vector", mm, mmv)
            P.wait("vector", se, se_sq)
            P.op("vector", "tensor_copy", raw, bank, inc=[(ve, 1)])
            pool.release(ib, ve, ve.n)
            if DEBUG_PROJ < 3:
                continue
            ib2, bank2 = pool.grab()
            P.wait("tensor", se, se_sq)
            P.op("tensor", "matmul", bank2, consts["blockones"], q2s[:],
                 start=True, stop=True, inc=[(mm, 1)])
            mm_n2 = mm.n
            P.wait("scalar", mm, mm_n2)
            P.wait("scalar", ve, ve.n)
            P.op("scalar", "activation", rstdb[:], bank2, AF.Ln,
                 bias=consts["cvals"][:, 1:2])
            P.selfedge("scalar", se)
            P.op("scalar", "activation", rstdb[:], rstdb[:], AF.Exp, scale=-0.5,
                 inc=[(se, 1)])
            se_rstd = se.n
            pool.release(ib2, se, se_rstd)
            P.wait("vector", se, se_rstd)
            P.wait("vector", tab, tab_val)
            P.selfedge("vector", ve)
            P.op("vector", "tensor_tensor", tmp1[:], raw, rstdb[:], op=ALU.mult)
            P.selfedge("vector", ve)
            P.op("vector", "scalar_tensor_tensor", tmp2[:], tmp1[:], scl, ctab,
                 op0=ALU.mult, op1=ALU.mult)
            for (a, b) in ((0, 32), (32, 0), (64, 96), (96, 64)):
                P.op("vector", "scalar_tensor_tensor", tmp3[a:a + 32, :],
                     tmp1[b:b + 32, :], scl[b:b + 32, 0:1], stab[b:b + 32, :],
                     op0=ALU.mult, op1=ALU.mult)
            P.selfedge("vector", ve)
            P.op("vector", "tensor_tensor", raw, tmp2[:], tmp3[:],
                 op=ALU.add, inc=[(ve, 1)])
        return ve.n

    def project_v(c0):
        if DEBUG_PROJ < 1:
            return
        slot = kslot(c0)
        if wq_load is not None:
            wq_load(2)
        for tt in range(4):
            tc = c0 + tt * 128
            ib, bank = pool.grab()
            for ki in range(KCH):
                P.op("tensor", "matmul", bank, xsh(ki, tc, 128),
                     wq_sb(ki, 1024, 512), start=(ki == 0),
                     stop=(ki == KCH - 1),
                     inc=([(mm, 1)] if ki == KCH - 1 else []))
            mmv = mm.n
            base = slot * 4 * 520 + tt * 520
            dstv = vtok[:, base:base + 520].rearrange(
                "p (h d) -> p h d", d=65)[:, :, 0:64]
            P.wait("vector", mm, mmv)
            P.op("vector", "tensor_copy", dstv, bank, inc=[(ve, 1)])
            pool.release(ib, ve, ve.n)

    def attention(c0):
        wpar = ((c0 - QLO) // 512) % 2
        for qt in range(4):
            qc = c0 + 128 * qt
            for par in range(2):
                pool.gate(2 * par)
                pool.gate(2 * par + 1)
            for hp in range(4):
                for kh in range(2):
                    khc = c0 - 512 + 512 * kh
                    slot = kslot(khc)
                    for par in range(2):
                        wide = pool.wide(par)
                        P.op("tensor", "matmul", wide[:, 512 * kh:512 * kh + 512],
                             qn[64 * par:64 * par + 64,
                                hp * 512 + 128 * qt: hp * 512 + 128 * qt + 128],
                             kn[64 * par:64 * par + 64,
                                slot * 2048 + hp * 512: slot * 2048 + hp * 512 + 512],
                             start=True, stop=True, tile_position=(64 * par, 0),
                             inc=[(mm, 1)])
                mm_simA = mm.n
                P.wait("vector", mm, mm_simA)
                for par in range(2):
                    wide = pool.wide(par)
                    P.op("vector", "tensor_tensor",
                         wide[:, 512 + 128 * qt:512 + 128 * qt + 128],
                         wide[:, 512 + 128 * qt:512 + 128 * qt + 128],
                         consts["triuA"], op=ALU.add)
                P.selfedge("vector", ve)
                for par in range(2):
                    wide = pool.wide(par)
                    h = 2 * hp + par
                    P.op("vector", "reduce_max", mf32[:, h:h + 1],
                         wide[:, 0:512 + 128 * (qt + 1)],
                         axis=AX, inc=[(ve, 1)])
                ve_max = ve.n
                for par in range(2):
                    pool.release(2 * par, ve, ve_max)
                    pool.release(2 * par + 1, ve, ve_max)
                if hp < 3:
                    for par in range(2):
                        pool.gate(2 * par)
                        pool.gate(2 * par + 1)
            ve_maxall = ve.n
            P.wait("scalar", ve, ve_maxall)
            P.op("scalar", "activation", mbuf[:], mf32[:], AF.Copy, scale=-1.0,
                 inc=[(se, 1)])
            se_negm = se.n
            pool.gate(6)
            P.wait("tensor", se, se_negm)
            for h in range(8):
                P.op("tensor", "matmul", pool.view(6)[0:1, 0:128],
                     mbuf[:, h:h + 1], consts["identbf"], start=True, stop=True,
                     inc=[(mm, 1)])
                mmv = mm.n
                P.wait("vector", mm, mmv)
                P.op("vector", "tensor_copy", mrows[:, 128 * h:128 * h + 128],
                     pool.view(6)[0:1, 0:128], inc=[(ve, 1)])
                P.wait("tensor", ve, ve.n)
            pool.release(6, ve, ve.n)

            kts = [kt for kt in range(8) if c0 - 512 + 128 * kt <= qc]
            first_kt = kts[0]
            ve_rec = None
            for hpass in range(2):
                heads = [4 * hpass + i for i in range(4)]
                se_exp_prev = None
                mm_av_prev = None
                for kt in kts:
                    ktc = c0 - 512 + 128 * kt
                    slot = kslot(ktc)
                    j0 = (ktc - (c0 - 512)) % 512
                    if se_exp_prev is not None:
                        P.wait("tensor", se, se_exp_prev)
                    else:
                        pool.gate(0)
                        pool.gate(1)
                        if hpass == 0:
                            pool.gate(2)
                            pool.gate(3)
                        if kt == first_kt:
                            for i in range(4):
                                pool.gate(4 + i)
                    for i, h in enumerate(heads):
                        par = h % 2
                        hp = h // 2
                        sv = pool.wide(par)[:, 128 * (i // 2):128 * (i // 2) + 128]
                        P.op("tensor", "matmul", sv,
                             kn[64 * par:64 * par + 64,
                                slot * 2048 + hp * 512 + j0:
                                slot * 2048 + hp * 512 + j0 + 128],
                             qn[64 * par:64 * par + 64,
                                hp * 512 + 128 * qt: hp * 512 + 128 * qt + 128],
                             start=True, stop=False, tile_position=(64 * par, 0))
                        P.op("tensor", "matmul", sv,
                             consts["ones_row"][0:1, 0:128],
                             mrows[0:1, 128 * h:128 * h + 128], start=False,
                             stop=(ktc >= xoff or kmask_bf is None))
                        if ktc < xoff and kmask_bf is not None:
                            P.op("tensor", "matmul", sv,
                                 kmask_bf[0:1, ktc:ktc + 128],
                                 consts["ones_row"][0:1, 0:128], start=False,
                                 stop=True)
                    P.mark("tensor", mm)
                    mm_simB = mm.n
                    if ktc == qc:
                        P.wait("vector", mm, mm_simB)
                        for i in range(4):
                            sv = pool.wide(i % 2)[:, 128 * (i // 2):
                                                  128 * (i // 2) + 128]
                            P.op("vector", "tensor_tensor", sv, sv,
                                 consts["triuB"], op=ALU.add)
                        P.mark("vector", ve)
                        ve_diag = ve.n
                        P.wait("scalar", ve, ve_diag)
                    P.wait("scalar", mm, mm_simB)
                    if mm_av_prev is not None:
                        P.wait("scalar", mm, mm_av_prev)
                    P.op("scalar", "activation", probsT[:, 0:256],
                         pool.wide(0)[:, 0:256], AF.Exp)
                    P.op("scalar", "activation", probsT[:, 256:512],
                         pool.wide(1)[:, 0:256], AF.Exp, inc=[(se, 1)])
                    se_exp_prev = se.n
                    P.wait("tensor", se, se_exp_prev)
                    for i, h in enumerate(heads):
                        ov = pool.view(4 + i)[0:64, 0:128]
                        P.op("tensor", "matmul", ov,
                             vtok[:, slot * 4 * 520 + (j0 // 128) * 520 + 65 * h:
                                  slot * 4 * 520 + (j0 // 128) * 520 + 65 * h + 64],
                             probsT[:, 256 * (i % 2) + 128 * (i // 2):
                                    256 * (i % 2) + 128 * (i // 2) + 128],
                             start=(kt == first_kt), stop=(kt == kts[-1]))
                    P.op("tensor", "matmul", pool.wide(0)[0:1, 512:1024],
                         consts["ones_col"], probsT[:, 0:512],
                         start=(kt == first_kt), stop=(kt == kts[-1]))
                    P.mark("tensor", mm)
                    mm_av_prev = mm.n
                mm_av = mm_av_prev
                P.wait("vector", mm, mm_av)
                P.op("vector", "tensor_scalar", sumsr[:, 512 * hpass:512 * hpass + 512],
                     pool.wide(0)[0:1, 512:1024], 1e-30, None, op0=ALU.add)
                P.selfedge("vector", ve)
                P.op("vector", "reciprocal", recipr[:, 512 * hpass:512 * hpass + 512],
                     sumsr[:, 512 * hpass:512 * hpass + 512], inc=[(ve, 1)],
                     lowprec=True)
                ve_rec = ve.n
                P.wait("tensor", ve, ve_rec)
                for i, h in enumerate(heads):
                    P.op("tensor", "matmul", pool.view(3)[0:64, 0:128],
                         consts["ones_row"][0:1, 0:64],
                         recipr[0:1, 512 * hpass + 256 * (i % 2) + 128 * (i // 2):
                                512 * hpass + 256 * (i % 2) + 128 * (i // 2) + 128],
                         start=True, stop=True, inc=[(mm, 1)])
                    mmv = mm.n
                    P.wait("vector", mm, mmv)
                    P.op("vector", "tensor_copy", bcn[0:64, :],
                         pool.view(3)[0:64, 0:128])
                    P.selfedge("vector", ve)
                    P.op("vector", "tensor_tensor",
                         o_fm[64 * (h % 2):64 * (h % 2) + 64,
                              128 * (h // 2):128 * (h // 2) + 128],
                         pool.view(4 + i)[0:64, 0:128],
                         bcn[0:64, :], op=ALU.mult, inc=[(ve, 1)])
                    P.wait("tensor", ve, ve.n)
                ve_ofm = ve.n
                for i in range(8):
                    pool.release(i, ve, ve_ofm)
            for oc in range(KCH):
                ib, bank = pool.grab()
                for ki in range(4):
                    P.op("tensor", "matmul", bank[0:128, 0:128],
                         wts["wo"](ki, oc * 128, 128),
                         o_fm[:, 128 * ki:128 * ki + 128],
                         start=(ki == 0), stop=(ki == 3),
                         inc=([(mm, 1)] if ki == 3 else []))
                mmv = mm.n
                P.wait("vector", mm, mmv)
                dst = xa[:, wpar * KCH * 520 + oc * 520 + 8 + 128 * qt:
                         wpar * KCH * 520 + oc * 520 + 8 + 128 * qt + 128]
                P.op("vector", "tensor_tensor", dst, bank[0:128, 0:128],
                     xin(oc, qc, 128), op=ALU.add, inc=[(ve, 1)])
                pool.release(ib, ve, ve.n)
        return ve.n

    def ffn(c0, wi):
        wpar = ((c0 - QLO) // 512) % 2

        def xa_dat(ki, off=0, nw=512):
            return xa[:, wpar * KCH * 520 + ki * 520 + 8 + off:
                      wpar * KCH * 520 + ki * 520 + 8 + off + nw]

        def xt(ki):
            if ki < 4:
                return xa_dat(ki)
            return xa[:, wpar * KCH * 520 + ki * 520 + 7:
                      wpar * KCH * 520 + ki * 520 + 7 + 512]

        def ot(ki):
            return hln[:, ki * 512:(ki + 1) * 512]

        if state["store"] is not None:
            P.wait("vector", *state["store"])
            P.wait("scalar", *state["store"])
        emit_ln_win(P, pool, lnb, lnst, xt, ot, 512, consts, sems,
                    gain=wts["lg"], bias=wts["lb"])
        ve_hln = ve.n
        mm_hc = {}
        mm_w2 = {}
        groups = [(0, 6), (6, 12), (12, 18), (18, 22)]
        for gi, (h0, h1) in enumerate(groups):
            nhc = h1 - h0
            for j in range(nhc):
                hc = h0 + j
                p = hc % 2
                if hc >= 2:
                    P.wait("sync", mm, mm_hc[hc - 2])
                P.op("sync", "dma_start", out=w1st[:, p * 2048:p * 2048 + 1024],
                     in_=wts["w1scr"][hc], inc=[(wst2[p], 16)])
                P.op("sync", "dma_start",
                     out=w1st[:, p * 2048 + 1024:p * 2048 + 2048],
                     in_=wts["w1scr"][22 + hc], inc=[(wst2[p], 16)])
                wv = wst2[p].n
                ia, banka = pool.grab()
                ig, bankg = pool.grab()
                P.wait("tensor", wst2[p], wv)
                if hc == 0:
                    P.wait("tensor", ve, ve_hln)
                for ki in range(KCH):
                    P.op("tensor", "matmul", banka,
                         w1st[:, p * 2048 + ki * 128:p * 2048 + ki * 128 + 128],
                         hln[:, ki * 512:(ki + 1) * 512], start=(ki == 0),
                         stop=(ki == KCH - 1),
                         inc=([(mm, 1)] if ki == KCH - 1 else []))
                for ki in range(KCH):
                    P.op("tensor", "matmul", bankg,
                         w1st[:, p * 2048 + 1024 + ki * 128:
                              p * 2048 + 1024 + ki * 128 + 128],
                         hln[:, ki * 512:(ki + 1) * 512], start=(ki == 0),
                         stop=(ki == KCH - 1),
                         inc=([(mm, 1)] if ki == KCH - 1 else []))
                mm_hc[hc] = mm.n
                P.wait("scalar", mm, mm_hc[hc])
                P.wait("scalar", ve, ve.n)
                P.op("scalar", "activation", gts[:], bankg, AF.Gelu,
                     bias=wts["b1g"](hc), inc=[(se, 1)])
                se_g = se.n
                P.wait("vector", se, se_g)
                P.op("vector", "scalar_tensor_tensor",
                     ffnh[:, j * 512:(j + 1) * 512], banka, wts["b1a"](hc),
                     gts[:], op0=ALU.add, op1=ALU.mult, inc=[(ve, 1)])
                pool.release(ia, ve, ve.n)
                pool.release(ig, ve, ve.n)
            ve_ffnh = ve.n
            for i in range(8):
                pool.gate(i)
            for j in range(nhc):
                hc = h0 + j
                if hc >= 1:
                    P.wait("sync", mm, mm_w2.get(hc - 1, 0))
                P.op("sync", "dma_start", out=w2st[:, 0:1024],
                     in_=wts["w2scr"][hc], inc=[(wst2[2], 16)])
                wv = wst2[2].n
                P.wait("tensor", wst2[2], wv)
                if j == 0:
                    P.wait("tensor", ve, ve_ffnh)
                for oc in range(8):
                    P.op("tensor", "matmul", pool.view(oc),
                         w2st[:, oc * 128:oc * 128 + 128],
                         ffnh[:, j * 512:(j + 1) * 512], start=(j == 0),
                         stop=(j == nhc - 1))
                P.mark("tensor", mm)
                mm_w2[hc] = mm.n
            mm_w2f = mm.n
            P.wait("vector", mm, mm_w2f)
            for oc in range(8):
                if gi == 0:
                    P.op("vector", "scalar_tensor_tensor",
                         xo_stage[:, oc * 512:(oc + 1) * 512], pool.view(oc),
                         wts["b2"](oc), xa_dat(oc), op0=ALU.add, op1=ALU.add,
                         inc=[(ve, 1)])
                else:
                    P.op("vector", "tensor_tensor",
                         xo_stage[:, oc * 512:(oc + 1) * 512], pool.view(oc),
                         xo_stage[:, oc * 512:(oc + 1) * 512], op=ALU.add,
                         inc=[(ve, 1)])
            ve_out = ve.n
            for i in range(8):
                pool.release(i, ve, ve_out)
        state["store"] = out_cb(wi, xo_stage, ve_out)

    if DEBUG_PHASE < 1:
        return
    if xin_load is not None:
        xin_load(KLO)
    project_v(KLO)
    project_qk(KLO, False)
    for wi in range(NW):
        c0 = QLO + 512 * wi
        if xin_load is not None:
            xin_load(c0)
        project_v(c0)
        project_qk(c0, False)
        project_qk(c0, True)
        barrier(P, sems)
        if DEBUG_PHASE < 2:
            continue
        attention(c0)
        P.selfedge("vector", ve)
        wpar = ((c0 - QLO) // 512) % 2
        for ki in range(KCH):
            P.op("vector", "tensor_copy",
                 xa[:, (1 - wpar) * KCH * 520 + ki * 520:
                    (1 - wpar) * KCH * 520 + ki * 520 + 8],
                 xa[:, wpar * KCH * 520 + ki * 520 + 512:
                    wpar * KCH * 520 + ki * 520 + 520], inc=[(ve, 1)])
        P.selfedge("vector", ve)
        barrier(P, sems)
        if DEBUG_PHASE < 3 or wi < OUT0:
            continue
        ffn(c0, wi - OUT0)
        barrier(P, sems)


def build_launch_B(l, fx=None):
    if fx is None:
        nc = bass.Bass("TRN2", target_bir_lowering=False)
        dram = lambda name, shape, dt, kind="Internal": nc.dram_tensor(
            name, shape, dt, kind=kind).ap()
    else:
        nc = fx["nc"]
        dram = lambda name, shape, dt, kind="Internal": fx["io"][name]
    okind = "ExternalOutput" if fx is None else "Internal"
    if l == 0:
        NIN, XOFF = 3600, 1544
        QLO = XOFF - 1024
    else:
        NIN, XOFF = 3072, 1024
        QLO = XOFF - 512
    QHI = XOFF + G
    Tq = QHI - QLO
    Tk = QHI - (QLO - 512)

    if l == 0:
        xh = dram("xh2", [C, NIN], F32, "ExternalInput")
        xout_h = dram("xout0", [C, 3072], BF16, okind)
    else:
        xh = dram("xin0", [C, NIN], BF16, "ExternalInput")
        af0_h = dram("af0", [C, G], F32, okind)
        f0scr = dram("f0scr", [C, G], BF16)
        rwscr = dram("rwscr", [C, G], BF16)
        wkvl_h = dram("wkvl", [C, G], F32, "ExternalInput")
        wkl_h = dram("wkl", [C, G], F32, "ExternalInput")
        rbf_h = dram("rbf", [C, G], BF16, "ExternalInput")
        carrH_h = dram("carrH", [C, 8], F32, "ExternalInput")
        tdP_h = dram("tdP", [128, KCH], F32, "ExternalInput")
        tfP_h = dram("tfP", [128, KCH], F32, "ExternalInput")
        iota_h = dram("iota2d", [128, G], F32, "ExternalInput")
        woTr_h = dram("woTr", [C, C], F32, "ExternalInput")

    wqkv_h = dram("wqkv", [C, 1536], F32, "ExternalInput")
    wo_h = dram("wo", [512, C], F32, "ExternalInput")
    w1lm_h = dram("w1lm", [44, 128, 1024], F32, "ExternalInput")
    w2lm_h = dram("w2lm", [22, 128, 1024], F32, "ExternalInput")
    b1aP_h = dram("b1aP", [128, 22], F32, "ExternalInput")
    b1gP_h = dram("b1gP", [128, 22], F32, "ExternalInput")
    b2P_h = dram("b2P", [128, KCH], F32, "ExternalInput")
    lgP_h = dram("lgP", [128, KCH], F32, "ExternalInput")
    lbP_h = dram("lbP", [128, KCH], F32, "ExternalInput")
    scal4_h = dram("scal4", [128, 4], F32, "ExternalInput")
    cq_h = dram("cq", [128, Tq], F32, "ExternalInput")
    sq_h = dram("sq", [128, Tq], F32, "ExternalInput")
    ck_h = dram("ck", [128, Tk], F32, "ExternalInput")
    sk_h = dram("sk", [128, Tk], F32, "ExternalInput")
    identbf_h = dram("identbf", [128, 128], BF16, "ExternalInput")
    blockones_h = dram("blockones", [128, 128], BF16, "ExternalInput")
    triuA_h = dram("triuA", [128, 128], F32, "ExternalInput")
    triuB_h = dram("triuB", [128, 128], F32, "ExternalInput")
    KML = (XOFF // 128) * 128 + 128
    kmask_h = dram("kmask", [1, KML], BF16, "ExternalInput")
    w1scr = dram("w1scr", [44, 128, 1024], BF16)
    w2scr = dram("w2scr", [22, 128, 1024], BF16)

    with ExitStack() as stack:
        if fx is None:
            P = Prog(nc, stack)
            rr = R(nc, stack)
        else:
            P, rr = fx["P"], fx["rr"]
            rr.push(f"B{l}_")
        ld = P.sem("ld")
        lc = P.sem("lc")
        wl = P.sem("wl")
        pc_in = P.sem("pci")
        pc_out = P.sem("pco")
        mm = P.sem("mm")
        ve = P.sem("ve")
        se = P.sem("se")
        st = P.sem("st")
        tabsem = P.sem("tab")
        wst2 = (P.sem("wsta"), P.sem("wstb"), P.sem("wstc"))
        sems = {"mm": mm, "ve": ve, "se": se, "tab": tabsem, "wst2": wst2}

        if fx is None:
            consts = make_consts(P, rr, ve)
        else:
            consts = dict(fx["consts"])
        pool = PsumPool(P, rr, "pp")
        xo_stage = rr.sb("xo_stage", (128, KCH * 512), BF16)
        lnb = make_ln_bufs(rr, "lnb", x2=xo_stage[:, :])

        cst = rr.sb("cstb", (128, 2 * 128), BF16)
        cstf = rr.sb("cstf", (128, 2 * 128), F32)
        kmask_sb = rr.sb("kmask_sb", (1, KML), BF16)
        P.op("sync", "dma_start", out=cst[:, 0:128], in_=identbf_h, inc=[(lc, 16)])
        P.op("sync", "dma_start", out=cst[:, 128:256], in_=blockones_h, inc=[(lc, 16)])
        P.op("sync", "dma_start", out=cstf[:, 0:128], in_=triuA_h, inc=[(lc, 16)])
        P.op("sync", "dma_start", out=cstf[:, 128:256], in_=triuB_h, inc=[(lc, 16)])
        P.op("sync", "dma_start", out=kmask_sb[:], in_=kmask_h, inc=[(lc, 16)])

        pars = rr.sb("parsB", (128, 22 * 2 + KCH * 3 + 4 + (2 * KCH if l else 0)), F32)
        p_b1a = pars[:, 0:22]
        p_b1g = pars[:, 22:44]
        p_b2 = pars[:, 44:52]
        p_lg = pars[:, 52:60]
        p_lb = pars[:, 60:68]
        p_sc = pars[:, 68:72]
        P.op("sync", "dma_start", out=p_b1a, in_=b1aP_h, inc=[(lc, 16)])
        P.op("sync", "dma_start", out=p_b1g, in_=b1gP_h, inc=[(lc, 16)])
        P.op("sync", "dma_start", out=p_b2, in_=b2P_h, inc=[(lc, 16)])
        P.op("sync", "dma_start", out=p_lg, in_=lgP_h, inc=[(lc, 16)])
        P.op("sync", "dma_start", out=p_lb, in_=lbP_h, inc=[(lc, 16)])
        P.op("sync", "dma_start", out=p_sc, in_=scal4_h, inc=[(lc, 16)])
        if l == 1:
            p_td = pars[:, 72:80]
            p_tf = pars[:, 80:88]
            P.op("sync", "dma_start", out=p_td, in_=tdP_h, inc=[(lc, 16)])
            P.op("sync", "dma_start", out=p_tf, in_=tfP_h, inc=[(lc, 16)])
        lc_all = lc.n
        for e in ("tensor", "vector", "scalar"):
            P.wait(e, lc, lc_all)
        consts["identbf"] = cst[:, 0:128]
        consts["blockones"] = cst[:, 128:256]
        consts["triuA"] = cstf[:, 0:128]
        consts["triuB"] = cstf[:, 128:256]

        wo_sb = rr.sb("wo_sb", (128, 4 * 1024), BF16)
        for ki in range(4):
            P.op("gpsimd", "dma_start", out=wo_sb[:, ki * 1024:(ki + 1) * 1024],
                 in_=wo_h[ki * 128:(ki + 1) * 128, :], inc=[(wl, 16)])
        wl_all = wl.n
        P.wait("tensor", wl, wl_all)

        wqst = rr.sb("wqst", (128, KCH * 512), BF16)
        wq_state = {"blk": -1}

        def wq_load(blk):
            if wq_state["blk"] == blk:
                return
            wq_state["blk"] = blk
            P.wait("gpsimd", mm, mm.n)
            P.wait("gpsimd", wl, wl.n)
            for ki in range(KCH):
                P.op("gpsimd", "dma_start", out=wqst[:, ki * 512:(ki + 1) * 512],
                     in_=wqkv_h[ki * 128:(ki + 1) * 128,
                                512 * blk:512 * blk + 512], inc=[(wl, 16)])
            for e in ("tensor",):
                P.wait(e, wl, wl.n)

        def wq_sb(ki, m0, mr):
            blk = m0 // 512
            assert (m0 + mr - 1) // 512 == blk
            return wqst[:, ki * 512 + m0 - 512 * blk: ki * 512 + m0 - 512 * blk + mr]

        def wo_sl(ki, m0, mr):
            return wo_sb[:, ki * 1024 + m0: ki * 1024 + m0 + mr]

        pstage = rr.sb("pstage", (128, 1024), BF16)
        npc = 0
        for (src, dst, n) in ((w1lm_h, w1scr, 44), (w2lm_h, w2scr, 22)):
            for i in range(n):
                if npc >= 1:
                    P.wait("gpsimd", pc_out, 16 * npc)
                P.op("gpsimd", "dma_start", out=pstage[:],
                     in_=src[i], inc=[(pc_in, 16)])
                P.wait("scalar", pc_in, pc_in.n)
                P.op("scalar", "dma_start", out=dst[i],
                     in_=pstage[:], inc=[(pc_out, 16)])
                npc += 1
        pc_done = pc_out.n
        P.wait("sync", pc_out, pc_done)

        if l == 0:
            xb = rr.sb("xb2", (128, KCH * NIN), BF16)
            for ki in range(KCH):
                P.op("gpsimd", "dma_start", out=xb[:, ki * NIN:(ki + 1) * NIN],
                     in_=xh[ki * 128:(ki + 1) * 128, :], inc=[(ld, 16)])
            ld_x = ld.n
            emit_ln(P, rr, pool, "ln0", xb, NIN, consts, sems, ready=(ld, ld_x))
            xin = fm_tiles(xb, NIN)
            xin_load = None
        else:
            xinr = rr.sb("xinr", (128, 2 * KCH * 520), BF16)
            P.op("vector", "memset", xinr[:], 0.0, inc=[(ve, 1)])
            xl = P.sem("xl")
            def xin_load(c0):
                w = c0 // 512
                slot = w % 2
                P.wait("sync", ve, ve.n)
                for ki in range(KCH):
                    lo = max(0, c0 - 8)
                    P.op("sync", "dma_start",
                         out=xinr[:, slot * KCH * 520 + ki * 520 + (8 - (c0 - lo)):
                                  slot * KCH * 520 + (ki + 1) * 520],
                         in_=xh[ki * 128:(ki + 1) * 128, lo:c0 + 512],
                         inc=[(xl, 16)])
                P.wait("tensor", xl, xl.n)
                P.wait("vector", xl, xl.n)
                P.wait("scalar", xl, xl.n)

            def xin(ki, c, nw):
                wstar = (c + nw - 1) // 512
                slot = wstar % 2
                off = 8 + c - 512 * wstar
                assert off >= 0, (c, nw)
                return xinr[:, slot * KCH * 520 + ki * 520 + off:
                            slot * KCH * 520 + ki * 520 + off + nw]

        wts = {
            "w1scr": [w1scr[i] for i in range(44)],
            "w2scr": [w2scr[i] for i in range(22)],
            "b1a": lambda hc: p_b1a[:, hc:hc + 1],
            "b1g": lambda hc: p_b1g[:, hc:hc + 1],
            "b2": lambda oc: p_b2[:, oc:oc + 1],
            "lg": lambda ki: p_lg[:, ki:ki + 1],
            "lb": lambda ki: p_lb[:, ki:ki + 1],
            "qs": p_sc[:, 0:1],
            "qss": p_sc[:, 1:2],
            "ks": p_sc[:, 2:3],
            "kss": p_sc[:, 3:4],
            "wo": wo_sl,
        }
        tabs = {"cq": cq_h, "sq": sq_h, "ck": ck_h, "sk": sk_h}

        tgt_h = xout_h if l == 0 else f0scr
        NTG = 3072 if l == 0 else G
        tgt3 = tgt_h.rearrange("(k p) n -> p k n", p=128)

        def out_cb(wi, stage, vev):
            P.wait("gpsimd", ve, vev)
            P.wait("gpsimd", st, st.n)
            P.op("gpsimd", "dma_start",
                 out=tgt3[:, :, 512 * wi:512 * wi + 512],
                 in_=stage[:].rearrange("p (k n) -> p k n", n=512),
                 inc=[(st, 16)])
            return (st, st.n)

        emit_ffn_inner(P, rr, pool, "il", xin=xin, QLO=QLO, QHI=QHI,
                       wq_sb=wq_sb, tabs=tabs, wts=wts, consts=consts,
                       sems=sems, out_cb=out_cb, kmask_bf=kmask_sb[:],
                       xoff=XOFF, lnb=lnb, wq_load=wq_load,
                       xo_stage=xo_stage[:, :], OUT0=(0 if l == 0 else 1),
                       xin_load=(None if l == 0 else xin_load))

        if l == 1:
            emit_mixer_post(P, rr, pool, sems, consts,
                            wkvl_h=wkvl_h, wkl_h=wkl_h, rbf_h=rbf_h,
                            carrH_h=carrH_h, p_td=p_td, p_tf=p_tf,
                            iota_h=iota_h, woTr_h=woTr_h, f0scr=f0scr,
                            rwscr=rwscr, af0_h=af0_h, st=st)

        P.wait("gpsimd", st, st.n)
        assert rr.sb_bytes <= 222 * 1024 * 128, rr.sb_bytes
        if fx is None:
            P.build()
        else:
            quiesce(P, sems)
            rr.pop()
    return nc


def emit_mixer_post(P, rr, pool, sems, consts, *, wkvl_h, wkl_h, rbf_h, carrH_h,
                    p_td, p_tf, iota_h, woTr_h, f0scr, rwscr, af0_h, st):
    mm, ve, se = sems["mm"], sems["ve"], sems["se"]
    barrier(P, sems)
    lm = P.sem("lm")
    lg2 = P.sem("lg2")
    GH = G // 2
    wkv_t = rr.sb("mp_wkv", (128, GH), F32)
    wk_t = rr.sb("mp_wk", (128, GH), F32)
    fac_t = rr.sb("mp_fac", (128, GH), F32)
    rec_t = rr.sb("mp_rec", (128, GH), F32)
    sg_t = rr.sb("mp_sg", (128, GH), BF16)
    rb_t = rr.sb("mp_rb", (128, GH), BF16)
    rw_t = rr.sb("mp_rw", (128, GH), BF16)
    iota_t = rr.sb("mp_iota", (128, GH), F32)
    carr_t = rr.sb("mp_carr", (128, 8 * KCH), F32)
    chp = rr.sb("mp_chp", (128, 4 * KCH), F32)
    pw = rr.sb("mp_pw", (128, 4), F32)
    woT_sb = rr.sb("mp_woT", (128, KCH * 1024), BF16)
    for ki in range(KCH):
        P.op("gpsimd", "dma_start", out=woT_sb[:, ki * 1024:(ki + 1) * 1024],
             in_=woTr_h[ki * 128:(ki + 1) * 128, :], inc=[(lg2, 16)])
    lmc = P.sem("lmc")
    for ci in range(KCH):
        P.op("sync", "dma_start", out=carr_t[:, 8 * ci:8 * ci + 8],
             in_=carrH_h[128 * ci:128 * ci + 128, :], inc=[(lmc, 16)])
    lm_all = lmc.n
    lg_all = lg2.n
    st_prev = None
    for ci in range(KCH):
        cs = 128 * ci
        P.wait("scalar", lmc, lm_all)
        P.op("scalar", "activation", chp[:, 4 * ci:4 * ci + 1],
             p_td[:, ci:ci + 1], AF.Exp, inc=[(se, 1)])
        P.selfedge("scalar", se)
        for mI in range(1, 4):
            P.op("scalar", "activation", pw[:, mI:mI + 1],
                 chp[:, 4 * ci:4 * ci + 1], AF.Exp, scale=-float(G * mI),
                 inc=[(se, 1)])
        P.op("scalar", "activation", pw[:, 0:1],
             chp[:, 4 * ci:4 * ci + 1], AF.Copy, scale=-1.0, inc=[(se, 1)])
        se_pw = se.n
        P.wait("vector", se, se_pw)
        P.wait("vector", lmc, lm_all)
        for sc in range(2):
            dst = chp[:, 4 * ci + 1 + sc:4 * ci + 2 + sc]
            P.op("vector", "tensor_copy", dst,
                 carr_t[:, 8 * ci + sc:8 * ci + sc + 1])
            for mI in range(1, 4):
                P.selfedge("vector", ve)
                P.op("vector", "scalar_tensor_tensor", dst,
                     carr_t[:, 8 * ci + 2 * mI + sc:8 * ci + 2 * mI + sc + 1],
                     pw[:, mI:mI + 1], dst, op0=ALU.mult, op1=ALU.add)
            P.selfedge("vector", ve)
        ve_cin = ve.n
        for hf in range(2):
            h0 = hf * GH
            if st_prev is not None:
                P.wait("sync", ve, st_prev)
            P.op("sync", "dma_start", out=wkv_t[:],
                 in_=wkvl_h[cs:cs + 128, h0:h0 + GH], inc=[(lm, 16)])
            P.op("sync", "dma_start", out=wk_t[:],
                 in_=wkl_h[cs:cs + 128, h0:h0 + GH], inc=[(lm, 16)])
            P.op("sync", "dma_start", out=rb_t[:],
                 in_=rbf_h[cs:cs + 128, h0:h0 + GH], inc=[(lm, 16)])
            P.op("sync", "dma_start", out=iota_t[:],
                 in_=iota_h[:, h0:h0 + GH], inc=[(lm, 16)])
            lm_c = lm.n
            P.wait("scalar", ve, ve_cin)
            P.wait("scalar", lm, lm_c)
            if st_prev is not None:
                P.wait("scalar", ve, st_prev)
            P.op("scalar", "activation", fac_t[:], iota_t[:], AF.Exp,
                 scale=pw[:, 0:1], inc=[(se, 1)])
            P.op("scalar", "activation", sg_t[:], rb_t[:], AF.Sigmoid,
                 inc=[(se, 1)])
            se_fac = se.n
            P.wait("vector", se, se_fac)
            P.wait("vector", lm, lm_c)
            P.op("vector", "scalar_tensor_tensor", wkv_t[:], fac_t[:],
                 chp[:, 4 * ci + 1:4 * ci + 2], wkv_t[:], op0=ALU.mult,
                 op1=ALU.add)
            P.op("vector", "scalar_tensor_tensor", wk_t[:], fac_t[:],
                 chp[:, 4 * ci + 2:4 * ci + 3], wk_t[:], op0=ALU.mult,
                 op1=ALU.add)
            P.selfedge("vector", ve)
            P.op("vector", "reciprocal", rec_t[:], wk_t[:])
            P.selfedge("vector", ve)
            P.op("vector", "tensor_tensor", wkv_t[:], wkv_t[:], rec_t[:],
                 op=ALU.mult)
            P.selfedge("vector", ve)
            P.wait("vector", st, st.n)
            P.op("vector", "tensor_tensor", rw_t[:], wkv_t[:], sg_t[:],
                 op=ALU.mult, inc=[(ve, 1)])
            ve_rw = ve.n
            st_prev = ve_rw
            P.wait("gpsimd", ve, ve_rw)
            P.op("gpsimd", "dma_start", out=rwscr[cs:cs + 128, h0:h0 + GH],
                 in_=rw_t[:], inc=[(st, 16)])
    st_rw = st.n
    rwst = rr.sb("mp_rwst", (128, KCH * 512), BF16)
    f0st = rr.sb("mp_f0st", (128, 2 * 512), BF16)
    afst = rr.sb("mp_afst", (128, 2 * 512), F32)
    lrw = P.sem("lrw")
    lrf2 = (P.sem("lrf0"), P.sem("lrf1"))
    P.wait("sync", st, st_rw)
    P.wait("tensor", lg2, lg_all)
    af_done = None
    for win in range(G // 512):
        P.wait("sync", mm, mm.n)
        for ki in range(KCH):
            P.op("sync", "dma_start", out=rwst[:, ki * 512:(ki + 1) * 512],
                 in_=rwscr[ki * 128:(ki + 1) * 128, 512 * win:512 * win + 512],
                 inc=[(lrw, 16)])
        lr_w = lrw.n
        P.wait("tensor", lrw, lr_w)
        for oc in range(KCH):
            pp = (win * KCH + oc) % 2
            if af_done is not None and (win * KCH + oc) >= 2:
                P.wait("sync", st, af_done.get((win * KCH + oc) - 2, 0))
            P.op("sync", "dma_start", out=f0st[:, pp * 512:(pp + 1) * 512],
                 in_=f0scr[oc * 128:(oc + 1) * 128, 512 * win:512 * win + 512],
                 inc=[(lrf2[pp], 16)])
            lr_c = lrf2[pp].n
            ib, bank = pool.grab()
            for ki in range(KCH):
                P.op("tensor", "matmul", bank,
                     woT_sb[:, ki * 1024 + oc * 128: ki * 1024 + oc * 128 + 128],
                     rwst[:, ki * 512:(ki + 1) * 512], start=(ki == 0),
                     stop=(ki == KCH - 1),
                     inc=([(mm, 1)] if ki == KCH - 1 else []))
            mmv = mm.n
            P.wait("vector", mm, mmv)
            P.wait("vector", lrf2[pp], lr_c)
            P.op("vector", "tensor_tensor", afst[:, pp * 512:(pp + 1) * 512],
                 bank, f0st[:, pp * 512:(pp + 1) * 512], op=ALU.add,
                 inc=[(ve, 1)])
            pool.release(ib, ve, ve.n)
            P.wait("gpsimd", ve, ve.n)
            P.wait("gpsimd", st, st.n)
            P.op("gpsimd", "dma_start",
                 out=af0_h[oc * 128:(oc + 1) * 128, 512 * win:512 * win + 512],
                 in_=afst[:, pp * 512:(pp + 1) * 512], inc=[(st, 16)])
            if af_done is None:
                af_done = {}
            af_done[win * KCH + oc] = st.n


DEBUG_A = False
DEBUG_NW = None
DEBUG_PHASE = 3
DEBUG_PROJ = 3


def build_launch_A(fx=None):
    NA = G + 8
    if fx is None:
        nc = bass.Bass("TRN2", target_bir_lowering=False)
        dram = lambda name, shape, dt, kind: nc.dram_tensor(
            name, shape, dt, kind=kind).ap()
    else:
        nc = fx["nc"]
        dram = lambda name, shape, dt, kind: fx["io"][name]
    xh = dram("xh", [C, NA], F32, "ExternalInput")
    wkT = dram("wkT", [C, C], F32, "ExternalInput")
    wvT = dram("wvT", [C, C], F32, "ExternalInput")
    wrT = dram("wrT", [C, C], F32, "ExternalInput")
    tmP = dram("tmP", [128, KCH], F32, "ExternalInput")
    tdP = dram("tdP", [128, KCH], F32, "ExternalInput")
    tfP = dram("tfP", [128, KCH], F32, "ExternalInput")
    if DEBUG_A:
        xn_dbg = dram("xn_dbg", [C, NA], BF16, "ExternalOutput")
        xm_dbg = dram("xm_dbg", [C, G], BF16, "ExternalOutput")
        k_dbg = dram("k_dbg", [128, G], F32, "ExternalOutput")
        kv_dbg = dram("kv_dbg", [128, G], F32, "ExternalOutput")
    okind = "ExternalOutput" if fx is None else "Internal"
    wkvl_h = dram("wkvl", [C, G], F32, okind)
    wkl_h = dram("wkl", [C, G], F32, okind)
    rbf_h = dram("rbf", [C, G], BF16, okind)
    carr_h = dram("carr", [C, 2], F32, okind)

    with ExitStack() as stack:
        if fx is None:
            P = Prog(nc, stack)
            rr = R(nc, stack)
        else:
            P, rr = fx["P"], fx["rr"]
            rr.push("A_")
        ld = P.sem("ld")
        wl = P.sem("wl")
        mm = P.sem("mm")
        ve = P.sem("ve")
        se = P.sem("se")
        st = P.sem("st")
        sems = {"mm": mm, "ve": ve, "se": se}

        if fx is None:
            consts = make_consts(P, rr, ve)
        else:
            consts = dict(fx["consts"])
        pr = PsumRing(P, rr, "pr", 6)

        xb = rr.sb("xb", (128, KCH * NA), BF16)
        for ki in range(KCH):
            P.op("gpsimd", "dma_start", out=xb[:, ki * NA:(ki + 1) * NA],
                 in_=xh[ki * 128:(ki + 1) * 128, :], inc=[(ld, 16)])
        ld_x = ld.n
        lp = P.sem("lp")
        pars = rr.sb("pars", (128, 3 * KCH), F32)
        P.op("sync", "dma_start", out=pars[:, 0:KCH], in_=tmP, inc=[(lp, 16)])
        P.op("sync", "dma_start", out=pars[:, KCH:2 * KCH], in_=tdP, inc=[(lp, 16)])
        P.op("sync", "dma_start", out=pars[:, 2 * KCH:3 * KCH], in_=tfP, inc=[(lp, 16)])
        ld_pars = lp.n
        wk_w, _ = load_weight_fm(P, rr, "wk", wkT, C, C, wl)
        wv_w, _ = load_weight_fm(P, rr, "wv", wvT, C, C, wl)
        wr_w, _ = load_weight_fm(P, rr, "wr", wrT, C, C, wl)
        wl_all = wl.n

        emit_ln(P, rr, pr, "ln0", xb, NA, consts, sems, ready=(ld, ld_x))
        ve_ln = ve.n

        xm = rr.sb("xm", (128, KCH * G), BF16)
        dsc = rr.sb("dsc", (128, G), BF16)
        P.wait("vector", lp, ld_pars)
        P.wait("vector", ve, ve_ln)
        for ki in range(KCH):
            xnk = xb[:, ki * NA: (ki + 1) * NA]
            if ki:
                P.wait("vector", ve, ve.n)
            P.op("vector", "tensor_tensor", dsc[:], xnk[:, 1:G + 1], xnk[:, 0:G],
                 op=ALU.subtract)
            P.selfedge("vector", ve)
            P.op("vector", "scalar_tensor_tensor", xm[:, ki * G:(ki + 1) * G],
                 dsc[:], pars[:, ki:ki + 1], xnk[:, 0:G], op0=ALU.mult, op1=ALU.add,
                 inc=[(ve, 1)])
        ve_xm = ve.n
        xmt = fm_tiles(xm, G)

        k_sb = rr.sb("k_sb", (128, G), F32)
        kv_sb = rr.sb("kv_sb", (128, G), F32)
        s1 = rr.sb("s1", (128, G + 1), F32)
        s2 = rr.sb("s2", (128, G + 1), F32)
        wkv_o = rr.sb("wkv_o", (128, G), F32)
        wk_o = rr.sb("wk_o", (128, G), F32)
        r_o = rr.sb("r_o", (128, G), BF16)
        chp = rr.sb("chp", (128, 3 * KCH), F32)

        for ci in range(KCH):
            col = lambda j, _c=ci: chp[:, 3 * _c + j:3 * _c + j + 1]
            P.wait("scalar", lp, ld_pars)
            P.op("scalar", "activation", col(0), pars[:, KCH + ci:KCH + ci + 1], AF.Exp)
            P.op("scalar", "activation", col(2), pars[:, 2 * KCH + ci:2 * KCH + ci + 1],
                 AF.Exp, inc=[(se, 1)])
            se_p1 = se.n
            P.wait("scalar", se, se_p1)
            P.op("scalar", "activation", col(1), col(0), AF.Exp, scale=-1.0,
                 inc=[(se, 1)])
            se_par = se.n

            def evac_k(m0, mr, n0, nw, ps, mmv):
                P.wait("vector", mm, mmv)
                P.op("vector", "tensor_scalar", k_sb[:, n0:n0 + nw], ps, 60.0, None,
                     op0=ALU.min, inc=[(ve, 1)])
                return (ve, ve.n)

            mm_proj(P, pr, wt=wk_w, xt=xmt, kch=KCH, Mtot=C, N=G, mm_sem=mm,
                    evac=evac_k, mlist=[ci * 128],
                    wt_ready=(wl, wl_all), xt_ready=(ve, ve_xm))
            ve_kmin = ve.n
            P.wait("scalar", ve, ve_kmin)
            P.op("scalar", "activation", k_sb[:], k_sb[:], AF.Exp, inc=[(se, 1)])
            se_k = se.n

            def evac_v(m0, mr, n0, nw, ps, mmv):
                P.wait("vector", mm, mmv)
                P.wait("vector", se, se_k)
                P.op("vector", "tensor_tensor", kv_sb[:, n0:n0 + nw], ps,
                     k_sb[:, n0:n0 + nw], op=ALU.mult, inc=[(ve, 1)])
                return (ve, ve.n)

            mm_proj(P, pr, wt=wv_w, xt=xmt, kch=KCH, Mtot=C, N=G, mm_sem=mm,
                    evac=evac_v, mlist=[ci * 128])
            ve_kv = ve.n

            def evac_r(m0, mr, n0, nw, ps, mmv):
                P.wait("scalar", mm, mmv)
                P.op("scalar", "activation", r_o[:, n0:n0 + nw], ps, AF.Copy,
                     inc=[(se, 1)])
                return (se, se.n)

            mm_proj(P, pr, wt=wr_w, xt=xmt, kch=KCH, Mtot=C, N=G, mm_sem=mm,
                    evac=evac_r, mlist=[ci * 128])
            se_r = se.n

            P.op("vector", "memset", s1[:, 0:1], 0.0)
            P.op("vector", "memset", s2[:, 0:1], 0.0)
            P.wait("vector", se, se_par)
            P.wait("vector", ve, ve_kv)
            P.op("vector", "tensor_tensor_scan", s1[:, 1:G + 1],
                 col(1).broadcast_to((128, G)), kv_sb[:], 0.0,
                 op0=ALU.mult, op1=ALU.add, inc=[(ve, 1)])
            P.op("vector", "tensor_tensor_scan", s2[:, 1:G + 1],
                 col(1).broadcast_to((128, G)), k_sb[:], 0.0,
                 op0=ALU.mult, op1=ALU.add, inc=[(ve, 1)])
            ve_scan = ve.n
            P.wait("vector", ve, ve_scan)
            P.op("vector", "scalar_tensor_tensor", wkv_o[:], kv_sb[:], col(2),
                 s1[:, 0:G], op0=ALU.mult, op1=ALU.add, inc=[(ve, 1)])
            P.op("vector", "scalar_tensor_tensor", wk_o[:], k_sb[:], col(2),
                 s2[:, 0:G], op0=ALU.mult, op1=ALU.add)
            P.selfedge("vector", ve)
            P.op("vector", "tensor_scalar", wk_o[:], wk_o[:], 1e-16, None,
                 op0=ALU.add, inc=[(ve, 1)])
            ve_out = ve.n

            P.wait("gpsimd", ve, ve_out)
            P.wait("gpsimd", se, se_r)
            cs = ci * 128
            if DEBUG_A and ci == 0:
                P.op("gpsimd", "dma_start", out=k_dbg, in_=k_sb[:], inc=[(st, 16)])
                P.op("gpsimd", "dma_start", out=kv_dbg, in_=kv_sb[:], inc=[(st, 16)])
                for kj in range(KCH):
                    P.op("gpsimd", "dma_start",
                         out=xn_dbg[kj * 128:(kj + 1) * 128, :],
                         in_=xb[:, kj * NA:(kj + 1) * NA], inc=[(st, 16)])
                    P.op("gpsimd", "dma_start",
                         out=xm_dbg[kj * 128:(kj + 1) * 128, :],
                         in_=xm[:, kj * G:(kj + 1) * G], inc=[(st, 16)])
            P.op("gpsimd", "dma_start", out=wkvl_h[cs:cs + 128, :], in_=wkv_o[:],
                 inc=[(st, 16)])
            P.op("gpsimd", "dma_start", out=wkl_h[cs:cs + 128, :], in_=wk_o[:],
                 inc=[(st, 16)])
            P.op("gpsimd", "dma_start", out=rbf_h[cs:cs + 128, :], in_=r_o[:],
                 inc=[(st, 16)])
            P.op("gpsimd", "dma_start", out=carr_h[cs:cs + 128, 0:1],
                 in_=s1[:, G:G + 1], inc=[(st, 16)],
                 allow_slow_non_contiguous=True)
            P.op("gpsimd", "dma_start", out=carr_h[cs:cs + 128, 1:2],
                 in_=s2[:, G:G + 1], inc=[(st, 16)],
                 allow_slow_non_contiguous=True)
            P.wait("vector", st, st.n)
            P.wait("scalar", st, st.n)

        P.wait("gpsimd", st, st.n)
        if fx is None:
            P.build()
        else:
            quiesce(P, sems)
            rr.pop()
    return nc


def build_launch_D(fx=None):
    TT_ = T
    QLO = TT_ - G
    if fx is None:
        nc = bass.Bass("TRN2", target_bir_lowering=False)
        dram = lambda name, shape, dt, kind="Internal": nc.dram_tensor(
            name, shape, dt, kind=kind).ap()
    else:
        nc = fx["nc"]
        dram = lambda name, shape, dt, kind="Internal": fx["io"][name]
    xT_h = dram("xTf", [C, TT_], F32, "ExternalInput")
    afT_h = dram("afT", [C, TT_], F32, "ExternalInput")
    wq_h = [dram(f"wq{v}", [2048, 512], F32, "ExternalInput")
            for v in range(4)]
    wk_h = [dram(f"wk{v}", [1024, 512], F32, "ExternalInput")
            for v in range(4)]
    wvT_h = dram("wvT1", [C, C], F32, "ExternalInput")
    bv_h = dram("bv1", [1, C], F32, "ExternalInput")
    woT_h = dram("woT1", [C, C], F32, "ExternalInput")
    boP_h = dram("boP1", [128, KCH], F32, "ExternalInput")
    sel128_h = dram("sel128", [128, 128], BF16, "ExternalInput")
    wflm_h = dram("wflm", [64, 128, 1024], F32, "ExternalInput")
    bf_h = dram("bf1", [8, C], F32, "ExternalInput")
    sigsT_h = dram("sigsT", [32, 8], F32, "ExternalInput")
    peT_h = dram("peT", [32, G], F32, "ExternalInput")
    identf_h = dram("identf", [128, 128], F32, "ExternalInput")
    ident8_h = dram("ident8", [8, 8], BF16, "ExternalInput")
    triuD_h = dram("triuD", [128, 128], F32, "ExternalInput")
    kmask_h = dram("kmaskD", [1, TT_], BF16, "ExternalInput")
    out_h = dram("outT", [C, G], F32, "ExternalOutput")
    xn1scr = dram("xn1scr", [C, TT_], BF16)
    jk4scr = dram("jk4scr", [4, 128, TT_], BF16)
    vscr = dram("vscrD", [TT_, 1040], BF16)
    w8scr = dram("w8scr", [64, 128, 1024], BF16)

    with ExitStack() as stack:
        if fx is None:
            P = Prog(nc, stack)
            rr = R(nc, stack)
        else:
            P, rr = fx["P"], fx["rr"]
            rr.push("D_")
        mm = P.sem("mm")
        ve = P.sem("ve")
        se = P.sem("se")
        st = P.sem("st")
        lx = P.sem("lx")
        la = P.sem("la")
        lc = P.sem("lc")
        wl = P.sem("wl")
        ws2 = (P.sem("ws0"), P.sem("ws1"))
        kv2 = (P.sem("kv0"), P.sem("kv1"))
        xr2 = (P.sem("xr0"), P.sem("xr1"))
        pc_in = P.sem("pci")
        pc_out = P.sem("pco")
        sems = {"mm": mm, "ve": ve, "se": se}
        if fx is None:
            consts = make_consts(P, rr, ve)
        else:
            consts = dict(fx["consts"])
        pool = PsumPool(P, rr, "pp")
        lnb = make_ln_bufs(rr, "lnd")
        lnst = LNState()

        xw = rr.sb("xw", (128, KCH * 512), F32)
        xaf = rr.sb("xaf", (128, KCH * 512), F32)
        xnw = rr.sb("xnw", (128, KCH * 520), BF16)
        wstage = rr.sb("wstage", (128, 16 * 512), BF16)
        lsq = rr.sb("lsq", (128, 512), BF16)
        linb = rr.sb("linb", (128, 512), BF16)
        rstdb = rr.sb("rstdD", (128, 512), BF16)
        vw = rr.sb("vw", (128, 4 * 1040), BF16)
        lnf = xaf[:, 512:1024]
        q_pl = rr.sb("q_pl", (128, 4 * G), BF16)
        jkst = rr.sb("jkst", (128, 2 * 512), BF16)
        vst = rr.sb("vst", (128, 2 * 4 * 1040), BF16)
        probsT = rr.sb("probsTD", (128, 4 * 512), BF16)
        o_fm = rr.sb("o_fmD", (128, 16 * 1024), BF16)
        kvar = o_fm
        xbw = o_fm[:, 8192:8192 + KCH * 512]
        acc = rr.sb("accD", (128, KCH * G), BF16)
        wvo = acc
        sumsr = rr.sb("sumsD", (1, 512), F32)
        recipr = rr.sb("recipD", (1, 512), BF16)
        bcn = rr.sb("bcnD", (128, 128), BF16)
        peT = rr.sb("peTs", (32, 128), F32)
        signsT = rr.sb("signsT", (32, 8), F32)
        onehot = rr.sb("onehot", (128, 8), F32)
        onehotT = rr.sb("onehotT", (8, G), BF16)
        ohrow = rr.sb("ohrow", (1, 512), BF16)
        obc = rr.sb("obc", (128, 512), BF16)
        cstD = rr.sb("cstD", (128, 2 * 128 + 8), BF16)
        cstDf = rr.sb("cstDf", (128, 2 * 128), F32)
        kmst = rr.sb("kmstD", (1, 2 * 512), BF16)
        b8 = rr.sb("b8", (8, C), BF16)
        sigs = rr.sb("sigs", (32, 8), F32)
        mwin = rr.sb("mwin", (128, 8), F32)

        P.op("sync", "dma_start", out=cstD[:, 0:128], in_=sel128_h, inc=[(lc, 16)])
        P.op("sync", "dma_start", out=cstD[0:8, 128:136], in_=ident8_h, inc=[(lc, 16)])
        P.op("sync", "dma_start", out=cstDf[:, 0:128], in_=identf_h, inc=[(lc, 16)])
        P.op("sync", "dma_start", out=cstDf[:, 128:256], in_=triuD_h, inc=[(lc, 16)])


        P.op("sync", "dma_start", out=sigs[:], in_=sigsT_h, inc=[(lc, 16)])
        P.op("gpsimd", "dma_start", out=b8[:], in_=bf_h, inc=[(wl, 16)])
        boP = rr.sb("boPD", (128, KCH), F32)
        bvr = rr.sb("bvrD", (1, C), BF16)
        P.op("sync", "dma_start", out=boP[:], in_=boP_h, inc=[(lc, 16)])
        P.op("gpsimd", "dma_start", out=bvr[:], in_=bv_h, inc=[(wl, 16)])
        for ki in range(KCH):
            P.op("gpsimd", "dma_start", out=wvo[:, ki * 1024:(ki + 1) * 1024],
                 in_=wvT_h[ki * 128:(ki + 1) * 128, :], inc=[(wl, 16)])
        lc_all = lc.n
        wl_all = wl.n
        for e in ("tensor", "vector", "scalar"):
            P.wait(e, lc, lc_all)
            P.wait(e, wl, wl_all)

        pst = wstage[:, 0:1024]
        for i in range(64):
            if i >= 1:
                P.wait("gpsimd", pc_out, 16 * i)
            P.op("gpsimd", "dma_start", out=pst, in_=wflm_h[i], inc=[(pc_in, 16)])
            P.wait("scalar", pc_in, pc_in.n)
            P.op("scalar", "dma_start", out=w8scr[i], in_=pst, inc=[(pc_out, 16)])
        pc_done = pc_out.n
        P.wait("sync", pc_out, pc_done)

        P.op("vector", "memset", vw[:], 1.0)
        P.op("vector", "memset", xnw[:], 0.0)
        P.op("vector", "memset", vst[:], 1.0, inc=[(ve, 1)])
        P.selfedge("vector", ve)

        def stream_wvar(src_h, nk, parity):
            wsx = ws2[parity]
            P.wait("gpsimd", mm, mm.n)
            P.wait("gpsimd", wsx, wsx.n)
            P.op("gpsimd", "dma_start",
                 out=wstage[:, 0:nk * 512].rearrange("p (k n) -> p k n", n=512),
                 in_=src_h.rearrange("(k p) n -> p k n", p=128),
                 inc=[(wsx, 16)])
            P.wait("tensor", wsx, wsx.n)

        def plucker(dst, dstoff, w_hs, nk, xt_fn, nw, tabcols):
            for v in range(4):
                stream_wvar(w_hs[v], nk, v % 2)
                for m in range(4):
                    ib, bank = pool.grab()
                    for ki in range(nk):
                        P.op("tensor", "matmul", bank[0:128, 0:nw],
                             wstage[:, ki * 512 + m * 128: ki * 512 + m * 128 + 128],
                             xt_fn(ki), start=(ki == 0), stop=(ki == nk - 1),
                             inc=([(mm, 1)] if ki == nk - 1 else []))
                    mmv = mm.n
                    P.wait("vector", mm, mmv)
                    P.op("vector", "tensor_copy",
                         kvar[:, (v * 4 + m) * 512:(v * 4 + m) * 512 + nw],
                         bank[0:128, 0:nw], inc=[(ve, 1)])
                    pool.release(ib, ve, ve.n)
            P.selfedge("vector", ve)
            for m in range(4):
                t1i = kvar[:, (0 * 4 + m) * 512:(0 * 4 + m) * 512 + nw]
                t1j = kvar[:, (1 * 4 + m) * 512:(1 * 4 + m) * 512 + nw]
                t2i = kvar[:, (2 * 4 + m) * 512:(2 * 4 + m) * 512 + nw]
                t2j = kvar[:, (3 * 4 + m) * 512:(3 * 4 + m) * 512 + nw]
                P.op("vector", "tensor_tensor", lnf[:, 0:nw], t1i, t2j, op=ALU.mult)
                P.op("vector", "tensor_tensor", xaf[:, 0:nw], t1j, t2i,
                     op=ALU.mult)
                P.selfedge("vector", ve)
                P.op("vector", "tensor_tensor", lnf[:, 0:nw], lnf[:, 0:nw],
                     xaf[:, 0:nw], op=ALU.subtract)
                P.selfedge("vector", ve)
                P.op("vector", "tensor_copy", linb[:, 0:nw], lnf[:, 0:nw],
                     inc=[(ve, 1)])
                ve_lin = ve.n
                P.wait("scalar", ve, ve_lin)
                P.op("scalar", "activation", lsq[:, 0:nw], lnf[:, 0:nw], AF.Square,
                     inc=[(se, 1)])
                se_sq = se.n
                ib, bank = pool.grab()
                P.wait("tensor", se, se_sq)
                P.op("tensor", "matmul", bank[0:128, 0:nw], cstD[:, 0:128],
                     lsq[:, 0:nw], start=True, stop=True, inc=[(mm, 1)])
                mm_n2 = mm.n
                P.wait("scalar", mm, mm_n2)
                P.wait("scalar", ve, ve.n)
                P.op("scalar", "activation", rstdb[:, 0:nw], bank[0:128, 0:nw],
                     AF.Ln, bias=consts["cvals"][:, 1:2])
                P.selfedge("scalar", se)
                P.op("scalar", "activation", rstdb[:, 0:nw], rstdb[:, 0:nw],
                     AF.Exp, scale=-0.5, inc=[(se, 1)])
                se_r = se.n
                pool.release(ib, se, se_r)
                P.wait("vector", se, se_r)
                P.op("vector", "tensor_tensor", dst[:, dstoff(m):dstoff(m) + nw],
                     linb[:, 0:nw], rstdb[:, 0:nw], op=ALU.mult, inc=[(ve, 1)])
            P.selfedge("vector", ve)

        st_xn = None
        for win in range(TT_ // 512):
            c0 = 512 * win
            P.wait("sync", ve, ve.n)
            P.wait("gpsimd", ve, ve.n)
            for ki in range(KCH):
                P.op("sync", "dma_start", out=xw[:, ki * 512:(ki + 1) * 512],
                     in_=xT_h[ki * 128:(ki + 1) * 128, c0:c0 + 512],
                     inc=[(lx, 16)])
                P.op("gpsimd", "dma_start", out=xaf[:, ki * 512:(ki + 1) * 512],
                     in_=afT_h[ki * 128:(ki + 1) * 128, c0:c0 + 512],
                     inc=[(la, 16)])
            lx_w = lx.n
            la_w = la.n
            P.wait("vector", lx, lx_w)
            P.wait("vector", la, la_w)
            for ki in range(KCH):
                P.op("vector", "tensor_tensor", xw[:, ki * 512:(ki + 1) * 512],
                     xw[:, ki * 512:(ki + 1) * 512],
                     xaf[:, ki * 512:(ki + 1) * 512], op=ALU.add, inc=[(ve, 1)])
            P.selfedge("vector", ve)
            for ki in range(KCH):
                P.op("vector", "tensor_copy", xbw[:, ki * 512:(ki + 1) * 512],
                     xw[:, ki * 512:(ki + 1) * 512], inc=[(ve, 1)])
            P.selfedge("vector", ve)
            ve_x1 = ve.n
            P.wait("vector", mm, mm.n)
            emit_ln_win(P, pool, lnb, lnst,
                        lambda ki: xbw[:, ki * 512:(ki + 1) * 512],
                        lambda ki: xnw[:, ki * 520 + 8:(ki + 1) * 520], 512,
                        consts, sems, ready=(ve, ve_x1))
            ve_ln = ve.n
            P.wait("gpsimd", ve, ve_ln)
            for ki in range(KCH):
                P.op("gpsimd", "dma_start",
                     out=xn1scr[ki * 128:(ki + 1) * 128, c0:c0 + 512],
                     in_=xnw[:, ki * 520 + 8:(ki + 1) * 520], inc=[(st, 16)])
            st_xn = st.n
            P.wait("tensor", ve, ve_ln)
            for tt in range(4):
                for half in range(2):
                    ib, bank = pool.grab()
                    for ki in range(KCH):
                        P.op("tensor", "matmul", bank,
                             xnw[:, ki * 520 + 8 + 128 * tt:
                                 ki * 520 + 8 + 128 * tt + 128],
                             wvo[:, ki * 1024 + half * 512:
                                 ki * 1024 + half * 512 + 512],
                             start=(ki == 0), stop=False)
                    P.op("tensor", "matmul", bank, consts["ones_row"][0:1, 0:128],
                         bvr[0:1, half * 512:half * 512 + 512], start=False,
                         stop=True, inc=[(mm, 1)])
                    mmv = mm.n
                    dstv = vw[:, tt * 1040:tt * 1040 + 1040].rearrange(
                        "p (h d) -> p h d", d=65)[:, 8 * half:8 * half + 8, 0:64]
                    P.wait("vector", mm, mmv)
                    P.op("vector", "tensor_copy", dstv,
                         bank.rearrange("p (h d) -> p h d", d=64),
                         inc=[(ve, 1)])
                    pool.release(ib, ve, ve.n)
            P.selfedge("vector", ve)
            ve_v = ve.n
            P.wait("gpsimd", ve, ve_v)
            for tt in range(4):
                P.op("gpsimd", "dma_start",
                     out=vscr[c0 + 128 * tt:c0 + 128 * tt + 128, :],
                     in_=vw[:, tt * 1040:(tt + 1) * 1040], inc=[(st, 16)])
            def xk(ki, _c0=c0):
                if ki < 4:
                    return xnw[:, ki * 520 + 8:(ki + 1) * 520]
                return xnw[:, ki * 520 + 7:ki * 520 + 519]
            plucker(probsT, lambda m: m * 512, wk_h, KCH, xk, 512, None)
            ve_jk = ve.n
            P.wait("gpsimd", ve, ve_jk)
            for m in range(4):
                P.op("gpsimd", "dma_start",
                     out=jk4scr[m][:, c0:c0 + 512],
                     in_=probsT[:, m * 512:(m + 1) * 512], inc=[(st, 16)])
            P.wait("vector", mm, mm.n)
            for ki in range(KCH):
                P.op("vector", "tensor_copy", xnw[:, ki * 520:ki * 520 + 8],
                     xnw[:, ki * 520 + 512:ki * 520 + 520], inc=[(ve, 1)])
            P.selfedge("vector", ve)
            P.wait("vector", st, st.n)
        barrier(P, sems)

        def load_xn_win(c0, parity, halo=8):
            xr = xr2[parity]
            P.wait("sync", mm, mm.n)
            P.wait("sync", ve, ve.n)
            for ki in range(KCH):
                P.op("sync", "dma_start", out=xnw[:, ki * 520 + 8 - halo:
                                                 (ki + 1) * 520],
                     in_=xn1scr[ki * 128:(ki + 1) * 128, c0 - halo:c0 + 512],
                     inc=[(xr, 16)])
            for e in ("tensor", "vector", "scalar"):
                P.wait(e, xr, xr.n)

        P.wait("sync", st, st.n)
        for wi in range(4):
            c0 = QLO + 512 * wi
            load_xn_win(c0, wi % 2)

            def xq2(ki):
                if ki < 8:
                    return xnw[:, (ki % 8) * 520 + 8:(ki % 8) * 520 + 520]
                return xnw[:, (ki - 8) * 520 + 7:(ki - 8) * 520 + 519]
            plucker(q_pl, lambda m, _w=wi: m * G + 512 * _w, wq_h, 16, xq2, 512,
                    None)
        barrier(P, sems)

        NQW = G // 128
        for t in range(4):
            for qw in range(NQW):
                qc = QLO + 128 * qw
                nks = (qc + 128 + 511) // 512
                first = True
                kv_par = 0
                se_exp = None
                for ks in range(nks):
                    kc = 512 * ks
                    kts = [kt for kt in range(4) if kc + 128 * kt <= qc]
                    kvs = kv2[kv_par]
                    P.wait("sync", mm, mm.n)
                    P.op("sync", "dma_start",
                         out=jkst[:, kv_par * 512:kv_par * 512 + 512],
                         in_=jk4scr[t][:, kc:kc + 512], inc=[(kvs, 16)])
                    P.op("sync", "dma_start",
                         out=kmst[:, kv_par * 512:kv_par * 512 + 512],
                         in_=kmask_h[:, kc:kc + 512], inc=[(kvs, 16)])
                    for tt in range(4):
                        P.op("sync", "dma_start",
                             out=vst[:, kv_par * 4160 + tt * 1040:
                                     kv_par * 4160 + (tt + 1) * 1040],
                             in_=vscr[kc + 128 * tt:kc + 128 * tt + 128, :],
                             inc=[(kvs, 16)])
                    P.wait("tensor", kvs, kvs.n)
                    if first:
                        for i in range(8):
                            pool.gate(i)
                    else:
                        P.wait("tensor", se, se_exp)
                    for j in range(4):
                        for kt in kts:
                            sv = pool.view(j)[:, 128 * kt:128 * kt + 128]
                            P.op("tensor", "matmul", sv,
                                 jkst[32 * j:32 * j + 6,
                                      kv_par * 512 + 128 * kt:
                                      kv_par * 512 + 128 * kt + 128],
                                 q_pl[32 * j:32 * j + 6,
                                      t * G + (qc - QLO):
                                      t * G + (qc - QLO) + 128],
                                 start=True, stop=False,
                                 tile_position=(32 * j, 0))
                            P.op("tensor", "matmul", sv,
                                 kmst[0:1, kv_par * 512 + 128 * kt:
                                      kv_par * 512 + 128 * kt + 128],
                                 consts["ones_row"][0:1, 0:128], start=False,
                                 stop=True)
                    P.mark("tensor", mm)
                    mm_sim = mm.n
                    if kc + 128 * (len(kts) - 1) == qc:
                        P.wait("vector", mm, mm_sim)
                        for j in range(4):
                            sv = pool.view(j)[:, 128 * (len(kts) - 1):
                                              128 * len(kts)]
                            P.op("vector", "tensor_tensor", sv, sv,
                                 cstDf[:, 128:256], op=ALU.add)
                        P.mark("vector", ve)
                        P.wait("scalar", ve, ve.n)
                    P.wait("scalar", mm, mm_sim)
                    P.wait("scalar", mm, mm.n)
                    nk_ = len(kts)
                    for j in range(4):
                        P.op("scalar", "activation",
                             probsT[:, 512 * j:512 * j + 128 * nk_],
                             pool.view(j)[:, 0:128 * nk_], AF.Exp,
                             inc=([(se, 1)] if j == 3 else []))
                    se_exp = se.n
                    P.wait("tensor", se, se_exp)
                    for j in range(4):
                        h = 4 * t + j
                        for kt in kts:
                            P.op("tensor", "matmul",
                                 pool.view(4 + j)[0:65, 0:128],
                                 vst[:, kv_par * 4160 + kt * 1040 + 65 * h:
                                     kv_par * 4160 + kt * 1040 + 65 * h + 65],
                                 probsT[:, 512 * j + 128 * kt:
                                        512 * j + 128 * kt + 128],
                                 start=(first and kt == kts[0]),
                                 stop=(ks == nks - 1 and kt == kts[-1]))
                    P.mark("tensor", mm)
                    mm_av = mm.n
                    first = False
                    kv_par ^= 1
                P.wait("vector", mm, mm_av)
                for j in range(4):
                    P.op("vector", "tensor_scalar",
                         sumsr[:, 128 * j:128 * j + 128],
                         pool.view(4 + j)[64:65, 0:128], 1e-30, None,
                         op0=ALU.add)
                P.selfedge("vector", ve)
                P.op("vector", "reciprocal", recipr[:], sumsr[:],
                     inc=[(ve, 1)], lowprec=True)
                ve_rec = ve.n
                P.wait("tensor", ve, ve_rec)
                for j in range(4):
                    h = 4 * t + j
                    P.op("tensor", "matmul", pool.view(j)[0:64, 0:128],
                         consts["ones_row"][0:1, 0:64],
                         recipr[0:1, 128 * j:128 * j + 128], start=True,
                         stop=True, inc=[(mm, 1)])
                    mmv = mm.n
                    P.wait("vector", mm, mmv)
                    P.op("vector", "tensor_copy", bcn[0:64, :],
                         pool.view(j)[0:64, 0:128])
                    P.selfedge("vector", ve)
                    P.op("vector", "tensor_tensor",
                         o_fm[64 * (h % 2):64 * (h % 2) + 64,
                              qw * 1024 + (h // 2) * 128:
                              qw * 1024 + (h // 2) * 128 + 128],
                         pool.view(4 + j)[0:64, 0:128], bcn[0:64, :],
                         op=ALU.mult, inc=[(ve, 1)])
                    P.wait("tensor", ve, ve.n)
                for i in range(8):
                    pool.release(i, ve, ve.n)
        barrier(P, sems)

        P.wait("gpsimd", mm, mm.n)
        P.op("gpsimd", "dma_start",
             out=wstage[:].rearrange("p (k n) -> p k n", n=1024),
             in_=woT_h.rearrange("(k p) n -> p k n", p=128), inc=[(wl, 16)])
        P.wait("tensor", wl, wl.n)
        for qw in range(NQW):
            for m in range(KCH):
                ib, bank = pool.grab()
                for ki in range(KCH):
                    P.op("tensor", "matmul", bank[0:128, 0:128],
                         wstage[:, ki * 1024 + m * 128: ki * 1024 + m * 128 + 128],
                         o_fm[:, qw * 1024 + ki * 128: qw * 1024 + ki * 128 + 128],
                         start=(ki == 0), stop=(ki == KCH - 1),
                         inc=([(mm, 1)] if ki == KCH - 1 else []))
                mmv = mm.n
                P.wait("vector", mm, mmv)
                P.op("vector", "tensor_scalar",
                     acc[:, m * G + 128 * qw: m * G + 128 * qw + 128],
                     bank[0:128, 0:128], boP[:, m:m + 1], None, op0=ALU.add,
                     inc=[(ve, 1)])
                pool.release(ib, ve, ve.n)
        barrier(P, sems)

        P.op("scalar", "activation", signsT[:], sigs[:], AF.Sign, inc=[(se, 1)])
        se_sg = se.n
        P.wait("tensor", se, se_sg)
        pe_sem = P.sem("pe")
        for pt in range(G // 128):
            P.wait("sync", mm, mm.n)
            P.op("sync", "dma_start", out=peT[:],
                 in_=peT_h[:, 128 * pt:128 * pt + 128], inc=[(pe_sem, 16)])
            P.wait("tensor", pe_sem, pe_sem.n)
            ib, bank = pool.grab()
            P.op("tensor", "matmul", bank[0:128, 0:8],
                 peT[:], signsT[:], start=True,
                 stop=True, inc=[(mm, 1)])
            mmv = mm.n
            P.wait("vector", mm, mmv)
            P.op("vector", "reduce_max", mwin[:, 0:1], bank[0:128, 0:8], axis=AX,
                 inc=[(ve, 1)])
            P.selfedge("vector", ve)
            P.op("vector", "tensor_scalar", onehot[:], bank[0:128, 0:8],
                 mwin[:, 0:1], None, op0=ALU.is_ge, inc=[(ve, 1)])
            ve_oh = ve.n
            pool.release(ib, ve, ve_oh)
            ib2, bank2 = pool.grab()
            P.wait("tensor", ve, ve_oh)
            P.op("tensor", "matmul", bank2[0:8, 0:128], onehot[:],
                 cstDf[:, 0:128], start=True, stop=True, inc=[(mm, 1)])
            mmv2 = mm.n
            P.wait("vector", mm, mmv2)
            P.op("vector", "tensor_copy", onehotT[:, 128 * pt:128 * pt + 128],
                 bank2[0:8, 0:128], inc=[(ve, 1)])
            pool.release(ib2, ve, ve.n)
        P.selfedge("vector", ve)
        barrier(P, sems)

        for wi in range(4):
            for m in range(KCH):
                ib, bank = pool.grab()
                P.op("tensor", "matmul", bank,
                     b8[:, m * 128:(m + 1) * 128],
                     onehotT[:, 512 * wi:512 * wi + 512], start=True, stop=True,
                     inc=[(mm, 1)])
                mmv = mm.n
                P.wait("vector", mm, mmv)
                P.op("vector", "tensor_tensor",
                     acc[:, m * G + 512 * wi: m * G + 512 * wi + 512], bank,
                     acc[:, m * G + 512 * wi: m * G + 512 * wi + 512],
                     op=ALU.add, inc=[(ve, 1)])
                pool.release(ib, ve, ve.n)
        P.selfedge("vector", ve)
        for tl in range(8):
            wsx = ws2[tl % 2]
            P.wait("gpsimd", mm, mm.n)
            P.wait("gpsimd", wsx, wsx.n)
            P.op("gpsimd", "dma_start",
                 out=wstage[:].rearrange("p (k n) -> p k n", n=1024),
                 in_=w8scr[tl * 8:tl * 8 + 8].rearrange("k p n -> p k n"),
                 inc=[(wsx, 16)])
            P.wait("tensor", wsx, wsx.n)
            for wi in range(4):
                c0 = QLO + 512 * wi
                load_xn_win(c0, wi % 2, halo=0)
                ib, bank = pool.grab()
                P.op("tensor", "matmul", bank[0:1, 0:512],
                     cstD[0:8, 128 + tl:128 + tl + 1],
                     onehotT[:, 512 * wi:512 * wi + 512], start=True, stop=True,
                     inc=[(mm, 1)])
                mmv = mm.n
                P.wait("vector", mm, mmv)
                P.op("vector", "tensor_copy", ohrow[:], bank[0:1, 0:512],
                     inc=[(ve, 1)])
                pool.release(ib, ve, ve.n)
                ib2, bank2 = pool.grab()
                P.wait("tensor", ve, ve.n)
                P.op("tensor", "matmul", bank2, consts["ones_row"],
                     ohrow[:], start=True, stop=True, inc=[(mm, 1)])
                mmv2 = mm.n
                P.wait("vector", mm, mmv2)
                P.op("vector", "tensor_copy", obc[:], bank2, inc=[(ve, 1)])
                pool.release(ib2, ve, ve.n)
                P.selfedge("vector", ve)
                for ki in range(KCH):
                    P.op("vector", "tensor_tensor",
                         xw[:].bitcast(BF16)[:, ki * 512:(ki + 1) * 512],
                         xnw[:, ki * 520 + 8:(ki + 1) * 520], obc[:],
                         op=ALU.mult, inc=[(ve, 1)])
                P.selfedge("vector", ve)
                ve_xm = ve.n
                P.wait("tensor", ve, ve_xm)
                for m in range(KCH):
                    ib3, bank3 = pool.grab()
                    for ki in range(KCH):
                        P.op("tensor", "matmul", bank3,
                             wstage[:, ki * 1024 + m * 128:
                                    ki * 1024 + m * 128 + 128],
                             xw[:].bitcast(BF16)[:, ki * 512:(ki + 1) * 512],
                             start=(ki == 0), stop=(ki == KCH - 1),
                             inc=([(mm, 1)] if ki == KCH - 1 else []))
                    mmv3 = mm.n
                    P.wait("vector", mm, mmv3)
                    P.op("vector", "tensor_tensor",
                         acc[:, m * G + 512 * wi: m * G + 512 * wi + 512],
                         bank3,
                         acc[:, m * G + 512 * wi: m * G + 512 * wi + 512],
                         op=ALU.add, inc=[(ve, 1)])
                    pool.release(ib3, ve, ve.n)
        barrier(P, sems)

        for wi in range(4):
            c0 = QLO + 512 * wi
            P.wait("sync", ve, ve.n)
            P.wait("sync", st, st.n)
            P.wait("gpsimd", ve, ve.n)
            for ki in range(KCH):
                P.op("sync", "dma_start", out=xw[:, ki * 512:(ki + 1) * 512],
                     in_=xT_h[ki * 128:(ki + 1) * 128, c0:c0 + 512],
                     inc=[(lx, 16)])
                P.op("gpsimd", "dma_start", out=xaf[:, ki * 512:(ki + 1) * 512],
                     in_=afT_h[ki * 128:(ki + 1) * 128, c0:c0 + 512],
                     inc=[(la, 16)])
            P.wait("vector", lx, lx.n)
            P.wait("vector", la, la.n)
            for ki in range(KCH):
                P.op("vector", "tensor_tensor", xw[:, ki * 512:(ki + 1) * 512],
                     xw[:, ki * 512:(ki + 1) * 512],
                     xaf[:, ki * 512:(ki + 1) * 512], op=ALU.add)
            P.selfedge("vector", ve)
            for ki in range(KCH):
                P.op("vector", "tensor_tensor", xw[:, ki * 512:(ki + 1) * 512],
                     xw[:, ki * 512:(ki + 1) * 512],
                     acc[:, ki * G + 512 * wi: ki * G + 512 * wi + 512],
                     op=ALU.add, inc=[(ve, 1)])
            ve_out = ve.n
            P.wait("gpsimd", ve, ve_out)
            P.wait("gpsimd", st, st.n)
            P.op("gpsimd", "dma_start",
                 out=out_h.rearrange("(k p) n -> p k n", p=128)[
                     :, :, 512 * wi:512 * wi + 512],
                 in_=xw[:].rearrange("p (k n) -> p k n", n=512),
                 inc=[(st, 16)])
        P.wait("gpsimd", st, st.n)
        assert rr.sb_bytes <= 222 * 1024 * 128, rr.sb_bytes
        if fx is None:
            P.build()
        else:
            quiesce(P, sems)
            rr.pop()
    return nc


RG = [[0, 1, 2, 3], [4, 5, 6, 7]]


def _fused_io(nc):
    def mk(gname, shape, dt, kind="Internal"):
        return nc.dram_tensor(gname, list(shape), dt, kind=kind).ap()

    io = {"A": {}, "B0": {}, "B1": {}, "D": {}, "G": {}}
    A = io["A"]
    A["xh"] = mk("xh", [C, G + 8], F32, "ExternalInput")
    for n in ("wkT", "wvT", "wrT"):
        A[n] = mk(n, [C, C], F32, "ExternalInput")
    for n in ("tmP", "tdP", "tfP"):
        A[n] = mk(n, [128, KCH], F32, "ExternalInput")
    A["wkvl"] = mk("wkvl", [C, G], F32)
    A["wkl"] = mk("wkl", [C, G], F32)
    A["rbf"] = mk("rbf", [C, G], BF16)
    A["carr"] = mk("carr", [C, 2], F32)

    shared = {
        "identbf": mk("identbf", [128, 128], BF16, "ExternalInput"),
        "blockones": mk("blockones", [128, 128], BF16, "ExternalInput"),
        "triuA": mk("triuA", [128, 128], F32, "ExternalInput"),
        "triuB": mk("triuB", [128, 128], F32, "ExternalInput"),
    }
    w1scr = mk("w1scr", [44, 128, 1024], BF16)
    w2scr = mk("w2scr", [22, 128, 1024], BF16)
    for l in (0, 1):
        Bx = io[f"B{l}"]
        Bx.update(shared)
        Bx["w1scr"] = w1scr
        Bx["w2scr"] = w2scr
        pf = f"l{l}_"
        QLO2, XOFF = (520, 1544) if l == 0 else (512, 1024)
        QHI2 = XOFF + G
        Tq = QHI2 - QLO2
        Tk = QHI2 - (QLO2 - 512)
        KML = (XOFF // 128) * 128 + 128
        Bx["wqkv"] = mk(pf + "wqkv", [C, 1536], F32, "ExternalInput")
        Bx["wo"] = mk(pf + "wo", [512, C], F32, "ExternalInput")
        Bx["w1lm"] = mk(pf + "w1lm", [44, 128, 1024], F32, "ExternalInput")
        Bx["w2lm"] = mk(pf + "w2lm", [22, 128, 1024], F32, "ExternalInput")
        Bx["b1aP"] = mk(pf + "b1aP", [128, 22], F32, "ExternalInput")
        Bx["b1gP"] = mk(pf + "b1gP", [128, 22], F32, "ExternalInput")
        Bx["b2P"] = mk(pf + "b2P", [128, KCH], F32, "ExternalInput")
        Bx["lgP"] = mk(pf + "lgP", [128, KCH], F32, "ExternalInput")
        Bx["lbP"] = mk(pf + "lbP", [128, KCH], F32, "ExternalInput")
        Bx["scal4"] = mk(pf + "scal4", [128, 4], F32, "ExternalInput")
        Bx["cq"] = mk(pf + "cq", [128, Tq], F32, "ExternalInput")
        Bx["sq"] = mk(pf + "sq", [128, Tq], F32, "ExternalInput")
        Bx["ck"] = mk(pf + "ck", [128, Tk], F32, "ExternalInput")
        Bx["sk"] = mk(pf + "sk", [128, Tk], F32, "ExternalInput")
        Bx["kmask"] = mk(pf + "kmask", [1, KML], BF16, "ExternalInput")
    io["B0"]["xh2"] = mk("xh2", [C, 3600], F32, "ExternalInput")
    io["B0"]["xout0"] = mk("xout0", [C, 3072], BF16)
    B1 = io["B1"]
    B1["xin0"] = io["B0"]["xout0"]
    B1["af0"] = mk("af0", [C, G], F32)
    B1["f0scr"] = mk("f0scr", [C, G], BF16)
    B1["rwscr"] = mk("rwscr", [C, G], BF16)
    for n in ("wkvl", "wkl", "rbf", "tdP", "tfP"):
        B1[n] = A[n]
    B1["iota2d"] = mk("iota2d", [128, G], F32, "ExternalInput")
    B1["woTr"] = mk("woTr", [C, C], F32, "ExternalInput")
    carrX = mk("carrX", [C, 14], F32)
    B1["carrH"] = carrX[:, 6:14]

    D = io["D"]
    D["xTf"] = mk("xTf", [C, T], F32, "ExternalInput")
    afrot = mk("afrot", [C, 7 * G], F32)
    D["afT"] = afrot[:, 0:T]
    for v in range(4):
        D[f"wq{v}"] = mk(f"wq{v}", [2048, 512], F32, "ExternalInput")
        D[f"wk{v}"] = mk(f"wk{v}", [1024, 512], F32, "ExternalInput")
    D["wvT1"] = mk("wvT1", [C, C], F32, "ExternalInput")
    D["bv1"] = mk("bv1", [1, C], F32, "ExternalInput")
    D["woT1"] = mk("woT1", [C, C], F32, "ExternalInput")
    D["boP1"] = mk("boP1", [128, KCH], F32, "ExternalInput")
    D["sel128"] = mk("sel128", [128, 128], BF16, "ExternalInput")
    D["wflm"] = mk("wflm", [64, 128, 1024], F32, "ExternalInput")
    D["bf1"] = mk("bf1", [8, C], F32, "ExternalInput")
    D["sigsT"] = mk("sigsT", [32, 8], F32, "ExternalInput")
    D["peT"] = mk("peT", [32, G], F32, "ExternalInput")
    D["identf"] = mk("identf", [128, 128], F32, "ExternalInput")
    D["ident8"] = mk("ident8", [8, 8], BF16, "ExternalInput")
    D["triuD"] = mk("triuD", [128, 128], F32, "ExternalInput")
    D["kmaskD"] = mk("kmaskD", [1, T], BF16, "ExternalInput")
    D["outT"] = mk("outT", [C, G], F32, "ExternalOutput")
    D["xn1scr"] = mk("xn1scr", [C, T], BF16)
    D["jk4scr"] = mk("jk4scr", [4, 128, T], BF16)
    D["vscrD"] = mk("vscrD", [T, 1040], BF16)
    D["w8scr"] = mk("w8scr", [64, 128, 1024], BF16)

    Gd = io["G"]
    Gd["carrG"] = mk("carrG", [4 * C, 2], F32)
    Gd["carrX"] = carrX
    Gd["afG"] = mk("afG", [4 * C, G], F32)
    Gd["afrot"] = afrot
    Gd["zeroscr"] = mk("zeroscr", [C, 1024], BF16)
    return io


def _dyn(P, e, eng):
    et = {"sync": mybir.EngineType.SP, "scalar": mybir.EngineType.Activation,
          "gpsimd": mybir.EngineType.Pool}[eng]
    if not hasattr(P, "_dynst"):
        P._dynst = {}
    if eng not in P._dynst:
        pid = P.nc.partition_id((et,))
        jm = pid - (pid >= 4) * 4
        P._dynst[eng] = {"jm": jm, "e": e}
    return P._dynst[eng]


def dyn_dma(P, eng, sem, out_fn, in_ap):
    sem.bump(16)

    def fn(e, _of=out_fn, _i=in_ap, _s=sem, _eng=eng):
        d = _dyn(P, e, _eng)
        return e.dma_start(out=_of(e, d), in_=_i).then_inc(_s.h, 16)

    P.ops[eng].append(fn)
    P.kinds[eng].append("dma")
    P.meta[eng].append([(sem, sem.n)])


def _emit_glue1(P, rr, gio, A_io):
    cc = P.sem("cc1")
    gl = P.sem("gl1")
    gv = P.sem("gv1")
    carrG, carrX = gio["carrG"], gio["carrX"]
    rr.push("G1_")
    zc = rr.sb("z", (128, 16), F32)
    P.op("vector", "memset", zc[:], 0.0, inc=[(gv, 1)])
    P.wait("sync", gv, gv.n)
    for k in range(KCH):
        P.op("sync", "dma_start", out=carrX[k * 128:(k + 1) * 128, :],
             in_=zc[:, 0:14], inc=[(gl, 16)])
    P.op("gpsimd", "collective_compute", "AllGather", ALU.bypass,
         replica_groups=RG, ins=[A_io["carr"].opt()], outs=[carrG.opt()],
         inc=[(cc, 1)])
    P.wait("sync", cc, cc.n)
    P.wait("sync", gl, gl.n)
    for p in range(3):
        dyn_dma(P, "sync", gl,
                lambda e, d, _p=p: carrX[:, bass.ds(
                    _cv(e, d, "carb", lambda jm: (jm + 2) * 2) - _p * 2, 2)],
                carrG[p * C:(p + 1) * C, :])
    for e in ENGINES:
        P.wait(e, gl, gl.n)
        P.wait(e, cc, cc.n)
    rr.pop()


def _cv(e, d, key, expr):
    if key not in d:
        d[key] = e.compute_val(expr(d["jm"]))
    return d[key]


def _emit_glue2(P, rr, gio, B0_io):
    gl = P.sem("gl2")
    gv = P.sem("gv2")
    zeroscr = gio["zeroscr"]
    xout0 = B0_io["xout0"]
    rr.push("G2_")
    zb = rr.sb("z", (128, 1024), BF16)
    P.op("vector", "memset", zb[:], 0.0, inc=[(gv, 1)])
    P.wait("sync", gv, gv.n)
    for k in range(KCH):
        P.op("sync", "dma_start", out=zeroscr[k * 128:(k + 1) * 128, :],
             in_=zb[:], inc=[(gl, 16)])
    P.wait("sync", gl, gl.n)
    P.cond_dma("sync", out=xout0[:, 0:1024], in_=zeroscr, jval=0,
               inc=[(gl, 16)])
    for e in ENGINES:
        P.wait(e, gl, gl.n)
    rr.pop()


def _emit_glue3(P, rr, gio, B1_io):
    cc = P.sem("cc3")
    gl = P.sem("gl3")
    gv = P.sem("gv3")
    afG, afrot = gio["afG"], gio["afrot"]
    rr.push("G3_")
    zf = rr.sb("z", (128, 2048), F32)
    P.op("vector", "memset", zf[:], 0.0, inc=[(gv, 1)])
    P.wait("scalar", gv, gv.n)
    for b in range(4):
        for k in range(KCH):
            P.op("scalar", "dma_start",
                 out=afrot[k * 128:(k + 1) * 128, b * G:(b + 1) * G],
                 in_=zf[:], inc=[(gl, 16)])
    P.op("gpsimd", "collective_compute", "AllGather", ALU.bypass,
         replica_groups=RG, ins=[B1_io["af0"].opt()], outs=[afG.opt()],
         inc=[(cc, 1)])
    P.wait("scalar", cc, cc.n)
    P.wait("scalar", gl, gl.n)
    for p in range(4):
        dyn_dma(P, "scalar", gl,
                lambda e, d, _p=p: afrot[:, bass.ds(
                    _cv(e, d, "rotb", lambda jm: (3 - jm) * G) + _p * G, G)],
                afG[p * C:(p + 1) * C, :])
    for e in ENGINES:
        P.wait(e, gl, gl.n)
        P.wait(e, cc, cc.n)
    rr.pop()


def build_fused(upto="D"):
    order = ("A", "g1", "B0", "g2", "B1", "g3", "D")
    keep = order[:order.index(upto) + 1]
    nc = bass.Bass("TRN2", target_bir_lowering=False)
    io = _fused_io(nc)
    with ExitStack() as stack:
        P = Prog(nc, stack)
        rr = R(nc, stack)
        cv = P.sem("cv")
        consts = make_consts(P, rr, cv)
        fx = {"nc": nc, "P": P, "rr": rr, "consts": consts}
        if "A" in keep:
            fx["io"] = io["A"]
            build_launch_A(fx=fx)
        if "g1" in keep:
            _emit_glue1(P, rr, io["G"], io["A"])
        if "B0" in keep:
            fx["io"] = io["B0"]
            build_launch_B(0, fx=fx)
        if "g2" in keep:
            _emit_glue2(P, rr, io["G"], io["B0"])
        if "B1" in keep:
            fx["io"] = io["B1"]
            build_launch_B(1, fx=fx)
        if "g3" in keep:
            _emit_glue3(P, rr, io["G"], io["B1"])
        if "D" in keep:
            fx["io"] = io["D"]
            build_launch_D(fx=fx)
        assert len(P.all_sems) <= 95, len(P.all_sems)
        P.build()
    return nc


def fmpack(v):
    return np.ascontiguousarray(v.reshape(-1, 128).T)


def ct(a):
    return np.ascontiguousarray(a)


def host_prep_A(ins, xT):
    wkT = ct(ins["L0_attn_Wk"].T)
    wvT = ct(ins["L0_attn_Wv"].T)
    wrT = ct(ins["L0_attn_Wr"].T)
    tmP = fmpack(ins["L0_attn_time_mix"])
    tdP = fmpack(ins["L0_attn_time_decay"])
    tfP = fmpack(ins["L0_attn_time_first"])
    in_maps = []
    for c in range(NCORE):
        s, j = divmod(c, 4)
        o0 = j * G
        xhc = np.zeros((C, G + 8), np.float32)
        lo = max(0, o0 - 1)
        xhc[:, 1 - (o0 - lo):1 + G] = xT[s][:, lo:o0 + G]
        in_maps.append({
            "xh": xhc, "wkT": wkT, "wvT": wvT, "wrT": wrT,
            "tmP": tmP, "tdP": tdP, "tfP": tfP,
        })
    return in_maps


def _rotary_tables():
    half = np.arange(0, 64, 2, dtype=np.float64)
    inv_freq = 1.0 / (10000.0 ** (half / 64))
    t_arr = np.arange(T, dtype=np.float64)
    freqs = t_arr[:, None] * inv_freq[None, :]
    freqs2 = np.concatenate([freqs, freqs], 1)
    base = (half + 0.4 * 64) / (1.4 * 64)
    power = (t_arr - T // 2) / 256.0
    scale = base[None, :] ** power[:, None]
    scale2 = np.concatenate([scale, scale], 1)
    cos = np.cos(freqs2)
    sin = np.sin(freqs2)
    CQ = (cos * scale2).T.astype(np.float32)
    sq0 = (sin * scale2).T.astype(np.float32)[:32]
    SQ = np.concatenate([sq0, -sq0], 0)
    CK = (cos / scale2).T.astype(np.float32)
    sk0 = (sin / scale2).T.astype(np.float32)[:32]
    SK = np.concatenate([sk0, -sk0], 0)

    def dup(a):
        return ct(np.concatenate([a, a], 0))

    return dup(CQ), dup(SQ), dup(CK), dup(SK)


def _tab_slice(tab, start, n):
    out = np.zeros((128, n), np.float32)
    lo = max(0, start)
    hi = min(T, start + n)
    if hi > lo:
        out[:, lo - start:hi - start] = tab[:, lo:hi]
    return ct(out)


def _consts_maps():
    identbf = np.eye(128).astype(BF)
    blockones = np.zeros((128, 128), np.float32)
    blockones[:64, :64] = 1.0
    blockones[64:, 64:] = 1.0
    triuA = np.triu(np.full((128, 128), -1e9, np.float32), 1)
    triuB = np.tril(np.full((128, 128), -1e9, np.float32), -1)
    return {"identbf": identbf, "blockones": blockones.astype(BF),
            "triuA": triuA, "triuB": triuB}


def host_prep_B(l, ins, xT, prev, fused=False):
    wqkv = ct(ins["L0_ffn_wqkv"][l].astype(np.float32))
    qs = ins["L0_ffn_q_scale"][l]
    ksc = ins["L0_ffn_k_scale"][l]
    wo = ct(ins["L0_ffn_wo"][l])
    lg = ins["L0_ffn_ln_g"][l]
    lb = ins["L0_ffn_ln_b"][l]
    w1 = ins["L0_ffn_w1"][l]
    b1 = ins["L0_ffn_b1"][l]
    w2 = ins["L0_ffn_w2"][l]
    b2 = ins["L0_ffn_b2"][l]
    w1p = np.zeros((1024, 5632), np.float32)
    w1p[:, :2730] = w1[:, :2730]
    w1p[:, 2816:2816 + 2730] = w1[:, 2730:]
    b1p = np.zeros(5632, np.float32)
    b1p[:2730] = b1[:2730]
    b1p[2816:2816 + 2730] = b1[2730:]
    w2p = np.zeros((2816, 1024), np.float32)
    w2p[:2730] = w2
    w1lm = ct(w1p.reshape(8, 128, 44, 128).transpose(2, 1, 0, 3).reshape(44, 128, 1024))
    w2lm = ct(w2p.reshape(22, 128, 1024))
    b1aP = fmpack(b1p[:2816])
    b1gP = fmpack(b1p[2816:])
    b2P = fmpack(b2)
    lgP = fmpack(lg)
    lbP = fmpack(lb)
    qs2 = np.concatenate([qs, qs]).astype(np.float32)
    qss2 = np.concatenate([qs[32:], qs[:32], qs[32:], qs[:32]]).astype(np.float32)
    ks2 = np.concatenate([ksc, ksc]).astype(np.float32)
    kss2 = np.concatenate([ksc[32:], ksc[:32], ksc[32:], ksc[:32]]).astype(np.float32)
    scal4 = ct(np.stack([qs2, qss2, ks2, kss2], 1))
    CQ, SQ, CK, SK = _rotary_tables()
    cm = _consts_maps()
    if l == 0:
        NIN, XOFF = 3600, 1544
        QLO = XOFF - 1024
    else:
        NIN, XOFF = 3072, 1024
        QLO = XOFF - 512
    QHI = XOFF + G
    Tq = QHI - QLO
    Tk = QHI - (QLO - 512)
    maps = []
    for c in range(NCORE):
        sq, j = divmod(c, 4)
        o0 = j * G
        m = {
            "wqkv": wqkv, "wo": wo, "w1lm": w1lm, "w2lm": w2lm,
            "b1aP": b1aP, "b1gP": b1gP, "b2P": b2P, "lgP": lgP, "lbP": lbP,
            "scal4": scal4,
            "cq": _tab_slice(CQ, o0 + QLO - XOFF, Tq),
            "sq": _tab_slice(SQ, o0 + QLO - XOFF, Tq),
            "ck": _tab_slice(CK, o0 + QLO - 512 - XOFF, Tk),
            "sk": _tab_slice(SK, o0 + QLO - 512 - XOFF, Tk),
            "identbf": cm["identbf"], "blockones": cm["blockones"],
            "triuA": cm["triuA"], "triuB": cm["triuB"],
        }
        KML = (XOFF // 128) * 128 + 128
        tabs = o0 + np.arange(KML) - XOFF
        m["kmask"] = ct(np.where(tabs < 0, -1e9, 0.0)
                        .astype(BF).reshape(1, KML))
        if l == 0:
            xhc = np.zeros((C, NIN), np.float32)
            lo = max(0, o0 - XOFF)
            hi = min(T, o0 - XOFF + NIN)
            xhc[:, lo - (o0 - XOFF):hi - (o0 - XOFF)] = xT[sq][:, lo:hi]
            m["xh2"] = xhc
        elif fused:
            m["iota2d"] = ct(np.tile(np.arange(G, dtype=np.float32), (128, 1)))
            m["woTr"] = ct(ins["L0_attn_Wo"].T)
        else:
            xo = np.asarray(prev["xout0"][c])
            xo = xo.copy()
            tcols = o0 + np.arange(3072) - XOFF
            xo[:, tcols < 0] = np.zeros((C, 1), dtype=xo.dtype)
            m["xin0"] = xo
            m["wkvl"] = prev["wkvl"][c]
            m["wkl"] = prev["wkl"][c]
            m["rbf"] = prev["rbf"][c]
            carrH = np.zeros((C, 8), np.float32)
            for mi in range(4):
                p = j - 1 - mi
                if p >= 0:
                    carrH[:, 2 * mi:2 * mi + 2] = np.asarray(
                        prev["carr"][sq * 4 + p])
            m["carrH"] = ct(carrH)
            m["tdP"] = fmpack(ins["L0_attn_time_decay"])
            m["tfP"] = fmpack(ins["L0_attn_time_first"])
            m["iota2d"] = ct(np.tile(np.arange(G, dtype=np.float32), (128, 1)))
            m["woTr"] = ct(ins["L0_attn_Wo"].T)
        if fused:
            for nmm in ("wqkv", "wo", "w1lm", "w2lm", "b1aP", "b1gP", "b2P",
                        "lgP", "lbP", "scal4", "cq", "sq", "ck", "sk", "kmask"):
                m[f"l{l}_{nmm}"] = m.pop(nmm)
        maps.append(m)
    return maps


_PAIRS = [(0, 1), (0, 2), (0, 3), (1, 2), (1, 3), (2, 3)]


def _make_pe():
    import math as _m
    pos = np.arange(T, dtype=np.float32)[:, None]
    div = np.exp(np.arange(0, 32, 2, dtype=np.float32) *
                 np.float32(-_m.log(10000.0) / 32))
    ang = (pos * div).astype(np.float32)
    pe = np.zeros((T, 32), np.float32)
    pe[:, 0::2] = np.sin(ang)
    pe[:, 1::2] = np.cos(ang)
    return pe


def _qk_variant(WT, heads, comp, sign=None):
    din = WT.shape[0]
    out = np.zeros((din, 512), np.float32)
    for h in range(16):
        t, jj = divmod(h, 4)
        for p in range(6):
            c = WT[:, 4 * h + comp(p)]
            if sign is not None:
                c = c * sign[p]
            out[:, 128 * t + 32 * jj + p] = c
    return ct(out)


def host_prep_D(ins, xT, prev, fused=False):
    W1qT = ins["L1_attn_W1_q"].T.astype(np.float32)
    W2qT = ins["L1_attn_W2_q"].T.astype(np.float32)
    W1kT = ins["L1_attn_W1_k"].T.astype(np.float32)
    W2kT = ins["L1_attn_W2_k"].T.astype(np.float32)
    sig6 = np.array([1, -1, 1, 1, -1, 1], np.float32)
    perm = lambda p: _PAIRS[5 - p]
    wq = [_qk_variant(W1qT, None, lambda p: _PAIRS[p][0]),
          _qk_variant(W1qT, None, lambda p: _PAIRS[p][1]),
          _qk_variant(W2qT, None, lambda p: _PAIRS[p][0]),
          _qk_variant(W2qT, None, lambda p: _PAIRS[p][1])]
    wk = [_qk_variant(W1kT, None, lambda p: perm(p)[0], sig6),
          _qk_variant(W1kT, None, lambda p: perm(p)[1], sig6),
          _qk_variant(W2kT, None, lambda p: perm(p)[0]),
          _qk_variant(W2kT, None, lambda p: perm(p)[1])]
    wvT = ct(ins["L1_attn_Wv"].T)
    bv = ct(ins["L1_attn_bv"].reshape(1, C))
    woT = ct(ins["L1_attn_Wo"].T)
    boP = fmpack(ins["L1_attn_bo"])
    sel128 = np.zeros((128, 128), np.float32)
    for r in range(128):
        for m in range(128):
            if r // 32 == m // 32 and r % 32 < 6:
                sel128[r, m] = 1.0
    wflm = ct(np.stack([ins["L1_ffn_W"][t].T.reshape(8, 128, 1024)
                        for t in range(8)]).reshape(64, 128, 1024))
    bf1 = ct(ins["L1_ffn_b"].astype(np.float32))
    sigsT = ct(ins["L1_ffn_tile_sigs"].T.astype(np.float32))
    peT_full = ct(_make_pe().T)
    identf = np.eye(128, dtype=np.float32)
    ident8 = np.eye(8, dtype=BF)
    triuD = np.tril(np.full((128, 128), -1e9, np.float32), -1)
    af_seq = None if fused else [
        np.concatenate([np.asarray(prev["af0"][4 * sq + jj])
                        for jj in range(4)], axis=1) for sq in range(2)]
    maps = []
    for c in range(NCORE):
        sq, j = divmod(c, 4)
        o0 = j * G
        Pfx = o0 + G
        xrot = np.zeros((C, T), np.float32)
        xrot[:, T - Pfx:] = xT[sq][:, :Pfx]
        km = np.zeros((1, T), np.float32)
        km[0, :T - Pfx] = -1e9
        m = {
            "xTf": xrot,
            "wvT1": wvT, "bv1": bv, "woT1": woT, "boP1": boP,
            "sel128": sel128.astype(BF), "wflm": wflm, "bf1": bf1,
            "sigsT": sigsT, "peT": ct(peT_full[:, o0:o0 + G]),
            "identf": identf, "ident8": ident8, "triuD": triuD,
            "kmaskD": km.astype(BF),
        }
        for v in range(4):
            m[f"wq{v}"] = wq[v]
            m[f"wk{v}"] = wk[v]
        if not fused:
            afrot = np.zeros((C, T), np.float32)
            afrot[:, T - Pfx:] = af_seq[sq][:, :Pfx]
            m["afT"] = afrot
        maps.append(m)
    return maps


def host_prep_fused(ins, xT):
    mA = host_prep_A(ins, xT)
    mB0 = host_prep_B(0, ins, xT, {}, fused=True)
    mB1 = host_prep_B(1, ins, xT, {}, fused=True)
    mD = host_prep_D(ins, xT, {}, fused=True)
    maps = []
    for c in range(NCORE):
        m = {}
        for part in (mA[c], mB0[c], mB1[c], mD[c]):
            m.update(part)
        maps.append(m)
    return maps


_NC_FUSED = None


def kernel(**inputs):
    global _NC_FUSED
    ins = {k: np.asarray(v) for k, v in inputs.items()}
    xT = ct(ins["L0_attn_x"].transpose(0, 2, 1))
    if _NC_FUSED is None:
        _NC_FUSED = build_fused()
    res = run_bass_kernel_spmd(_NC_FUSED, host_prep_fused(ins, xT),
                               core_ids=list(range(NCORE)))
    out = np.empty((B, T, C), np.float32)
    for c in range(NCORE):
        sq, j = divmod(c, 4)
        out[sq, j * G:(j + 1) * G, :] = np.asarray(res.results[c]["outT"]).T
    return out




# version 04
# speedup vs baseline: 1.5409x; 1.5409x over previous
import math
from contextlib import ExitStack

import numpy as np
import ml_dtypes

import concourse.bass as bass
import concourse.mybir as mybir
from concourse.bass_utils import run_bass_kernel_spmd

F32 = mybir.dt.float32
BF16 = mybir.dt.bfloat16
AF = mybir.ActivationFunctionType
ALU = mybir.AluOpType
AX = mybir.AxisListType.X
BF = ml_dtypes.bfloat16

NCORE = 8
B, T, C = 2, 8192, 1024
G = T * B // NCORE
KCH = C // 128

ENGINES = ("sync", "tensor", "vector", "scalar", "gpsimd")


class Sem:
    __slots__ = ("h", "n", "name")

    def __init__(self, h, name):
        self.h = h
        self.n = 0
        self.name = name

    def bump(self, k):
        self.n += k
        return self.n


class Prog:
    def __init__(self, nc, stack):
        self.nc = nc
        self.stack = stack
        self.ops = {e: [] for e in ENGINES}
        self.kinds = {e: [] for e in ENGINES}
        self.meta = {e: [] for e in ENGINES}
        self.all_sems = []
        self._semcnt = 0

    def sem(self, name=None):
        name = f"{name or 's'}_{self._semcnt}"
        self._semcnt += 1
        h = self.stack.enter_context(self.nc.semaphore(name))
        s = Sem(h, name)
        self.all_sems.append(s)
        return s

    def op(self, eng, method, *args, inc=(), lowprec=False, **kw):
        inc = tuple(inc)

        def fn(e, _m=method, _a=args, _k=kw, _inc=inc, _lp=lowprec):
            if _lp:
                with self.nc.allow_low_precision(reason="bf16 out ok"):
                    ins = getattr(e, _m)(*_a, **_k)
            else:
                ins = getattr(e, _m)(*_a, **_k)
            for s, k in _inc:
                ins = ins.then_inc(s.h, k)
            return ins

        kind = "dma" if method in ("dma_start", "collective_compute") else "op"
        for s, k in inc:
            s.bump(k)
        self.ops[eng].append(fn)
        self.kinds[eng].append(kind)
        self.meta[eng].append([(s, s.n) for s, k in inc])

    def cond_dma(self, eng, *, out, in_, jval, inc=()):
        inc = tuple(inc)
        for s, k in inc:
            s.bump(k)
        et = {"sync": mybir.EngineType.SP, "gpsimd": mybir.EngineType.Pool,
              "scalar": mybir.EngineType.Activation}[eng]
        if not hasattr(self, "_pid"):
            self._pid = {}

        def fn(e, _o=out, _i=in_, _j=jval, _inc=inc, _et=et):
            if _et not in self._pid:
                self._pid[_et] = self.nc.partition_id((_et,))
            pid = self._pid[_et]
            from concourse.expressions import s_logical_or
            cond = s_logical_or(pid == _j, pid == _j + 4)
            ins = e.dma_start(out=_o, in_=_i, cond=cond)
            for s, k in _inc:
                ins = ins.then_inc(s.h, k)
            return ins

        self.ops[eng].append(fn)
        self.kinds[eng].append("dma")
        self.meta[eng].append([(s, s.n) for s, k in inc])

    def wait(self, eng, sem, val=None):
        v = sem.n if val is None else val
        self.ops[eng].append(lambda e, _s=sem.h, _v=v: e.wait_ge(_s, _v))
        self.kinds[eng].append("wait")
        self.meta[eng].append(None)

    def last_op_inc(self, eng):
        for i in range(len(self.ops[eng]) - 1, -1, -1):
            if self.kinds[eng][i] == "wait":
                continue
            if self.kinds[eng][i] != "op":
                return None
            m = self.meta[eng][i]
            return m[-1] if m else None
        return None

    def mark(self, eng, sem, k=1):
        stack = []
        while self.kinds[eng] and self.kinds[eng][-1] != "op":
            stack.append((self.ops[eng].pop(), self.kinds[eng].pop(),
                          self.meta[eng].pop()))
        assert self.kinds[eng], f"no compute op to mark on {eng}"
        fn = self.ops[eng].pop()
        meta = self.meta[eng].pop()
        self.kinds[eng].pop()

        def wrapped(e, _fn=fn, _s=sem, _k=k):
            ins = _fn(e)
            return ins.then_inc(_s.h, _k)

        sem.bump(k)
        self.ops[eng].append(wrapped)
        self.kinds[eng].append("op")
        self.meta[eng].append((meta or []) + [(sem, sem.n)])
        for f, kk, mmx in reversed(stack):
            self.ops[eng].append(f)
            self.kinds[eng].append(kk)
            self.meta[eng].append(mmx)

    def selfedge(self, eng, sem):
        last = self.last_op_inc(eng)
        if last is None:
            self.mark(eng, sem, 1)
            last = (sem, sem.n)
        self.wait(eng, last[0], last[1])

    def build(self):
        with self.nc.Block() as block:
            for e in ENGINES:
                if not self.ops[e]:
                    continue

                def section(eng, _e=e):
                    for fn in self.ops[_e]:
                        fn(eng)

                getattr(block, e)(section)


class R:

    def __init__(self, nc, stack):
        self.nc = nc
        self.stacks = [stack]
        self.tags = [""]
        self.sb_bytes = 0
        self.scope_bytes = [0]
        self.names = set()

    def push(self, tag=""):
        st = ExitStack()
        self.stacks.append(st)
        self.tags.append(tag)
        self.scope_bytes.append(0)
        return st

    def pop(self):
        self.tags.pop()
        self.sb_bytes -= self.scope_bytes.pop()
        self.stacks.pop().close()

    def sb(self, name, shape, dtype):
        name = self.tags[-1] + name
        assert name not in self.names, name
        self.names.add(name)
        nbytes = shape[1] * (2 if dtype == BF16 else 4)
        self.sb_bytes += nbytes * 128
        self.scope_bytes[-1] += nbytes * 128
        t = self.stacks[-1].enter_context(
            self.nc.sbuf_tensor(name, list(shape), dtype))
        return t

    def ps(self, name, shape, dtype=F32):
        name = self.tags[-1] + name
        assert name not in self.names, name
        self.names.add(name)
        return self.stacks[-1].enter_context(
            self.nc.psum_tensor(name, list(shape), dtype))


class PsumRing:

    def __init__(self, P, rr, name, n, shape=(128, 512), dtype=F32):
        self.P = P
        self.tiles = [rr.ps(f"{name}{i}", list(shape), dtype) for i in range(n)]
        self.free_at = [None] * n
        self.i = 0

    def grab(self):
        i = self.i
        self.i = (self.i + 1) % len(self.tiles)
        if self.free_at[i] is not None:
            s, v = self.free_at[i]
            self.P.wait("tensor", s, v)
        return i, self.tiles[i]

    def release(self, i, sem, val):
        self.free_at[i] = (sem, val)


def load_weight_fm(P, rr, name, hbm_ap, Ktot, Mtot, ldsem, queue="gpsimd"):
    kch = Ktot // 128
    sb = rr.sb(name, (128, kch * Mtot), BF16)
    for ki in range(kch):
        P.op(queue, "dma_start", out=sb[:, ki * Mtot:(ki + 1) * Mtot],
             in_=hbm_ap[ki * 128:(ki + 1) * 128, :], inc=[(ldsem, 16)])

    def w(ki, m0, mrows):
        return sb[:, ki * Mtot + m0: ki * Mtot + m0 + mrows]

    return w, ldsem.n


def fm_tiles(sb, N):

    def x(ki, n0, nw):
        return sb[:, ki * N + n0: ki * N + n0 + nw]

    return x


def mm_proj(P, pr, *, wt, xt, kch, Mtot, N, mm_sem, evac, nwin=512, rank1=(),
            wt_ready=None, xt_ready=None, mlist=None):
    first = True
    for m0 in (mlist if mlist is not None else range(0, Mtot, 128)):
        mr = min(128, Mtot - m0)
        for n0 in range(0, N, nwin):
            nw = min(nwin, N - n0)
            i, tile = pr.grab()
            if first:
                if wt_ready is not None:
                    P.wait("tensor", *wt_ready)
                if xt_ready is not None:
                    P.wait("tensor", *xt_ready)
                first = False
            nterms = kch + len(rank1)
            for ki in range(kch):
                last = ki == nterms - 1
                P.op("tensor", "matmul", tile[0:mr, 0:nw], wt(ki, m0, mr),
                     xt(ki, n0, nw), start=(ki == 0), stop=last,
                     inc=([(mm_sem, 1)] if last else []))
            for rj, (la, ra) in enumerate(rank1):
                last = kch + rj == nterms - 1
                P.op("tensor", "matmul", tile[0:mr, 0:nw], la(m0, mr), ra(n0, nw),
                     start=False, stop=last, inc=([(mm_sem, 1)] if last else []))
            free = evac(m0, mr, n0, nw, tile[0:mr, 0:nw], mm_sem.n)
            pr.release(i, *free)


def make_ln_bufs(rr, tag, x2=None):
    return {
        "x2": (x2 if x2 is not None else rr.sb(f"{tag}_x2", (128, KCH * 512), BF16)),
        "zsc": rr.sb(f"{tag}_z", (128, 512), BF16),
        "rowbuf": rr.sb(f"{tag}_rows", (1, 3 * 512), F32),
        "rowbf": rr.sb(f"{tag}_rowbf", (1, 2 * 512), BF16),
        "bc": rr.sb(f"{tag}_bc", (128, 2 * 512), BF16),
    }


class LNState:
    def __init__(self):
        self.mm_stats_prev = None
        self.ve_rows_prev = None
        self.ve_apply_prev = None


def emit_ln_win(P, pr, bufs, st8, xt, ot, nw, consts, sems, gain=None, bias=None,
                ready=None):
    mm, ve, se = sems["mm"], sems["ve"], sems["se"]
    x2, zsc = bufs["x2"], bufs["zsc"]
    rowbuf, rowbf, bc = bufs["rowbuf"], bufs["rowbf"], bufs["bc"]
    if ready is not None:
        P.wait("scalar", *ready)
    if st8.mm_stats_prev is not None:
        P.wait("scalar", mm, st8.mm_stats_prev)
    if st8.ve_apply_prev is not None:
        P.wait("scalar", ve, st8.ve_apply_prev)
    for ki in range(KCH):
        P.op("scalar", "activation", x2[:, ki * 512: ki * 512 + nw],
             xt(ki), AF.Square, inc=[(se, 1)])
    se_sq = se.n
    i1, t1 = pr.grab()
    i2, t2 = pr.grab()
    if ready is not None:
        P.wait("tensor", *ready)
    P.wait("tensor", se, se_sq)
    for ki in range(KCH):
        P.op("tensor", "matmul", t1[0:1, 0:nw], consts["ones_col"], xt(ki),
             start=(ki == 0), stop=(ki == KCH - 1),
             inc=([(mm, 1)] if ki == KCH - 1 else []))
    for ki in range(KCH):
        P.op("tensor", "matmul", t2[0:1, 0:nw], consts["ones_col"],
             x2[:, ki * 512: ki * 512 + nw],
             start=(ki == 0), stop=(ki == KCH - 1),
             inc=([(mm, 1)] if ki == KCH - 1 else []))
    mm_stats = mm.n
    st8.mm_stats_prev = mm_stats
    mrow = rowbuf[:, 0:nw]
    e2row = rowbuf[:, 512:512 + nw]
    m2row = rowbuf[:, 1024:1024 + nw]
    varrow = m2row
    lnrow = m2row
    rstd = m2row
    P.wait("scalar", mm, mm_stats)
    if st8.ve_rows_prev is not None:
        P.wait("scalar", ve, st8.ve_rows_prev)
    P.op("scalar", "activation", mrow, t1[0:1, 0:nw], AF.Copy, scale=1.0 / C)
    P.op("scalar", "activation", e2row, t2[0:1, 0:nw], AF.Copy, scale=1.0 / C,
         inc=[(se, 1)])
    se_rows = se.n
    P.wait("vector", se, se_rows)
    P.op("vector", "tensor_tensor", m2row, mrow, mrow, op=ALU.mult)
    P.selfedge("vector", ve)
    P.op("vector", "tensor_tensor", varrow, e2row, m2row, op=ALU.subtract,
         inc=[(ve, 1)])
    ve_var = ve.n
    P.wait("scalar", ve, ve_var)
    P.op("scalar", "activation", lnrow, varrow, AF.Ln,
         bias=consts["cvals"][0:1, 0:1])
    P.selfedge("scalar", se)
    P.op("scalar", "activation", rstd, lnrow, AF.Exp, scale=-0.5, inc=[(se, 1)])
    se_rstd = se.n
    P.wait("vector", se, se_rstd)
    P.op("vector", "tensor_copy", rowbf[:, 0:nw], mrow)
    P.op("vector", "tensor_copy", rowbf[:, 512:512 + nw], rstd, inc=[(ve, 1)])
    ve_rows = ve.n
    st8.ve_rows_prev = ve_rows
    i3, t3 = pr.grab()
    i4, t4 = pr.grab()
    P.wait("tensor", ve, ve_rows)
    P.op("tensor", "matmul", t3[:, 0:nw], consts["ones_row"], rowbf[:, 0:nw],
         start=True, stop=True, inc=[(mm, 1)])
    P.op("tensor", "matmul", t4[:, 0:nw], consts["ones_row"],
         rowbf[:, 512:512 + nw], start=True, stop=True, inc=[(mm, 1)])
    mm_bc = mm.n
    P.wait("vector", mm, mm_bc)
    P.op("vector", "tensor_copy", bc[:, 0:nw], t3[:, 0:nw])
    P.op("vector", "tensor_copy", bc[:, 512:512 + nw], t4[:, 0:nw], inc=[(ve, 1)])
    P.selfedge("vector", ve)
    pr.release(i1, ve, ve.n)
    pr.release(i2, ve, ve.n)
    pr.release(i3, ve, ve.n)
    pr.release(i4, ve, ve.n)
    for ki in range(KCH):
        P.op("vector", "tensor_tensor", x2[:, ki * 512: ki * 512 + nw],
             xt(ki), bc[:, 0:nw], op=ALU.subtract)
    P.selfedge("vector", ve)
    if gain is None:
        for ki in range(KCH):
            P.op("vector", "tensor_tensor", ot(ki),
                 x2[:, ki * 512: ki * 512 + nw],
                 bc[:, 512:512 + nw], op=ALU.mult, inc=[(ve, 1)])
    else:
        for ki in range(KCH):
            P.op("vector", "tensor_tensor", x2[:, ki * 512: ki * 512 + nw],
                 x2[:, ki * 512: ki * 512 + nw],
                 bc[:, 512:512 + nw], op=ALU.mult)
        P.selfedge("vector", ve)
        for ki in range(KCH):
            P.op("vector", "tensor_scalar", ot(ki),
                 x2[:, ki * 512: ki * 512 + nw],
                 gain(ki), bias(ki), op0=ALU.mult, op1=ALU.add, inc=[(ve, 1)])
    st8.ve_apply_prev = ve.n
    return ve.n


def emit_ln(P, rr, pr, tag, xb, N, consts, sems, gain=None, bias=None, out=None,
            ready=None):
    out = out if out is not None else xb
    bufs = make_ln_bufs(rr, tag)
    st8 = LNState()
    xt0 = fm_tiles(xb, N)
    ot0 = fm_tiles(out, N)
    r = None
    first = True
    for n0 in range(0, N, 512):
        nw = min(512, N - n0)
        xt = lambda ki, _n0=n0, _nw=nw: xt0(ki, _n0, _nw)
        ot = lambda ki, _n0=n0, _nw=nw: ot0(ki, _n0, _nw)
        r = emit_ln_win(P, pr, bufs, st8, xt, ot, nw, consts, sems,
                        gain=gain, bias=bias, ready=(ready if first else None))
        first = False
    return r


def make_consts(P, rr, ve, tag=""):
    ones_col = rr.sb(f"cons_col{tag}", (128, 1), BF16)
    ones_row = rr.sb(f"cons_row{tag}", (1, 128), BF16)
    cvals = rr.sb(f"cons_vals{tag}", (128, 4), F32)
    P.op("vector", "memset", ones_col[:], 1.0)
    P.op("vector", "memset", ones_row[:], 1.0)
    P.op("vector", "memset", cvals[:, 0:1], 1e-5)
    P.op("vector", "memset", cvals[:, 1:2], 1e-24)
    P.op("vector", "memset", cvals[:, 2:3], 0.0)
    P.op("vector", "memset", cvals[:, 3:4], 1.0, inc=[(ve, 1)])
    d = {"ones_col": ones_col[:], "ones_row": ones_row[:], "cvals": cvals,
         "ready": ve.n}
    for e in ("tensor", "scalar", "gpsimd"):
        P.wait(e, ve, d["ready"])
    return d


class PsumPool:

    def __init__(self, P, rr, tag):
        self.P = P
        self.tA = [rr.ps(f"{tag}A{i}", [128, 1024], F32) for i in range(2)]
        self.tB = [rr.ps(f"{tag}B{i}", [128, 512], F32) for i in range(4)]
        self.free_at = [None] * 8
        self.i = 0

    def view(self, idx):
        if idx < 4:
            return self.tA[idx // 2][:, 512 * (idx % 2):512 * (idx % 2) + 512]
        return self.tB[idx - 4][:, :]

    def wide(self, j):
        return self.tA[j][:, :]

    def gate(self, idx, eng="tensor"):
        f = self.free_at[idx]
        if f is not None:
            self.P.wait(eng, f[0], f[1])
            self.free_at[idx] = None

    def release(self, idx, sem, val):
        self.free_at[idx] = (sem, val)

    def grab(self):
        i = self.i
        self.i = (self.i + 1) % 6
        self.gate(i)
        return i, self.view(i)


def quiesce(P, sems):
    barrier(P, sems, engines=("tensor", "vector", "scalar", "gpsimd"))
    targets = [(sm, sm.n) for sm in P.all_sems if sm.n > 0]
    for e in ENGINES:
        if not any(k == "op" for k in P.kinds[e]):
            continue
        for (sm, v) in targets:
            P.wait(e, sm, v)


def barrier(P, sems, engines=("tensor", "vector", "scalar")):
    names = {"tensor": "mm", "vector": "ve", "scalar": "se", "gpsimd": "ve"}
    marks = []
    for e in engines:
        if not any(k == "op" for k in P.kinds[e]):
            continue
        last = P.last_op_inc(e)
        if last is None:
            s = sems[names[e]]
            P.mark(e, s)
            last = (s, s.n)
        marks.append((e,) + last)
    for e in engines:
        if not any(k == "op" for k in P.kinds[e]):
            continue
        for (e2, s, v) in marks:
            if e2 != e:
                P.wait(e, s, v)


def emit_ffn_inner(P, rr, pool, tag, *, xin, QLO, QHI, wq_sb, tabs, wts, consts,
                   sems, out_cb, kmask_bf, xoff, lnb=None, xin_load=None,
                   wq_load=None, xo_stage=None, OUT0=0):
    mm, ve, se = sems["mm"], sems["ve"], sems["se"]
    tab = sems["tab"]
    wst2 = sems["wst2"]
    NW = (QHI - QLO) // 512
    if DEBUG_NW is not None:
        NW = min(NW, DEBUG_NW)
    KLO = QLO - 512

    kn = rr.sb(f"{tag}_kn", (128, 2 * 4 * 512), BF16)
    vtok = rr.sb(f"{tag}_v", (128, 2 * 4 * 520), BF16)
    qn = rr.sb(f"{tag}_qn", (128, 4 * 512), BF16)
    q2s = rr.sb(f"{tag}_q2", (128, 512), BF16)
    rstdb = rr.sb(f"{tag}_rstd", (128, 512), BF16)
    tmp1 = rr.sb(f"{tag}_t1", (128, 512), BF16)
    tmp2 = rr.sb(f"{tag}_t2", (128, 512), BF16)
    tmp3 = rr.sb(f"{tag}_t3", (128, 512), BF16)
    tabst = rr.sb(f"{tag}_tabs", (128, 2 * 512), BF16)
    mf32 = rr.sb(f"{tag}_mf", (128, 8), F32)
    mbuf = rr.sb(f"{tag}_mb", (128, 8), BF16)
    mrows = rr.sb(f"{tag}_mrows", (1, 8 * 128), BF16)
    probsT = rr.sb(f"{tag}_pT", (128, 8 * 128), BF16)
    sumsr = rr.sb(f"{tag}_sums", (1, 1024), F32)
    recipr = rr.sb(f"{tag}_recip", (1, 1024), BF16)
    o_fm = rr.sb(f"{tag}_ofm", (128, 4 * 128), BF16)
    bcn = rr.sb(f"{tag}_bcn", (128, 128), BF16)
    xa = rr.sb(f"{tag}_xa", (128, 2 * KCH * 520), BF16)
    hln = rr.sb(f"{tag}_hln", (128, KCH * 512), BF16)
    ffnh = rr.sb(f"{tag}_ffnh", (128, 6 * 512), BF16)
    gts = rr.sb(f"{tag}_gt", (128, 512), BF16)
    w1st = rr.sb(f"{tag}_w1st", (128, 4 * 1024), BF16)
    w2st = rr.sb(f"{tag}_w2st", (128, 1024), BF16)
    if xo_stage is None:
        xo_stage = rr.sb(f"{tag}_xo", (128, KCH * 512), BF16)
    if lnb is None:
        lnb = make_ln_bufs(rr, f"{tag}_iln")
    lnst = LNState()
    state = {"store": None}

    P.op("vector", "memset", vtok[:], 1.0)
    P.op("vector", "memset", xa[:], 0.0, inc=[(ve, 1)])
    P.selfedge("vector", ve)

    def xsh(ki, c0, nw):
        return xin(ki, c0 - (1 if ki >= 4 else 0), nw)

    def kslot(c0):
        return ((c0 - KLO) // 512) % 2

    def kn_ap(c0, base_par, j0, nj):
        slot = kslot(c0)
        hp = None
        return kn[base_par:base_par + 64,
                  slot * 2048 + j0: slot * 2048 + j0 + nj]

    def project_qk(c0, is_q):
        if DEBUG_PROJ < 2:
            return 0
        if is_q:
            t0 = c0 - QLO
            P.wait("gpsimd", ve, ve.n)
            P.op("gpsimd", "dma_start", out=tabst[:, 0:512],
                 in_=tabs["cq"][:, t0:t0 + 512], inc=[(tab, 16)])
            P.op("gpsimd", "dma_start", out=tabst[:, 512:1024],
                 in_=tabs["sq"][:, t0:t0 + 512], inc=[(tab, 16)])
            ctab, stab = tabst[:, 0:512], tabst[:, 512:1024]
            scl, scls = wts["qs"], wts["qss"]
            mlo = 0
        else:
            t0 = c0 - KLO
            P.wait("gpsimd", ve, ve.n)
            P.op("gpsimd", "dma_start", out=tabst[:, 0:512],
                 in_=tabs["ck"][:, t0:t0 + 512], inc=[(tab, 16)])
            P.op("gpsimd", "dma_start", out=tabst[:, 512:1024],
                 in_=tabs["sk"][:, t0:t0 + 512], inc=[(tab, 16)])
            ctab, stab = tabst[:, 0:512], tabst[:, 512:1024]
            scl, scls = wts["ks"], wts["kss"]
            mlo = 512
        tab_val = tab.n
        if wq_load is not None:
            wq_load(mlo // 512)
        for hp in range(4):
            m0 = mlo + hp * 128
            ib, bank = pool.grab()
            for ki in range(KCH):
                P.op("tensor", "matmul", bank, wq_sb(ki, m0, 128),
                     xsh(ki, c0, 512), start=(ki == 0), stop=(ki == KCH - 1),
                     inc=([(mm, 1)] if ki == KCH - 1 else []))
            mmv = mm.n
            if is_q:
                raw = qn[:, hp * 512:(hp + 1) * 512]
            else:
                slot = kslot(c0)
                raw = kn[:, slot * 2048 + hp * 512: slot * 2048 + (hp + 1) * 512]
            if DEBUG_PROJ == 21:
                pool.release(ib, mm, mmv)
                continue
            P.wait("scalar", mm, mmv)
            P.op("scalar", "activation", q2s[:], bank, AF.Square, inc=[(se, 1)])
            se_sq = se.n
            if DEBUG_PROJ == 22:
                pool.release(ib, se, se_sq)
                continue
            P.wait("vector", mm, mmv)
            P.wait("vector", se, se_sq)
            P.op("vector", "tensor_copy", raw, bank, inc=[(ve, 1)])
            pool.release(ib, ve, ve.n)
            if DEBUG_PROJ < 3:
                continue
            ib2, bank2 = pool.grab()
            P.wait("tensor", se, se_sq)
            P.op("tensor", "matmul", bank2, consts["blockones"], q2s[:],
                 start=True, stop=True, inc=[(mm, 1)])
            mm_n2 = mm.n
            P.wait("scalar", mm, mm_n2)
            P.wait("scalar", ve, ve.n)
            P.op("scalar", "activation", rstdb[:], bank2, AF.Ln,
                 bias=consts["cvals"][:, 1:2])
            P.selfedge("scalar", se)
            P.op("scalar", "activation", rstdb[:], rstdb[:], AF.Exp, scale=-0.5,
                 inc=[(se, 1)])
            se_rstd = se.n
            pool.release(ib2, se, se_rstd)
            P.wait("vector", se, se_rstd)
            P.wait("vector", tab, tab_val)
            P.selfedge("vector", ve)
            P.op("vector", "tensor_tensor", tmp1[:], raw, rstdb[:], op=ALU.mult)
            P.selfedge("vector", ve)
            P.op("vector", "scalar_tensor_tensor", tmp2[:], tmp1[:], scl, ctab,
                 op0=ALU.mult, op1=ALU.mult)
            for (a, b) in ((0, 32), (32, 0), (64, 96), (96, 64)):
                P.op("vector", "scalar_tensor_tensor", tmp3[a:a + 32, :],
                     tmp1[b:b + 32, :], scl[b:b + 32, 0:1], stab[b:b + 32, :],
                     op0=ALU.mult, op1=ALU.mult)
            P.selfedge("vector", ve)
            P.op("vector", "tensor_tensor", raw, tmp2[:], tmp3[:],
                 op=ALU.add, inc=[(ve, 1)])
        return ve.n

    def project_v(c0):
        if DEBUG_PROJ < 1:
            return
        slot = kslot(c0)
        if wq_load is not None:
            wq_load(2)
        for tt in range(4):
            tc = c0 + tt * 128
            ib, bank = pool.grab()
            for ki in range(KCH):
                P.op("tensor", "matmul", bank, xsh(ki, tc, 128),
                     wq_sb(ki, 1024, 512), start=(ki == 0),
                     stop=(ki == KCH - 1),
                     inc=([(mm, 1)] if ki == KCH - 1 else []))
            mmv = mm.n
            base = slot * 4 * 520 + tt * 520
            dstv = vtok[:, base:base + 520].rearrange(
                "p (h d) -> p h d", d=65)[:, :, 0:64]
            P.wait("vector", mm, mmv)
            P.op("vector", "tensor_copy", dstv, bank, inc=[(ve, 1)])
            pool.release(ib, ve, ve.n)

    def attention(c0):
        wpar = ((c0 - QLO) // 512) % 2
        for qt in range(4):
            qc = c0 + 128 * qt
            for par in range(2):
                pool.gate(2 * par)
                pool.gate(2 * par + 1)
            for hp in range(4):
                for kh in range(2):
                    khc = c0 - 512 + 512 * kh
                    slot = kslot(khc)
                    for par in range(2):
                        wide = pool.wide(par)
                        P.op("tensor", "matmul", wide[:, 512 * kh:512 * kh + 512],
                             qn[64 * par:64 * par + 64,
                                hp * 512 + 128 * qt: hp * 512 + 128 * qt + 128],
                             kn[64 * par:64 * par + 64,
                                slot * 2048 + hp * 512: slot * 2048 + hp * 512 + 512],
                             start=True, stop=True, tile_position=(64 * par, 0),
                             inc=[(mm, 1)])
                mm_simA = mm.n
                P.wait("vector", mm, mm_simA)
                for par in range(2):
                    wide = pool.wide(par)
                    P.op("vector", "tensor_tensor",
                         wide[:, 512 + 128 * qt:512 + 128 * qt + 128],
                         wide[:, 512 + 128 * qt:512 + 128 * qt + 128],
                         consts["triuA"], op=ALU.add)
                P.selfedge("vector", ve)
                for par in range(2):
                    wide = pool.wide(par)
                    h = 2 * hp + par
                    P.op("vector", "reduce_max", mf32[:, h:h + 1],
                         wide[:, 0:512 + 128 * (qt + 1)],
                         axis=AX, inc=[(ve, 1)])
                ve_max = ve.n
                for par in range(2):
                    pool.release(2 * par, ve, ve_max)
                    pool.release(2 * par + 1, ve, ve_max)
                if hp < 3:
                    for par in range(2):
                        pool.gate(2 * par)
                        pool.gate(2 * par + 1)
            ve_maxall = ve.n
            P.wait("scalar", ve, ve_maxall)
            P.op("scalar", "activation", mbuf[:], mf32[:], AF.Copy, scale=-1.0,
                 inc=[(se, 1)])
            se_negm = se.n
            pool.gate(6)
            P.wait("tensor", se, se_negm)
            for h in range(8):
                P.op("tensor", "matmul", pool.view(6)[0:1, 0:128],
                     mbuf[:, h:h + 1], consts["identbf"], start=True, stop=True,
                     inc=[(mm, 1)])
                mmv = mm.n
                P.wait("vector", mm, mmv)
                P.op("vector", "tensor_copy", mrows[:, 128 * h:128 * h + 128],
                     pool.view(6)[0:1, 0:128], inc=[(ve, 1)])
                P.wait("tensor", ve, ve.n)
            pool.release(6, ve, ve.n)

            kts = [kt for kt in range(8) if c0 - 512 + 128 * kt <= qc]
            first_kt = kts[0]
            ve_rec = None
            for hpass in range(2):
                heads = [4 * hpass + i for i in range(4)]
                se_exp_prev = None
                mm_av_prev = None
                for kt in kts:
                    ktc = c0 - 512 + 128 * kt
                    slot = kslot(ktc)
                    j0 = (ktc - (c0 - 512)) % 512
                    if se_exp_prev is not None:
                        P.wait("tensor", se, se_exp_prev)
                    else:
                        pool.gate(0)
                        pool.gate(1)
                        if hpass == 0:
                            pool.gate(2)
                            pool.gate(3)
                        if kt == first_kt:
                            for i in range(4):
                                pool.gate(4 + i)
                    for i, h in enumerate(heads):
                        par = h % 2
                        hp = h // 2
                        sv = pool.wide(par)[:, 128 * (i // 2):128 * (i // 2) + 128]
                        P.op("tensor", "matmul", sv,
                             kn[64 * par:64 * par + 64,
                                slot * 2048 + hp * 512 + j0:
                                slot * 2048 + hp * 512 + j0 + 128],
                             qn[64 * par:64 * par + 64,
                                hp * 512 + 128 * qt: hp * 512 + 128 * qt + 128],
                             start=True, stop=False, tile_position=(64 * par, 0))
                        P.op("tensor", "matmul", sv,
                             consts["ones_row"][0:1, 0:128],
                             mrows[0:1, 128 * h:128 * h + 128], start=False,
                             stop=(ktc >= xoff or kmask_bf is None))
                        if ktc < xoff and kmask_bf is not None:
                            P.op("tensor", "matmul", sv,
                                 kmask_bf[0:1, ktc:ktc + 128],
                                 consts["ones_row"][0:1, 0:128], start=False,
                                 stop=True)
                    P.mark("tensor", mm)
                    mm_simB = mm.n
                    if ktc == qc:
                        P.wait("vector", mm, mm_simB)
                        for i in range(4):
                            sv = pool.wide(i % 2)[:, 128 * (i // 2):
                                                  128 * (i // 2) + 128]
                            P.op("vector", "tensor_tensor", sv, sv,
                                 consts["triuB"], op=ALU.add)
                        P.mark("vector", ve)
                        ve_diag = ve.n
                        P.wait("scalar", ve, ve_diag)
                    P.wait("scalar", mm, mm_simB)
                    if mm_av_prev is not None:
                        P.wait("scalar", mm, mm_av_prev)
                    P.op("scalar", "activation", probsT[:, 0:256],
                         pool.wide(0)[:, 0:256], AF.Exp)
                    P.op("scalar", "activation", probsT[:, 256:512],
                         pool.wide(1)[:, 0:256], AF.Exp, inc=[(se, 1)])
                    se_exp_prev = se.n
                    P.wait("tensor", se, se_exp_prev)
                    for i, h in enumerate(heads):
                        ov = pool.view(4 + i)[0:64, 0:128]
                        P.op("tensor", "matmul", ov,
                             vtok[:, slot * 4 * 520 + (j0 // 128) * 520 + 65 * h:
                                  slot * 4 * 520 + (j0 // 128) * 520 + 65 * h + 64],
                             probsT[:, 256 * (i % 2) + 128 * (i // 2):
                                    256 * (i % 2) + 128 * (i // 2) + 128],
                             start=(kt == first_kt), stop=(kt == kts[-1]))
                    P.op("tensor", "matmul", pool.wide(0)[0:1, 512:1024],
                         consts["ones_col"], probsT[:, 0:512],
                         start=(kt == first_kt), stop=(kt == kts[-1]))
                    P.mark("tensor", mm)
                    mm_av_prev = mm.n
                mm_av = mm_av_prev
                P.wait("vector", mm, mm_av)
                P.op("vector", "tensor_scalar", sumsr[:, 512 * hpass:512 * hpass + 512],
                     pool.wide(0)[0:1, 512:1024], 1e-30, None, op0=ALU.add)
                P.selfedge("vector", ve)
                P.op("vector", "reciprocal", recipr[:, 512 * hpass:512 * hpass + 512],
                     sumsr[:, 512 * hpass:512 * hpass + 512], inc=[(ve, 1)],
                     lowprec=True)
                ve_rec = ve.n
                P.wait("tensor", ve, ve_rec)
                for i, h in enumerate(heads):
                    P.op("tensor", "matmul", pool.view(3)[0:64, 0:128],
                         consts["ones_row"][0:1, 0:64],
                         recipr[0:1, 512 * hpass + 256 * (i % 2) + 128 * (i // 2):
                                512 * hpass + 256 * (i % 2) + 128 * (i // 2) + 128],
                         start=True, stop=True, inc=[(mm, 1)])
                    mmv = mm.n
                    P.wait("vector", mm, mmv)
                    P.op("vector", "tensor_copy", bcn[0:64, :],
                         pool.view(3)[0:64, 0:128])
                    P.selfedge("vector", ve)
                    P.op("vector", "tensor_tensor",
                         o_fm[64 * (h % 2):64 * (h % 2) + 64,
                              128 * (h // 2):128 * (h // 2) + 128],
                         pool.view(4 + i)[0:64, 0:128],
                         bcn[0:64, :], op=ALU.mult, inc=[(ve, 1)])
                    P.wait("tensor", ve, ve.n)
                ve_ofm = ve.n
                for i in range(8):
                    pool.release(i, ve, ve_ofm)
            for oc in range(KCH):
                ib, bank = pool.grab()
                for ki in range(4):
                    P.op("tensor", "matmul", bank[0:128, 0:128],
                         wts["wo"](ki, oc * 128, 128),
                         o_fm[:, 128 * ki:128 * ki + 128],
                         start=(ki == 0), stop=(ki == 3),
                         inc=([(mm, 1)] if ki == 3 else []))
                mmv = mm.n
                P.wait("vector", mm, mmv)
                dst = xa[:, wpar * KCH * 520 + oc * 520 + 8 + 128 * qt:
                         wpar * KCH * 520 + oc * 520 + 8 + 128 * qt + 128]
                P.op("vector", "tensor_tensor", dst, bank[0:128, 0:128],
                     xin(oc, qc, 128), op=ALU.add, inc=[(ve, 1)])
                pool.release(ib, ve, ve.n)
        return ve.n

    def ffn(c0, wi):
        wpar = ((c0 - QLO) // 512) % 2

        def xa_dat(ki, off=0, nw=512):
            return xa[:, wpar * KCH * 520 + ki * 520 + 8 + off:
                      wpar * KCH * 520 + ki * 520 + 8 + off + nw]

        def xt(ki):
            if ki < 4:
                return xa_dat(ki)
            return xa[:, wpar * KCH * 520 + ki * 520 + 7:
                      wpar * KCH * 520 + ki * 520 + 7 + 512]

        def ot(ki):
            return hln[:, ki * 512:(ki + 1) * 512]

        if state["store"] is not None:
            P.wait("vector", *state["store"])
            P.wait("scalar", *state["store"])
        emit_ln_win(P, pool, lnb, lnst, xt, ot, 512, consts, sems,
                    gain=wts["lg"], bias=wts["lb"])
        ve_hln = ve.n
        mm_hc = {}
        mm_w2 = {}
        groups = [(0, 6), (6, 12), (12, 18), (18, 22)]
        for gi, (h0, h1) in enumerate(groups):
            nhc = h1 - h0
            for j in range(nhc):
                hc = h0 + j
                p = hc % 2
                if hc >= 2:
                    P.wait("sync", mm, mm_hc[hc - 2])
                P.op("sync", "dma_start", out=w1st[:, p * 2048:p * 2048 + 1024],
                     in_=wts["w1scr"][hc], inc=[(wst2[p], 16)])
                P.op("sync", "dma_start",
                     out=w1st[:, p * 2048 + 1024:p * 2048 + 2048],
                     in_=wts["w1scr"][22 + hc], inc=[(wst2[p], 16)])
                wv = wst2[p].n
                ia, banka = pool.grab()
                ig, bankg = pool.grab()
                P.wait("tensor", wst2[p], wv)
                if hc == 0:
                    P.wait("tensor", ve, ve_hln)
                for ki in range(KCH):
                    P.op("tensor", "matmul", banka,
                         w1st[:, p * 2048 + ki * 128:p * 2048 + ki * 128 + 128],
                         hln[:, ki * 512:(ki + 1) * 512], start=(ki == 0),
                         stop=(ki == KCH - 1),
                         inc=([(mm, 1)] if ki == KCH - 1 else []))
                for ki in range(KCH):
                    P.op("tensor", "matmul", bankg,
                         w1st[:, p * 2048 + 1024 + ki * 128:
                              p * 2048 + 1024 + ki * 128 + 128],
                         hln[:, ki * 512:(ki + 1) * 512], start=(ki == 0),
                         stop=(ki == KCH - 1),
                         inc=([(mm, 1)] if ki == KCH - 1 else []))
                mm_hc[hc] = mm.n
                P.wait("scalar", mm, mm_hc[hc])
                P.wait("scalar", ve, ve.n)
                P.op("scalar", "activation", gts[:], bankg, AF.Gelu,
                     bias=wts["b1g"](hc), inc=[(se, 1)])
                se_g = se.n
                P.wait("vector", se, se_g)
                P.op("vector", "scalar_tensor_tensor",
                     ffnh[:, j * 512:(j + 1) * 512], banka, wts["b1a"](hc),
                     gts[:], op0=ALU.add, op1=ALU.mult, inc=[(ve, 1)])
                pool.release(ia, ve, ve.n)
                pool.release(ig, ve, ve.n)
            ve_ffnh = ve.n
            for i in range(8):
                pool.gate(i)
            for j in range(nhc):
                hc = h0 + j
                if hc >= 1:
                    P.wait("sync", mm, mm_w2.get(hc - 1, 0))
                P.op("sync", "dma_start", out=w2st[:, 0:1024],
                     in_=wts["w2scr"][hc], inc=[(wst2[2], 16)])
                wv = wst2[2].n
                P.wait("tensor", wst2[2], wv)
                if j == 0:
                    P.wait("tensor", ve, ve_ffnh)
                for oc in range(8):
                    P.op("tensor", "matmul", pool.view(oc),
                         w2st[:, oc * 128:oc * 128 + 128],
                         ffnh[:, j * 512:(j + 1) * 512], start=(j == 0),
                         stop=(j == nhc - 1))
                P.mark("tensor", mm)
                mm_w2[hc] = mm.n
            mm_w2f = mm.n
            P.wait("vector", mm, mm_w2f)
            for oc in range(8):
                if gi == 0:
                    P.op("vector", "scalar_tensor_tensor",
                         xo_stage[:, oc * 512:(oc + 1) * 512], pool.view(oc),
                         wts["b2"](oc), xa_dat(oc), op0=ALU.add, op1=ALU.add,
                         inc=[(ve, 1)])
                else:
                    P.op("vector", "tensor_tensor",
                         xo_stage[:, oc * 512:(oc + 1) * 512], pool.view(oc),
                         xo_stage[:, oc * 512:(oc + 1) * 512], op=ALU.add,
                         inc=[(ve, 1)])
            ve_out = ve.n
            for i in range(8):
                pool.release(i, ve, ve_out)
        state["store"] = out_cb(wi, xo_stage, ve_out)

    if DEBUG_PHASE < 1:
        return
    if xin_load is not None:
        xin_load(KLO)
    project_v(KLO)
    project_qk(KLO, False)
    for wi in range(NW):
        c0 = QLO + 512 * wi
        if xin_load is not None:
            xin_load(c0)
        project_v(c0)
        project_qk(c0, False)
        project_qk(c0, True)
        barrier(P, sems)
        if DEBUG_PHASE < 2:
            continue
        attention(c0)
        P.selfedge("vector", ve)
        wpar = ((c0 - QLO) // 512) % 2
        for ki in range(KCH):
            P.op("vector", "tensor_copy",
                 xa[:, (1 - wpar) * KCH * 520 + ki * 520:
                    (1 - wpar) * KCH * 520 + ki * 520 + 8],
                 xa[:, wpar * KCH * 520 + ki * 520 + 512:
                    wpar * KCH * 520 + ki * 520 + 520], inc=[(ve, 1)])
        P.selfedge("vector", ve)
        barrier(P, sems)
        if DEBUG_PHASE < 3 or wi < OUT0:
            continue
        ffn(c0, wi - OUT0)
        barrier(P, sems)


def build_launch_B(l, fx=None):
    if fx is None:
        nc = bass.Bass("TRN2", target_bir_lowering=False)
        dram = lambda name, shape, dt, kind="Internal": nc.dram_tensor(
            name, shape, dt, kind=kind).ap()
    else:
        nc = fx["nc"]
        dram = lambda name, shape, dt, kind="Internal": fx["io"][name]
    okind = "ExternalOutput" if fx is None else "Internal"
    if l == 0:
        NIN, XOFF = 3600, 1544
        QLO = XOFF - 1024
    else:
        NIN, XOFF = 3072, 1024
        QLO = XOFF - 512
    QHI = XOFF + G
    Tq = QHI - QLO
    Tk = QHI - (QLO - 512)

    if l == 0:
        xh = dram("xh2", [C, NIN], F32, "ExternalInput")
        xout_h = dram("xout0", [C, 3072], BF16, okind)
    else:
        xh = dram("xin0", [C, NIN], BF16, "ExternalInput")
        af0_h = dram("af0", [C, G], F32, okind)
        f0scr = dram("f0scr", [C, G], BF16)
        rwscr = dram("rwscr", [C, G], BF16)
        wkvl_h = dram("wkvl", [C, G], F32, "ExternalInput")
        wkl_h = dram("wkl", [C, G], F32, "ExternalInput")
        rbf_h = dram("rbf", [C, G], BF16, "ExternalInput")
        carrH_h = dram("carrH", [C, 8], F32, "ExternalInput")
        tdP_h = dram("tdP", [128, KCH], F32, "ExternalInput")
        tfP_h = dram("tfP", [128, KCH], F32, "ExternalInput")
        iota_h = dram("iota2d", [128, G], F32, "ExternalInput")
        woTr_h = dram("woTr", [C, C], F32, "ExternalInput")

    wqkv_h = dram("wqkv", [C, 1536], F32, "ExternalInput")
    wo_h = dram("wo", [512, C], F32, "ExternalInput")
    w1lm_h = dram("w1lm", [44, 128, 1024], F32, "ExternalInput")
    w2lm_h = dram("w2lm", [22, 128, 1024], F32, "ExternalInput")
    b1aP_h = dram("b1aP", [128, 22], F32, "ExternalInput")
    b1gP_h = dram("b1gP", [128, 22], F32, "ExternalInput")
    b2P_h = dram("b2P", [128, KCH], F32, "ExternalInput")
    lgP_h = dram("lgP", [128, KCH], F32, "ExternalInput")
    lbP_h = dram("lbP", [128, KCH], F32, "ExternalInput")
    scal4_h = dram("scal4", [128, 4], F32, "ExternalInput")
    cq_h = dram("cq", [128, Tq], F32, "ExternalInput")
    sq_h = dram("sq", [128, Tq], F32, "ExternalInput")
    ck_h = dram("ck", [128, Tk], F32, "ExternalInput")
    sk_h = dram("sk", [128, Tk], F32, "ExternalInput")
    identbf_h = dram("identbf", [128, 128], BF16, "ExternalInput")
    blockones_h = dram("blockones", [128, 128], BF16, "ExternalInput")
    triuA_h = dram("triuA", [128, 128], F32, "ExternalInput")
    triuB_h = dram("triuB", [128, 128], F32, "ExternalInput")
    KML = (XOFF // 128) * 128 + 128
    kmask_h = dram("kmask", [1, KML], BF16, "ExternalInput")
    w1scr = dram("w1scr", [44, 128, 1024], BF16)
    w2scr = dram("w2scr", [22, 128, 1024], BF16)

    with ExitStack() as stack:
        if fx is None:
            P = Prog(nc, stack)
            rr = R(nc, stack)
        else:
            P, rr = fx["P"], fx["rr"]
            rr.push(f"B{l}_")
        ld = P.sem("ld")
        lc = P.sem("lc")
        wl = P.sem("wl")
        pc_in = P.sem("pci")
        pc_out = P.sem("pco")
        mm = P.sem("mm")
        ve = P.sem("ve")
        se = P.sem("se")
        st = P.sem("st")
        tabsem = P.sem("tab")
        wst2 = (P.sem("wsta"), P.sem("wstb"), P.sem("wstc"))
        sems = {"mm": mm, "ve": ve, "se": se, "tab": tabsem, "wst2": wst2}

        if fx is None:
            consts = make_consts(P, rr, ve)
        else:
            consts = dict(fx["consts"])
        pool = PsumPool(P, rr, "pp")
        xo_stage = rr.sb("xo_stage", (128, KCH * 512), BF16)
        lnb = make_ln_bufs(rr, "lnb", x2=xo_stage[:, :])

        cst = rr.sb("cstb", (128, 2 * 128), BF16)
        cstf = rr.sb("cstf", (128, 2 * 128), F32)
        kmask_sb = rr.sb("kmask_sb", (1, KML), BF16)
        P.op("sync", "dma_start", out=cst[:, 0:128], in_=identbf_h, inc=[(lc, 16)])
        P.op("sync", "dma_start", out=cst[:, 128:256], in_=blockones_h, inc=[(lc, 16)])
        P.op("sync", "dma_start", out=cstf[:, 0:128], in_=triuA_h, inc=[(lc, 16)])
        P.op("sync", "dma_start", out=cstf[:, 128:256], in_=triuB_h, inc=[(lc, 16)])
        P.op("sync", "dma_start", out=kmask_sb[:], in_=kmask_h, inc=[(lc, 16)])

        pars = rr.sb("parsB", (128, 22 * 2 + KCH * 3 + 4 + (2 * KCH if l else 0)), F32)
        p_b1a = pars[:, 0:22]
        p_b1g = pars[:, 22:44]
        p_b2 = pars[:, 44:52]
        p_lg = pars[:, 52:60]
        p_lb = pars[:, 60:68]
        p_sc = pars[:, 68:72]
        P.op("sync", "dma_start", out=p_b1a, in_=b1aP_h, inc=[(lc, 16)])
        P.op("sync", "dma_start", out=p_b1g, in_=b1gP_h, inc=[(lc, 16)])
        P.op("sync", "dma_start", out=p_b2, in_=b2P_h, inc=[(lc, 16)])
        P.op("sync", "dma_start", out=p_lg, in_=lgP_h, inc=[(lc, 16)])
        P.op("sync", "dma_start", out=p_lb, in_=lbP_h, inc=[(lc, 16)])
        P.op("sync", "dma_start", out=p_sc, in_=scal4_h, inc=[(lc, 16)])
        if l == 1:
            p_td = pars[:, 72:80]
            p_tf = pars[:, 80:88]
            P.op("sync", "dma_start", out=p_td, in_=tdP_h, inc=[(lc, 16)])
            P.op("sync", "dma_start", out=p_tf, in_=tfP_h, inc=[(lc, 16)])
        lc_all = lc.n
        for e in ("tensor", "vector", "scalar"):
            P.wait(e, lc, lc_all)
        consts["identbf"] = cst[:, 0:128]
        consts["blockones"] = cst[:, 128:256]
        consts["triuA"] = cstf[:, 0:128]
        consts["triuB"] = cstf[:, 128:256]

        wo_sb = rr.sb("wo_sb", (128, 4 * 1024), BF16)
        for ki in range(4):
            P.op("gpsimd", "dma_start", out=wo_sb[:, ki * 1024:(ki + 1) * 1024],
                 in_=wo_h[ki * 128:(ki + 1) * 128, :], inc=[(wl, 16)])
        wl_all = wl.n
        P.wait("tensor", wl, wl_all)

        wqst = rr.sb("wqst", (128, KCH * 512), BF16)
        wq_state = {"blk": -1}

        def wq_load(blk):
            if wq_state["blk"] == blk:
                return
            wq_state["blk"] = blk
            P.wait("gpsimd", mm, mm.n)
            P.wait("gpsimd", wl, wl.n)
            for ki in range(KCH):
                P.op("gpsimd", "dma_start", out=wqst[:, ki * 512:(ki + 1) * 512],
                     in_=wqkv_h[ki * 128:(ki + 1) * 128,
                                512 * blk:512 * blk + 512], inc=[(wl, 16)])
            for e in ("tensor",):
                P.wait(e, wl, wl.n)

        def wq_sb(ki, m0, mr):
            blk = m0 // 512
            assert (m0 + mr - 1) // 512 == blk
            return wqst[:, ki * 512 + m0 - 512 * blk: ki * 512 + m0 - 512 * blk + mr]

        def wo_sl(ki, m0, mr):
            return wo_sb[:, ki * 1024 + m0: ki * 1024 + m0 + mr]

        pstage = rr.sb("pstage", (128, 1024), BF16)
        npc = 0
        for (src, dst, n) in ((w1lm_h, w1scr, 44), (w2lm_h, w2scr, 22)):
            for i in range(n):
                if npc >= 1:
                    P.wait("gpsimd", pc_out, 16 * npc)
                P.op("gpsimd", "dma_start", out=pstage[:],
                     in_=src[i], inc=[(pc_in, 16)])
                P.wait("scalar", pc_in, pc_in.n)
                P.op("scalar", "dma_start", out=dst[i],
                     in_=pstage[:], inc=[(pc_out, 16)])
                npc += 1
        pc_done = pc_out.n
        P.wait("sync", pc_out, pc_done)

        if l == 0:
            xb = rr.sb("xb2", (128, KCH * NIN), BF16)
            for ki in range(KCH):
                P.op("gpsimd", "dma_start", out=xb[:, ki * NIN:(ki + 1) * NIN],
                     in_=xh[ki * 128:(ki + 1) * 128, :], inc=[(ld, 16)])
            ld_x = ld.n
            emit_ln(P, rr, pool, "ln0", xb, NIN, consts, sems, ready=(ld, ld_x))
            xin = fm_tiles(xb, NIN)
            xin_load = None
        else:
            xinr = rr.sb("xinr", (128, 2 * KCH * 520), BF16)
            P.op("vector", "memset", xinr[:], 0.0, inc=[(ve, 1)])
            xl = P.sem("xl")
            def xin_load(c0):
                w = c0 // 512
                slot = w % 2
                P.wait("sync", ve, ve.n)
                for ki in range(KCH):
                    lo = max(0, c0 - 8)
                    P.op("sync", "dma_start",
                         out=xinr[:, slot * KCH * 520 + ki * 520 + (8 - (c0 - lo)):
                                  slot * KCH * 520 + (ki + 1) * 520],
                         in_=xh[ki * 128:(ki + 1) * 128, lo:c0 + 512],
                         inc=[(xl, 16)])
                P.wait("tensor", xl, xl.n)
                P.wait("vector", xl, xl.n)
                P.wait("scalar", xl, xl.n)

            def xin(ki, c, nw):
                wstar = (c + nw - 1) // 512
                slot = wstar % 2
                off = 8 + c - 512 * wstar
                assert off >= 0, (c, nw)
                return xinr[:, slot * KCH * 520 + ki * 520 + off:
                            slot * KCH * 520 + ki * 520 + off + nw]

        wts = {
            "w1scr": [w1scr[i] for i in range(44)],
            "w2scr": [w2scr[i] for i in range(22)],
            "b1a": lambda hc: p_b1a[:, hc:hc + 1],
            "b1g": lambda hc: p_b1g[:, hc:hc + 1],
            "b2": lambda oc: p_b2[:, oc:oc + 1],
            "lg": lambda ki: p_lg[:, ki:ki + 1],
            "lb": lambda ki: p_lb[:, ki:ki + 1],
            "qs": p_sc[:, 0:1],
            "qss": p_sc[:, 1:2],
            "ks": p_sc[:, 2:3],
            "kss": p_sc[:, 3:4],
            "wo": wo_sl,
        }
        tabs = {"cq": cq_h, "sq": sq_h, "ck": ck_h, "sk": sk_h}

        tgt_h = xout_h if l == 0 else f0scr
        NTG = 3072 if l == 0 else G
        tgt3 = tgt_h.rearrange("(k p) n -> p k n", p=128)

        def out_cb(wi, stage, vev):
            P.wait("gpsimd", ve, vev)
            P.wait("gpsimd", st, st.n)
            P.op("gpsimd", "dma_start",
                 out=tgt3[:, :, 512 * wi:512 * wi + 512],
                 in_=stage[:].rearrange("p (k n) -> p k n", n=512),
                 inc=[(st, 16)])
            return (st, st.n)

        emit_ffn_inner(P, rr, pool, "il", xin=xin, QLO=QLO, QHI=QHI,
                       wq_sb=wq_sb, tabs=tabs, wts=wts, consts=consts,
                       sems=sems, out_cb=out_cb, kmask_bf=kmask_sb[:],
                       xoff=XOFF, lnb=lnb, wq_load=wq_load,
                       xo_stage=xo_stage[:, :], OUT0=(0 if l == 0 else 1),
                       xin_load=(None if l == 0 else xin_load))

        if l == 1:
            emit_mixer_post(P, rr, pool, sems, consts,
                            wkvl_h=wkvl_h, wkl_h=wkl_h, rbf_h=rbf_h,
                            carrH_h=carrH_h, p_td=p_td, p_tf=p_tf,
                            iota_h=iota_h, woTr_h=woTr_h, f0scr=f0scr,
                            rwscr=rwscr, af0_h=af0_h, st=st)

        P.wait("gpsimd", st, st.n)
        assert rr.sb_bytes <= 222 * 1024 * 128, rr.sb_bytes
        if fx is None:
            P.build()
        else:
            quiesce(P, sems)
            rr.pop()
    return nc


def emit_mixer_post(P, rr, pool, sems, consts, *, wkvl_h, wkl_h, rbf_h, carrH_h,
                    p_td, p_tf, iota_h, woTr_h, f0scr, rwscr, af0_h, st):
    mm, ve, se = sems["mm"], sems["ve"], sems["se"]
    barrier(P, sems)
    lm = P.sem("lm")
    lg2 = P.sem("lg2")
    GH = G // 2
    wkv_t = rr.sb("mp_wkv", (128, GH), F32)
    wk_t = rr.sb("mp_wk", (128, GH), F32)
    fac_t = rr.sb("mp_fac", (128, GH), F32)
    rec_t = rr.sb("mp_rec", (128, GH), F32)
    sg_t = rr.sb("mp_sg", (128, GH), BF16)
    rb_t = rr.sb("mp_rb", (128, GH), BF16)
    rw_t = rr.sb("mp_rw", (128, GH), BF16)
    iota_t = rr.sb("mp_iota", (128, GH), F32)
    carr_t = rr.sb("mp_carr", (128, 8 * KCH), F32)
    chp = rr.sb("mp_chp", (128, 4 * KCH), F32)
    pw = rr.sb("mp_pw", (128, 4), F32)
    woT_sb = rr.sb("mp_woT", (128, KCH * 1024), BF16)
    for ki in range(KCH):
        P.op("gpsimd", "dma_start", out=woT_sb[:, ki * 1024:(ki + 1) * 1024],
             in_=woTr_h[ki * 128:(ki + 1) * 128, :], inc=[(lg2, 16)])
    lmc = P.sem("lmc")
    for ci in range(KCH):
        P.op("sync", "dma_start", out=carr_t[:, 8 * ci:8 * ci + 8],
             in_=carrH_h[128 * ci:128 * ci + 128, :], inc=[(lmc, 16)])
    lm_all = lmc.n
    lg_all = lg2.n
    st_prev = None
    for ci in range(KCH):
        cs = 128 * ci
        P.wait("scalar", lmc, lm_all)
        P.op("scalar", "activation", chp[:, 4 * ci:4 * ci + 1],
             p_td[:, ci:ci + 1], AF.Exp, inc=[(se, 1)])
        P.selfedge("scalar", se)
        for mI in range(1, 4):
            P.op("scalar", "activation", pw[:, mI:mI + 1],
                 chp[:, 4 * ci:4 * ci + 1], AF.Exp, scale=-float(G * mI),
                 inc=[(se, 1)])
        P.op("scalar", "activation", pw[:, 0:1],
             chp[:, 4 * ci:4 * ci + 1], AF.Copy, scale=-1.0, inc=[(se, 1)])
        se_pw = se.n
        P.wait("vector", se, se_pw)
        P.wait("vector", lmc, lm_all)
        for sc in range(2):
            dst = chp[:, 4 * ci + 1 + sc:4 * ci + 2 + sc]
            P.op("vector", "tensor_copy", dst,
                 carr_t[:, 8 * ci + sc:8 * ci + sc + 1])
            for mI in range(1, 4):
                P.selfedge("vector", ve)
                P.op("vector", "scalar_tensor_tensor", dst,
                     carr_t[:, 8 * ci + 2 * mI + sc:8 * ci + 2 * mI + sc + 1],
                     pw[:, mI:mI + 1], dst, op0=ALU.mult, op1=ALU.add)
            P.selfedge("vector", ve)
        ve_cin = ve.n
        for hf in range(2):
            h0 = hf * GH
            if st_prev is not None:
                P.wait("sync", ve, st_prev)
            P.op("sync", "dma_start", out=wkv_t[:],
                 in_=wkvl_h[cs:cs + 128, h0:h0 + GH], inc=[(lm, 16)])
            P.op("sync", "dma_start", out=wk_t[:],
                 in_=wkl_h[cs:cs + 128, h0:h0 + GH], inc=[(lm, 16)])
            P.op("sync", "dma_start", out=rb_t[:],
                 in_=rbf_h[cs:cs + 128, h0:h0 + GH], inc=[(lm, 16)])
            P.op("sync", "dma_start", out=iota_t[:],
                 in_=iota_h[:, h0:h0 + GH], inc=[(lm, 16)])
            lm_c = lm.n
            P.wait("scalar", ve, ve_cin)
            P.wait("scalar", lm, lm_c)
            if st_prev is not None:
                P.wait("scalar", ve, st_prev)
            P.op("scalar", "activation", fac_t[:], iota_t[:], AF.Exp,
                 scale=pw[:, 0:1], inc=[(se, 1)])
            P.op("scalar", "activation", sg_t[:], rb_t[:], AF.Sigmoid,
                 inc=[(se, 1)])
            se_fac = se.n
            P.wait("vector", se, se_fac)
            P.wait("vector", lm, lm_c)
            P.op("vector", "scalar_tensor_tensor", wkv_t[:], fac_t[:],
                 chp[:, 4 * ci + 1:4 * ci + 2], wkv_t[:], op0=ALU.mult,
                 op1=ALU.add)
            P.op("vector", "scalar_tensor_tensor", wk_t[:], fac_t[:],
                 chp[:, 4 * ci + 2:4 * ci + 3], wk_t[:], op0=ALU.mult,
                 op1=ALU.add)
            P.selfedge("vector", ve)
            P.op("vector", "reciprocal", rec_t[:], wk_t[:])
            P.selfedge("vector", ve)
            P.op("vector", "tensor_tensor", wkv_t[:], wkv_t[:], rec_t[:],
                 op=ALU.mult)
            P.selfedge("vector", ve)
            P.wait("vector", st, st.n)
            P.op("vector", "tensor_tensor", rw_t[:], wkv_t[:], sg_t[:],
                 op=ALU.mult, inc=[(ve, 1)])
            ve_rw = ve.n
            st_prev = ve_rw
            P.wait("gpsimd", ve, ve_rw)
            P.op("gpsimd", "dma_start", out=rwscr[cs:cs + 128, h0:h0 + GH],
                 in_=rw_t[:], inc=[(st, 16)])
    st_rw = st.n
    rwst = rr.sb("mp_rwst", (128, KCH * 512), BF16)
    f0st = rr.sb("mp_f0st", (128, 2 * 512), BF16)
    afst = rr.sb("mp_afst", (128, 2 * 512), F32)
    lrw = P.sem("lrw")
    lrf2 = (P.sem("lrf0"), P.sem("lrf1"))
    P.wait("sync", st, st_rw)
    P.wait("tensor", lg2, lg_all)
    af_done = None
    for win in range(G // 512):
        P.wait("sync", mm, mm.n)
        for ki in range(KCH):
            P.op("sync", "dma_start", out=rwst[:, ki * 512:(ki + 1) * 512],
                 in_=rwscr[ki * 128:(ki + 1) * 128, 512 * win:512 * win + 512],
                 inc=[(lrw, 16)])
        lr_w = lrw.n
        P.wait("tensor", lrw, lr_w)
        for oc in range(KCH):
            pp = (win * KCH + oc) % 2
            if af_done is not None and (win * KCH + oc) >= 2:
                P.wait("sync", st, af_done.get((win * KCH + oc) - 2, 0))
            P.op("sync", "dma_start", out=f0st[:, pp * 512:(pp + 1) * 512],
                 in_=f0scr[oc * 128:(oc + 1) * 128, 512 * win:512 * win + 512],
                 inc=[(lrf2[pp], 16)])
            lr_c = lrf2[pp].n
            ib, bank = pool.grab()
            for ki in range(KCH):
                P.op("tensor", "matmul", bank,
                     woT_sb[:, ki * 1024 + oc * 128: ki * 1024 + oc * 128 + 128],
                     rwst[:, ki * 512:(ki + 1) * 512], start=(ki == 0),
                     stop=(ki == KCH - 1),
                     inc=([(mm, 1)] if ki == KCH - 1 else []))
            mmv = mm.n
            P.wait("vector", mm, mmv)
            P.wait("vector", lrf2[pp], lr_c)
            P.op("vector", "tensor_tensor", afst[:, pp * 512:(pp + 1) * 512],
                 bank, f0st[:, pp * 512:(pp + 1) * 512], op=ALU.add,
                 inc=[(ve, 1)])
            pool.release(ib, ve, ve.n)
            P.wait("gpsimd", ve, ve.n)
            P.wait("gpsimd", st, st.n)
            P.op("gpsimd", "dma_start",
                 out=af0_h[oc * 128:(oc + 1) * 128, 512 * win:512 * win + 512],
                 in_=afst[:, pp * 512:(pp + 1) * 512], inc=[(st, 16)])
            if af_done is None:
                af_done = {}
            af_done[win * KCH + oc] = st.n


DEBUG_A = False
DEBUG_NW = None
DEBUG_PHASE = 3
DEBUG_PROJ = 3


def build_launch_A(fx=None):
    NA = G + 8
    if fx is None:
        nc = bass.Bass("TRN2", target_bir_lowering=False)
        dram = lambda name, shape, dt, kind: nc.dram_tensor(
            name, shape, dt, kind=kind).ap()
    else:
        nc = fx["nc"]
        dram = lambda name, shape, dt, kind: fx["io"][name]
    xh = dram("xh", [C, NA], F32, "ExternalInput")
    wkT = dram("wkT", [C, C], F32, "ExternalInput")
    wvT = dram("wvT", [C, C], F32, "ExternalInput")
    wrT = dram("wrT", [C, C], F32, "ExternalInput")
    tmP = dram("tmP", [128, KCH], F32, "ExternalInput")
    tdP = dram("tdP", [128, KCH], F32, "ExternalInput")
    tfP = dram("tfP", [128, KCH], F32, "ExternalInput")
    if DEBUG_A:
        xn_dbg = dram("xn_dbg", [C, NA], BF16, "ExternalOutput")
        xm_dbg = dram("xm_dbg", [C, G], BF16, "ExternalOutput")
        k_dbg = dram("k_dbg", [128, G], F32, "ExternalOutput")
        kv_dbg = dram("kv_dbg", [128, G], F32, "ExternalOutput")
    okind = "ExternalOutput" if fx is None else "Internal"
    wkvl_h = dram("wkvl", [C, G], F32, okind)
    wkl_h = dram("wkl", [C, G], F32, okind)
    rbf_h = dram("rbf", [C, G], BF16, okind)
    carr_h = dram("carr", [C, 2], F32, okind)

    with ExitStack() as stack:
        if fx is None:
            P = Prog(nc, stack)
            rr = R(nc, stack)
        else:
            P, rr = fx["P"], fx["rr"]
            rr.push("A_")
        ld = P.sem("ld")
        wl = P.sem("wl")
        mm = P.sem("mm")
        ve = P.sem("ve")
        se = P.sem("se")
        st = P.sem("st")
        sems = {"mm": mm, "ve": ve, "se": se}

        if fx is None:
            consts = make_consts(P, rr, ve)
        else:
            consts = dict(fx["consts"])
        pr = PsumRing(P, rr, "pr", 6)

        xb = rr.sb("xb", (128, KCH * NA), BF16)
        for ki in range(KCH):
            P.op("gpsimd", "dma_start", out=xb[:, ki * NA:(ki + 1) * NA],
                 in_=xh[ki * 128:(ki + 1) * 128, :], inc=[(ld, 16)])
        ld_x = ld.n
        lp = P.sem("lp")
        pars = rr.sb("pars", (128, 3 * KCH), F32)
        P.op("sync", "dma_start", out=pars[:, 0:KCH], in_=tmP, inc=[(lp, 16)])
        P.op("sync", "dma_start", out=pars[:, KCH:2 * KCH], in_=tdP, inc=[(lp, 16)])
        P.op("sync", "dma_start", out=pars[:, 2 * KCH:3 * KCH], in_=tfP, inc=[(lp, 16)])
        ld_pars = lp.n
        wk_w, _ = load_weight_fm(P, rr, "wk", wkT, C, C, wl)
        wv_w, _ = load_weight_fm(P, rr, "wv", wvT, C, C, wl)
        wr_w, _ = load_weight_fm(P, rr, "wr", wrT, C, C, wl)
        wl_all = wl.n

        emit_ln(P, rr, pr, "ln0", xb, NA, consts, sems, ready=(ld, ld_x))
        ve_ln = ve.n

        xm = rr.sb("xm", (128, KCH * G), BF16)
        dsc = rr.sb("dsc", (128, G), BF16)
        P.wait("vector", lp, ld_pars)
        P.wait("vector", ve, ve_ln)
        for ki in range(KCH):
            xnk = xb[:, ki * NA: (ki + 1) * NA]
            if ki:
                P.wait("vector", ve, ve.n)
            P.op("vector", "tensor_tensor", dsc[:], xnk[:, 1:G + 1], xnk[:, 0:G],
                 op=ALU.subtract)
            P.selfedge("vector", ve)
            P.op("vector", "scalar_tensor_tensor", xm[:, ki * G:(ki + 1) * G],
                 dsc[:], pars[:, ki:ki + 1], xnk[:, 0:G], op0=ALU.mult, op1=ALU.add,
                 inc=[(ve, 1)])
        ve_xm = ve.n
        xmt = fm_tiles(xm, G)

        k_sb = rr.sb("k_sb", (128, G), F32)
        kv_sb = rr.sb("kv_sb", (128, G), F32)
        s1 = rr.sb("s1", (128, G + 1), F32)
        s2 = rr.sb("s2", (128, G + 1), F32)
        wkv_o = rr.sb("wkv_o", (128, G), F32)
        wk_o = rr.sb("wk_o", (128, G), F32)
        r_o = rr.sb("r_o", (128, G), BF16)
        chp = rr.sb("chp", (128, 3 * KCH), F32)

        for ci in range(KCH):
            col = lambda j, _c=ci: chp[:, 3 * _c + j:3 * _c + j + 1]
            P.wait("scalar", lp, ld_pars)
            P.op("scalar", "activation", col(0), pars[:, KCH + ci:KCH + ci + 1], AF.Exp)
            P.op("scalar", "activation", col(2), pars[:, 2 * KCH + ci:2 * KCH + ci + 1],
                 AF.Exp, inc=[(se, 1)])
            se_p1 = se.n
            P.wait("scalar", se, se_p1)
            P.op("scalar", "activation", col(1), col(0), AF.Exp, scale=-1.0,
                 inc=[(se, 1)])
            se_par = se.n

            def evac_k(m0, mr, n0, nw, ps, mmv):
                P.wait("vector", mm, mmv)
                P.op("vector", "tensor_scalar", k_sb[:, n0:n0 + nw], ps, 60.0, None,
                     op0=ALU.min, inc=[(ve, 1)])
                return (ve, ve.n)

            mm_proj(P, pr, wt=wk_w, xt=xmt, kch=KCH, Mtot=C, N=G, mm_sem=mm,
                    evac=evac_k, mlist=[ci * 128],
                    wt_ready=(wl, wl_all), xt_ready=(ve, ve_xm))
            ve_kmin = ve.n
            P.wait("scalar", ve, ve_kmin)
            P.op("scalar", "activation", k_sb[:], k_sb[:], AF.Exp, inc=[(se, 1)])
            se_k = se.n

            def evac_v(m0, mr, n0, nw, ps, mmv):
                P.wait("vector", mm, mmv)
                P.wait("vector", se, se_k)
                P.op("vector", "tensor_tensor", kv_sb[:, n0:n0 + nw], ps,
                     k_sb[:, n0:n0 + nw], op=ALU.mult, inc=[(ve, 1)])
                return (ve, ve.n)

            mm_proj(P, pr, wt=wv_w, xt=xmt, kch=KCH, Mtot=C, N=G, mm_sem=mm,
                    evac=evac_v, mlist=[ci * 128])
            ve_kv = ve.n

            def evac_r(m0, mr, n0, nw, ps, mmv):
                P.wait("scalar", mm, mmv)
                P.op("scalar", "activation", r_o[:, n0:n0 + nw], ps, AF.Copy,
                     inc=[(se, 1)])
                return (se, se.n)

            mm_proj(P, pr, wt=wr_w, xt=xmt, kch=KCH, Mtot=C, N=G, mm_sem=mm,
                    evac=evac_r, mlist=[ci * 128])
            se_r = se.n

            P.op("vector", "memset", s1[:, 0:1], 0.0)
            P.op("vector", "memset", s2[:, 0:1], 0.0)
            P.wait("vector", se, se_par)
            P.wait("vector", ve, ve_kv)
            P.op("vector", "tensor_tensor_scan", s1[:, 1:G + 1],
                 col(1).broadcast_to((128, G)), kv_sb[:], 0.0,
                 op0=ALU.mult, op1=ALU.add, inc=[(ve, 1)])
            P.op("vector", "tensor_tensor_scan", s2[:, 1:G + 1],
                 col(1).broadcast_to((128, G)), k_sb[:], 0.0,
                 op0=ALU.mult, op1=ALU.add, inc=[(ve, 1)])
            ve_scan = ve.n
            P.wait("vector", ve, ve_scan)
            P.op("vector", "scalar_tensor_tensor", wkv_o[:], kv_sb[:], col(2),
                 s1[:, 0:G], op0=ALU.mult, op1=ALU.add, inc=[(ve, 1)])
            P.op("vector", "scalar_tensor_tensor", wk_o[:], k_sb[:], col(2),
                 s2[:, 0:G], op0=ALU.mult, op1=ALU.add)
            P.selfedge("vector", ve)
            P.op("vector", "tensor_scalar", wk_o[:], wk_o[:], 1e-16, None,
                 op0=ALU.add, inc=[(ve, 1)])
            ve_out = ve.n

            P.wait("gpsimd", ve, ve_out)
            P.wait("gpsimd", se, se_r)
            cs = ci * 128
            if DEBUG_A and ci == 0:
                P.op("gpsimd", "dma_start", out=k_dbg, in_=k_sb[:], inc=[(st, 16)])
                P.op("gpsimd", "dma_start", out=kv_dbg, in_=kv_sb[:], inc=[(st, 16)])
                for kj in range(KCH):
                    P.op("gpsimd", "dma_start",
                         out=xn_dbg[kj * 128:(kj + 1) * 128, :],
                         in_=xb[:, kj * NA:(kj + 1) * NA], inc=[(st, 16)])
                    P.op("gpsimd", "dma_start",
                         out=xm_dbg[kj * 128:(kj + 1) * 128, :],
                         in_=xm[:, kj * G:(kj + 1) * G], inc=[(st, 16)])
            P.op("gpsimd", "dma_start", out=wkvl_h[cs:cs + 128, :], in_=wkv_o[:],
                 inc=[(st, 16)])
            P.op("gpsimd", "dma_start", out=wkl_h[cs:cs + 128, :], in_=wk_o[:],
                 inc=[(st, 16)])
            P.op("gpsimd", "dma_start", out=rbf_h[cs:cs + 128, :], in_=r_o[:],
                 inc=[(st, 16)])
            P.op("gpsimd", "dma_start", out=carr_h[cs:cs + 128, 0:1],
                 in_=s1[:, G:G + 1], inc=[(st, 16)],
                 allow_slow_non_contiguous=True)
            P.op("gpsimd", "dma_start", out=carr_h[cs:cs + 128, 1:2],
                 in_=s2[:, G:G + 1], inc=[(st, 16)],
                 allow_slow_non_contiguous=True)
            P.wait("vector", st, st.n)
            P.wait("scalar", st, st.n)

        P.wait("gpsimd", st, st.n)
        if fx is None:
            P.build()
        else:
            quiesce(P, sems)
            rr.pop()
    return nc


def build_launch_D(fx=None):
    TT_ = T
    QLO = TT_ - G
    if fx is None:
        nc = bass.Bass("TRN2", target_bir_lowering=False)
        dram = lambda name, shape, dt, kind="Internal": nc.dram_tensor(
            name, shape, dt, kind=kind).ap()
    else:
        nc = fx["nc"]
        dram = lambda name, shape, dt, kind="Internal": fx["io"][name]
    xT_h = dram("xTf", [C, TT_], F32, "ExternalInput")
    afT_h = dram("afT", [C, TT_], F32, "ExternalInput")
    wq_h = [dram(f"wq{v}", [2048, 512], F32, "ExternalInput")
            for v in range(4)]
    wk_h = [dram(f"wk{v}", [1024, 512], F32, "ExternalInput")
            for v in range(4)]
    wvT_h = dram("wvT1", [C, C], F32, "ExternalInput")
    bv_h = dram("bv1", [1, C], F32, "ExternalInput")
    woT_h = dram("woT1", [C, C], F32, "ExternalInput")
    boP_h = dram("boP1", [128, KCH], F32, "ExternalInput")
    sel128_h = dram("sel128", [128, 128], BF16, "ExternalInput")
    wflm_h = dram("wflm", [64, 128, 1024], F32, "ExternalInput")
    bf_h = dram("bf1", [8, C], F32, "ExternalInput")
    sigsT_h = dram("sigsT", [32, 8], F32, "ExternalInput")
    peT_h = dram("peT", [32, G], F32, "ExternalInput")
    identf_h = dram("identf", [128, 128], F32, "ExternalInput")
    ident8_h = dram("ident8", [8, 8], BF16, "ExternalInput")
    triuD_h = dram("triuD", [128, 128], F32, "ExternalInput")
    kmask_h = dram("kmaskD", [1, TT_], BF16, "ExternalInput")
    ones1r_h = dram("ones1r", [1, 4 * G], BF16, "ExternalInput")
    out_h = dram("outT", [C, G], F32, "ExternalOutput")
    xn1scr = dram("xn1scr", [C, TT_], BF16)
    jk4scr = dram("jk4scr", [4, 128, TT_], BF16)
    vscr = dram("vscrD", [TT_, 1040], BF16)
    w8scr = dram("w8scr", [64, 128, 1024], BF16)

    with ExitStack() as stack:
        if fx is None:
            P = Prog(nc, stack)
            rr = R(nc, stack)
        else:
            P, rr = fx["P"], fx["rr"]
            rr.push("D_")
        mm = P.sem("mm")
        ve = P.sem("ve")
        se = P.sem("se")
        st = P.sem("st")
        lx = P.sem("lx")
        la = P.sem("la")
        lc = P.sem("lc")
        wl = P.sem("wl")
        ws2 = (P.sem("ws0"), P.sem("ws1"))
        kv2 = (P.sem("kv0"), P.sem("kv1"))
        xr2 = (P.sem("xr0"), P.sem("xr1"))
        pc_in = P.sem("pci")
        pc_out = P.sem("pco")
        sems = {"mm": mm, "ve": ve, "se": se}
        if fx is None:
            consts = make_consts(P, rr, ve)
        else:
            consts = dict(fx["consts"])
        pool = PsumPool(P, rr, "pp")
        lnb = make_ln_bufs(rr, "lnd")
        lnst = LNState()

        xw = rr.sb("xw", (128, KCH * 512), F32)
        xaf = rr.sb("xaf", (128, KCH * 512), F32)
        xnw = rr.sb("xnw", (128, KCH * 520), BF16)
        wstage = rr.sb("wstage", (128, 16 * 512), BF16)
        lsq = rr.sb("lsq", (128, 512), BF16)
        linb = rr.sb("linb", (128, 512), BF16)
        rstdb = rr.sb("rstdD", (128, 512), BF16)
        vw = rr.sb("vw", (128, 4 * 1040), BF16)
        lnf = xaf[:, 512:1024]
        q_pl = rr.sb("q_pl", (128, 4 * G), BF16)
        jkst = rr.sb("jkst", (128, 2 * 512), BF16)
        vst = rr.sb("vst", (128, 2 * 4 * 1040), BF16)
        probsT = rr.sb("probsTD", (128, 4 * 512), BF16)
        o_fm = rr.sb("o_fmD", (128, 16 * 1024), BF16)
        kvar = o_fm
        xbw = o_fm[:, 8192:8192 + KCH * 512]
        acc = rr.sb("accD", (128, KCH * G), BF16)
        wvo = acc
        sumsr = rr.sb("sumsD", (1, 512), F32)
        recipr = rr.sb("recipD", (1, 512), BF16)
        bcn = rr.sb("bcnD", (128, 512), BF16)
        peT = rr.sb("peTs", (32, 128), F32)
        signsT = rr.sb("signsT", (32, 8), F32)
        onehot = rr.sb("onehot", (128, 8), F32)
        onehotT = rr.sb("onehotT", (8, G), BF16)
        ohrow = rr.sb("ohrow", (1, 512), BF16)
        obc = rr.sb("obc", (128, 512), BF16)
        cstD = rr.sb("cstD", (128, 2 * 128 + 8), BF16)
        cstDf = rr.sb("cstDf", (128, 2 * 128), F32)
        b8 = rr.sb("b8", (8, C), BF16)
        sigs = rr.sb("sigs", (32, 8), F32)
        mwin = rr.sb("mwin", (128, 8), F32)

        P.op("sync", "dma_start", out=cstD[:, 0:128], in_=sel128_h, inc=[(lc, 16)])
        P.op("sync", "dma_start", out=cstD[0:8, 128:136], in_=ident8_h, inc=[(lc, 16)])
        P.op("sync", "dma_start", out=cstDf[:, 0:128], in_=identf_h, inc=[(lc, 16)])
        P.op("sync", "dma_start", out=cstDf[:, 128:256], in_=triuD_h, inc=[(lc, 16)])


        P.op("sync", "dma_start", out=sigs[:], in_=sigsT_h, inc=[(lc, 16)])
        P.op("gpsimd", "dma_start", out=b8[:], in_=bf_h, inc=[(wl, 16)])
        boP = rr.sb("boPD", (128, KCH), F32)
        bvr = rr.sb("bvrD", (1, C), BF16)
        P.op("sync", "dma_start", out=boP[:], in_=boP_h, inc=[(lc, 16)])
        P.op("gpsimd", "dma_start", out=bvr[:], in_=bv_h, inc=[(wl, 16)])
        for ki in range(KCH):
            P.op("gpsimd", "dma_start", out=wvo[:, ki * 1024:(ki + 1) * 1024],
                 in_=wvT_h[ki * 128:(ki + 1) * 128, :], inc=[(wl, 16)])
        lc_all = lc.n
        wl_all = wl.n
        for e in ("tensor", "vector", "scalar"):
            P.wait(e, lc, lc_all)
            P.wait(e, wl, wl_all)

        pst = wstage[:, 0:1024]
        for i in range(64):
            if i >= 1:
                P.wait("gpsimd", pc_out, 16 * i)
            P.op("gpsimd", "dma_start", out=pst, in_=wflm_h[i], inc=[(pc_in, 16)])
            P.wait("scalar", pc_in, pc_in.n)
            P.op("scalar", "dma_start", out=w8scr[i], in_=pst, inc=[(pc_out, 16)])
        pc_done = pc_out.n
        P.wait("sync", pc_out, pc_done)

        P.op("vector", "memset", vw[:], 1.0)
        P.op("vector", "memset", xnw[:], 0.0)
        P.op("vector", "memset", vst[:], 1.0, inc=[(ve, 1)])
        P.selfedge("vector", ve)

        def stream_wvar(src_h, nk, parity):
            wsx = ws2[parity]
            P.wait("gpsimd", mm, mm.n)
            P.wait("gpsimd", wsx, wsx.n)
            P.op("gpsimd", "dma_start",
                 out=wstage[:, 0:nk * 512].rearrange("p (k n) -> p k n", n=512),
                 in_=src_h.rearrange("(k p) n -> p k n", p=128),
                 inc=[(wsx, 16)])
            P.wait("tensor", wsx, wsx.n)

        def plucker(dst, dstoff, w_hs, nk, xt_fn, nw, tabcols):
            for v in range(4):
                stream_wvar(w_hs[v], nk, v % 2)
                for m in range(4):
                    ib, bank = pool.grab()
                    for ki in range(nk):
                        P.op("tensor", "matmul", bank[0:128, 0:nw],
                             wstage[:, ki * 512 + m * 128: ki * 512 + m * 128 + 128],
                             xt_fn(ki), start=(ki == 0), stop=(ki == nk - 1),
                             inc=([(mm, 1)] if ki == nk - 1 else []))
                    mmv = mm.n
                    P.wait("vector", mm, mmv)
                    P.op("vector", "tensor_copy",
                         kvar[:, (v * 4 + m) * 512:(v * 4 + m) * 512 + nw],
                         bank[0:128, 0:nw], inc=[(ve, 1)])
                    pool.release(ib, ve, ve.n)
            P.selfedge("vector", ve)
            for m in range(4):
                t1i = kvar[:, (0 * 4 + m) * 512:(0 * 4 + m) * 512 + nw]
                t1j = kvar[:, (1 * 4 + m) * 512:(1 * 4 + m) * 512 + nw]
                t2i = kvar[:, (2 * 4 + m) * 512:(2 * 4 + m) * 512 + nw]
                t2j = kvar[:, (3 * 4 + m) * 512:(3 * 4 + m) * 512 + nw]
                P.op("vector", "tensor_tensor", lnf[:, 0:nw], t1i, t2j, op=ALU.mult)
                P.op("vector", "tensor_tensor", xaf[:, 0:nw], t1j, t2i,
                     op=ALU.mult)
                P.selfedge("vector", ve)
                P.op("vector", "tensor_tensor", lnf[:, 0:nw], lnf[:, 0:nw],
                     xaf[:, 0:nw], op=ALU.subtract)
                P.selfedge("vector", ve)
                P.op("vector", "tensor_copy", linb[:, 0:nw], lnf[:, 0:nw],
                     inc=[(ve, 1)])
                ve_lin = ve.n
                P.wait("scalar", ve, ve_lin)
                P.op("scalar", "activation", lsq[:, 0:nw], lnf[:, 0:nw], AF.Square,
                     inc=[(se, 1)])
                se_sq = se.n
                ib, bank = pool.grab()
                P.wait("tensor", se, se_sq)
                P.op("tensor", "matmul", bank[0:128, 0:nw], cstD[:, 0:128],
                     lsq[:, 0:nw], start=True, stop=True, inc=[(mm, 1)])
                mm_n2 = mm.n
                P.wait("scalar", mm, mm_n2)
                P.wait("scalar", ve, ve.n)
                P.op("scalar", "activation", rstdb[:, 0:nw], bank[0:128, 0:nw],
                     AF.Ln, bias=consts["cvals"][:, 1:2])
                P.selfedge("scalar", se)
                P.op("scalar", "activation", rstdb[:, 0:nw], rstdb[:, 0:nw],
                     AF.Exp, scale=-0.5, inc=[(se, 1)])
                se_r = se.n
                pool.release(ib, se, se_r)
                P.wait("vector", se, se_r)
                P.op("vector", "tensor_tensor", dst[:, dstoff(m):dstoff(m) + nw],
                     linb[:, 0:nw], rstdb[:, 0:nw], op=ALU.mult, inc=[(ve, 1)])
            P.selfedge("vector", ve)

        st_xn = None
        for win in range(TT_ // 512):
            c0 = 512 * win
            P.wait("sync", ve, ve.n)
            P.wait("gpsimd", ve, ve.n)
            for ki in range(KCH):
                P.op("sync", "dma_start", out=xw[:, ki * 512:(ki + 1) * 512],
                     in_=xT_h[ki * 128:(ki + 1) * 128, c0:c0 + 512],
                     inc=[(lx, 16)])
                P.op("gpsimd", "dma_start", out=xaf[:, ki * 512:(ki + 1) * 512],
                     in_=afT_h[ki * 128:(ki + 1) * 128, c0:c0 + 512],
                     inc=[(la, 16)])
            lx_w = lx.n
            la_w = la.n
            P.wait("vector", lx, lx_w)
            P.wait("vector", la, la_w)
            for ki in range(KCH):
                P.op("vector", "tensor_tensor", xw[:, ki * 512:(ki + 1) * 512],
                     xw[:, ki * 512:(ki + 1) * 512],
                     xaf[:, ki * 512:(ki + 1) * 512], op=ALU.add, inc=[(ve, 1)])
            P.selfedge("vector", ve)
            for ki in range(KCH):
                P.op("vector", "tensor_copy", xbw[:, ki * 512:(ki + 1) * 512],
                     xw[:, ki * 512:(ki + 1) * 512], inc=[(ve, 1)])
            P.selfedge("vector", ve)
            ve_x1 = ve.n
            P.wait("vector", mm, mm.n)
            emit_ln_win(P, pool, lnb, lnst,
                        lambda ki: xbw[:, ki * 512:(ki + 1) * 512],
                        lambda ki: xnw[:, ki * 520 + 8:(ki + 1) * 520], 512,
                        consts, sems, ready=(ve, ve_x1))
            ve_ln = ve.n
            P.wait("gpsimd", ve, ve_ln)
            for ki in range(KCH):
                P.op("gpsimd", "dma_start",
                     out=xn1scr[ki * 128:(ki + 1) * 128, c0:c0 + 512],
                     in_=xnw[:, ki * 520 + 8:(ki + 1) * 520], inc=[(st, 16)])
            st_xn = st.n
            P.wait("tensor", ve, ve_ln)
            for tt in range(4):
                for half in range(2):
                    ib, bank = pool.grab()
                    for ki in range(KCH):
                        P.op("tensor", "matmul", bank,
                             xnw[:, ki * 520 + 8 + 128 * tt:
                                 ki * 520 + 8 + 128 * tt + 128],
                             wvo[:, ki * 1024 + half * 512:
                                 ki * 1024 + half * 512 + 512],
                             start=(ki == 0), stop=False)
                    P.op("tensor", "matmul", bank, consts["ones_row"][0:1, 0:128],
                         bvr[0:1, half * 512:half * 512 + 512], start=False,
                         stop=True, inc=[(mm, 1)])
                    mmv = mm.n
                    dstv = vw[:, tt * 1040:tt * 1040 + 1040].rearrange(
                        "p (h d) -> p h d", d=65)[:, 8 * half:8 * half + 8, 0:64]
                    P.wait("vector", mm, mmv)
                    P.op("vector", "tensor_copy", dstv,
                         bank.rearrange("p (h d) -> p h d", d=64),
                         inc=[(ve, 1)])
                    pool.release(ib, ve, ve.n)
            P.selfedge("vector", ve)
            ve_v = ve.n
            P.wait("gpsimd", ve, ve_v)
            for tt in range(4):
                P.op("gpsimd", "dma_start",
                     out=vscr[c0 + 128 * tt:c0 + 128 * tt + 128, :],
                     in_=vw[:, tt * 1040:(tt + 1) * 1040], inc=[(st, 16)])
            def xk(ki, _c0=c0):
                if ki < 4:
                    return xnw[:, ki * 520 + 8:(ki + 1) * 520]
                return xnw[:, ki * 520 + 7:ki * 520 + 519]
            plucker(probsT, lambda m: m * 512, wk_h, KCH, xk, 512, None)
            ve_jk = ve.n
            P.wait("gpsimd", ve, ve_jk)
            for m in range(4):
                P.op("gpsimd", "dma_start",
                     out=jk4scr[m][:, c0:c0 + 512],
                     in_=probsT[:, m * 512:(m + 1) * 512], inc=[(st, 16)])
            P.wait("vector", mm, mm.n)
            for ki in range(KCH):
                P.op("vector", "tensor_copy", xnw[:, ki * 520:ki * 520 + 8],
                     xnw[:, ki * 520 + 512:ki * 520 + 520], inc=[(ve, 1)])
            P.selfedge("vector", ve)
            P.wait("vector", st, st.n)
        barrier(P, sems)

        def load_xn_win(c0, parity, halo=8):
            xr = xr2[parity]
            P.wait("sync", mm, mm.n)
            P.wait("sync", ve, ve.n)
            for ki in range(KCH):
                P.op("sync", "dma_start", out=xnw[:, ki * 520 + 8 - halo:
                                                 (ki + 1) * 520],
                     in_=xn1scr[ki * 128:(ki + 1) * 128, c0 - halo:c0 + 512],
                     inc=[(xr, 16)])
            for e in ("tensor", "vector", "scalar"):
                P.wait(e, xr, xr.n)

        P.wait("sync", st, st.n)
        for wi in range(4):
            c0 = QLO + 512 * wi
            load_xn_win(c0, wi % 2)

            def xq2(ki):
                if ki < 8:
                    return xnw[:, (ki % 8) * 520 + 8:(ki % 8) * 520 + 520]
                return xnw[:, (ki - 8) * 520 + 7:(ki - 8) * 520 + 519]
            plucker(q_pl, lambda m, _w=wi: m * G + 512 * _w, wq_h, 16, xq2, 512,
                    None)
        barrier(P, sems)

        P.wait("sync", ve, ve.n)
        P.wait("sync", st, st.n)
        for m in range(4):
            for j in range(4):
                P.op("sync", "dma_start", out=jk4scr[m][32 * j + 6:32 * j + 7, :],
                     in_=kmask_h, inc=[(lc, 16)])
        for j in range(4):
            P.op("sync", "dma_start", out=q_pl[32 * j + 6:32 * j + 7, :],
                 in_=ones1r_h, inc=[(lc, 16)])
        P.wait("sync", lc, lc.n)
        for e in ("tensor",):
            P.wait(e, lc, lc.n)

        NQW = G // 128
        NQW2 = G // 512
        for t in range(4):
            for qw in range(NQW2):
                qc = QLO + 512 * qw
                nks = (qc + 512) // 512
                kv_par = 0
                se_e = [None, None]
                mm_pv = None
                for i in range(8):
                    pool.gate(i)
                for ks in range(nks):
                    kc = 512 * ks
                    kvs = kv2[kv_par]
                    P.wait("sync", mm, mm.n)
                    P.op("sync", "dma_start",
                         out=jkst[:, kv_par * 512:kv_par * 512 + 512],
                         in_=jk4scr[t][:, kc:kc + 512], inc=[(kvs, 16)])
                    P.op("sync", "dma_start",
                         out=vst[:, kv_par * 4160:(kv_par + 1) * 4160]
                         .rearrange("p (k n) -> p k n", n=1040),
                         in_=vscr[kc:kc + 512, :]
                         .rearrange("(k p) n -> p k n", p=128), inc=[(kvs, 16)])
                    P.wait("tensor", kvs, kvs.n)
                    for kt in range(4):
                        for pair in range(2):
                            if se_e[pair] is not None:
                                P.wait("tensor", se, se_e[pair])
                            for j in (2 * pair, 2 * pair + 1):
                                P.op("tensor", "matmul", pool.view(j),
                                     jkst[32 * j:32 * j + 7,
                                          kv_par * 512 + 128 * kt:
                                          kv_par * 512 + 128 * kt + 128],
                                     q_pl[32 * j:32 * j + 7,
                                          t * G + 512 * qw:
                                          t * G + 512 * qw + 512],
                                     start=True, stop=True,
                                     tile_position=(32 * j, 0))
                            P.mark("tensor", mm)
                        mm_s = [mm.n - 1, mm.n]
                        if ks == nks - 1:
                            P.wait("vector", mm, mm.n)
                            for j in range(4):
                                sv = pool.view(j)
                                if kt:
                                    P.op("vector", "tensor_scalar",
                                         sv[:, 0:128 * kt], sv[:, 0:128 * kt],
                                         -1e9, None, op0=ALU.add)
                                P.op("vector", "tensor_tensor",
                                     sv[:, 128 * kt:128 * kt + 128],
                                     sv[:, 128 * kt:128 * kt + 128],
                                     cstDf[:, 128:256], op=ALU.add)
                            P.mark("vector", ve)
                            P.wait("scalar", ve, ve.n)
                        for pair in range(2):
                            P.wait("scalar", mm, mm_s[pair])
                            if mm_pv is not None:
                                P.wait("scalar", mm, mm_pv)
                            P.op("scalar", "activation",
                                 probsT[:, 1024 * pair:1024 * pair + 1024],
                                 pool.wide(pair), AF.Exp, inc=[(se, 1)])
                            se_e[pair] = se.n
                        for pair in range(2):
                            P.wait("tensor", se, se_e[pair])
                            for j in (2 * pair, 2 * pair + 1):
                                h = 4 * t + j
                                P.op("tensor", "matmul",
                                     pool.view(4 + j)[0:65, 0:512],
                                     vst[:, kv_par * 4160 + kt * 1040 + 65 * h:
                                         kv_par * 4160 + kt * 1040 + 65 * h + 65],
                                     probsT[:, 512 * j:512 * j + 512],
                                     start=(ks == 0 and kt == 0),
                                     stop=(ks == nks - 1 and kt == 3))
                        P.mark("tensor", mm)
                        mm_pv = mm.n
                    kv_par ^= 1
                P.wait("vector", mm, mm_pv)
                for pair in range(2):
                    P.wait("tensor", se, se_e[pair])
                for j in range(4):
                    h = 4 * t + j
                    P.op("vector", "tensor_scalar", sumsr[:],
                         pool.view(4 + j)[64:65, 0:512], 1e-30, None,
                         op0=ALU.add)
                    P.selfedge("vector", ve)
                    P.op("vector", "reciprocal", recipr[:], sumsr[:],
                         inc=[(ve, 1)], lowprec=True)
                    ve_rec = ve.n
                    P.wait("tensor", ve, ve_rec)
                    P.op("tensor", "matmul", pool.view(j)[0:64, 0:512],
                         consts["ones_row"][0:1, 0:64], recipr[0:1, 0:512],
                         start=True, stop=True, inc=[(mm, 1)])
                    mm_bc = mm.n
                    P.wait("vector", mm, mm_bc)
                    P.op("vector", "tensor_copy", bcn[0:64, 0:512],
                         pool.view(j)[0:64, 0:512])
                    P.selfedge("vector", ve)
                    for sseg in range(4):
                        P.op("vector", "tensor_tensor",
                             o_fm[64 * (h % 2):64 * (h % 2) + 64,
                                  (4 * qw + sseg) * 1024 + (h // 2) * 128:
                                  (4 * qw + sseg) * 1024 + (h // 2) * 128 + 128],
                             pool.view(4 + j)[0:64, 128 * sseg:128 * sseg + 128],
                             bcn[0:64, 128 * sseg:128 * sseg + 128],
                             op=ALU.mult, inc=[(ve, 1)])
                P.wait("tensor", ve, ve.n)
                for i in range(8):
                    pool.release(i, ve, ve.n)
        barrier(P, sems)

        P.wait("gpsimd", mm, mm.n)
        P.op("gpsimd", "dma_start",
             out=wstage[:].rearrange("p (k n) -> p k n", n=1024),
             in_=woT_h.rearrange("(k p) n -> p k n", p=128), inc=[(wl, 16)])
        P.wait("tensor", wl, wl.n)
        for qw in range(NQW):
            for m in range(KCH):
                ib, bank = pool.grab()
                for ki in range(KCH):
                    P.op("tensor", "matmul", bank[0:128, 0:128],
                         wstage[:, ki * 1024 + m * 128: ki * 1024 + m * 128 + 128],
                         o_fm[:, qw * 1024 + ki * 128: qw * 1024 + ki * 128 + 128],
                         start=(ki == 0), stop=(ki == KCH - 1),
                         inc=([(mm, 1)] if ki == KCH - 1 else []))
                mmv = mm.n
                P.wait("vector", mm, mmv)
                P.op("vector", "tensor_scalar",
                     acc[:, m * G + 128 * qw: m * G + 128 * qw + 128],
                     bank[0:128, 0:128], boP[:, m:m + 1], None, op0=ALU.add,
                     inc=[(ve, 1)])
                pool.release(ib, ve, ve.n)
        barrier(P, sems)

        P.op("scalar", "activation", signsT[:], sigs[:], AF.Sign, inc=[(se, 1)])
        se_sg = se.n
        P.wait("tensor", se, se_sg)
        pe_sem = P.sem("pe")
        for pt in range(G // 128):
            P.wait("sync", mm, mm.n)
            P.op("sync", "dma_start", out=peT[:],
                 in_=peT_h[:, 128 * pt:128 * pt + 128], inc=[(pe_sem, 16)])
            P.wait("tensor", pe_sem, pe_sem.n)
            ib, bank = pool.grab()
            P.op("tensor", "matmul", bank[0:128, 0:8],
                 peT[:], signsT[:], start=True,
                 stop=True, inc=[(mm, 1)])
            mmv = mm.n
            P.wait("vector", mm, mmv)
            P.op("vector", "reduce_max", mwin[:, 0:1], bank[0:128, 0:8], axis=AX,
                 inc=[(ve, 1)])
            P.selfedge("vector", ve)
            P.op("vector", "tensor_scalar", onehot[:], bank[0:128, 0:8],
                 mwin[:, 0:1], None, op0=ALU.is_ge, inc=[(ve, 1)])
            ve_oh = ve.n
            pool.release(ib, ve, ve_oh)
            ib2, bank2 = pool.grab()
            P.wait("tensor", ve, ve_oh)
            P.op("tensor", "matmul", bank2[0:8, 0:128], onehot[:],
                 cstDf[:, 0:128], start=True, stop=True, inc=[(mm, 1)])
            mmv2 = mm.n
            P.wait("vector", mm, mmv2)
            P.op("vector", "tensor_copy", onehotT[:, 128 * pt:128 * pt + 128],
                 bank2[0:8, 0:128], inc=[(ve, 1)])
            pool.release(ib2, ve, ve.n)
        P.selfedge("vector", ve)
        barrier(P, sems)

        for wi in range(4):
            for m in range(KCH):
                ib, bank = pool.grab()
                P.op("tensor", "matmul", bank,
                     b8[:, m * 128:(m + 1) * 128],
                     onehotT[:, 512 * wi:512 * wi + 512], start=True, stop=True,
                     inc=[(mm, 1)])
                mmv = mm.n
                P.wait("vector", mm, mmv)
                P.op("vector", "tensor_tensor",
                     acc[:, m * G + 512 * wi: m * G + 512 * wi + 512], bank,
                     acc[:, m * G + 512 * wi: m * G + 512 * wi + 512],
                     op=ALU.add, inc=[(ve, 1)])
                pool.release(ib, ve, ve.n)
        P.selfedge("vector", ve)
        for tl in range(8):
            wsx = ws2[tl % 2]
            P.wait("gpsimd", mm, mm.n)
            P.wait("gpsimd", wsx, wsx.n)
            P.op("gpsimd", "dma_start",
                 out=wstage[:].rearrange("p (k n) -> p k n", n=1024),
                 in_=w8scr[tl * 8:tl * 8 + 8].rearrange("k p n -> p k n"),
                 inc=[(wsx, 16)])
            P.wait("tensor", wsx, wsx.n)
            for wi in range(4):
                c0 = QLO + 512 * wi
                load_xn_win(c0, wi % 2, halo=0)
                ib, bank = pool.grab()
                P.op("tensor", "matmul", bank[0:1, 0:512],
                     cstD[0:8, 128 + tl:128 + tl + 1],
                     onehotT[:, 512 * wi:512 * wi + 512], start=True, stop=True,
                     inc=[(mm, 1)])
                mmv = mm.n
                P.wait("vector", mm, mmv)
                P.op("vector", "tensor_copy", ohrow[:], bank[0:1, 0:512],
                     inc=[(ve, 1)])
                pool.release(ib, ve, ve.n)
                ib2, bank2 = pool.grab()
                P.wait("tensor", ve, ve.n)
                P.op("tensor", "matmul", bank2, consts["ones_row"],
                     ohrow[:], start=True, stop=True, inc=[(mm, 1)])
                mmv2 = mm.n
                P.wait("vector", mm, mmv2)
                P.op("vector", "tensor_copy", obc[:], bank2, inc=[(ve, 1)])
                pool.release(ib2, ve, ve.n)
                P.selfedge("vector", ve)
                for ki in range(KCH):
                    P.op("vector", "tensor_tensor",
                         xw[:].bitcast(BF16)[:, ki * 512:(ki + 1) * 512],
                         xnw[:, ki * 520 + 8:(ki + 1) * 520], obc[:],
                         op=ALU.mult, inc=[(ve, 1)])
                P.selfedge("vector", ve)
                ve_xm = ve.n
                P.wait("tensor", ve, ve_xm)
                for m in range(KCH):
                    ib3, bank3 = pool.grab()
                    for ki in range(KCH):
                        P.op("tensor", "matmul", bank3,
                             wstage[:, ki * 1024 + m * 128:
                                    ki * 1024 + m * 128 + 128],
                             xw[:].bitcast(BF16)[:, ki * 512:(ki + 1) * 512],
                             start=(ki == 0), stop=(ki == KCH - 1),
                             inc=([(mm, 1)] if ki == KCH - 1 else []))
                    mmv3 = mm.n
                    P.wait("vector", mm, mmv3)
                    P.op("vector", "tensor_tensor",
                         acc[:, m * G + 512 * wi: m * G + 512 * wi + 512],
                         bank3,
                         acc[:, m * G + 512 * wi: m * G + 512 * wi + 512],
                         op=ALU.add, inc=[(ve, 1)])
                    pool.release(ib3, ve, ve.n)
        barrier(P, sems)

        for wi in range(4):
            c0 = QLO + 512 * wi
            P.wait("sync", ve, ve.n)
            P.wait("sync", st, st.n)
            P.wait("gpsimd", ve, ve.n)
            for ki in range(KCH):
                P.op("sync", "dma_start", out=xw[:, ki * 512:(ki + 1) * 512],
                     in_=xT_h[ki * 128:(ki + 1) * 128, c0:c0 + 512],
                     inc=[(lx, 16)])
                P.op("gpsimd", "dma_start", out=xaf[:, ki * 512:(ki + 1) * 512],
                     in_=afT_h[ki * 128:(ki + 1) * 128, c0:c0 + 512],
                     inc=[(la, 16)])
            P.wait("vector", lx, lx.n)
            P.wait("vector", la, la.n)
            for ki in range(KCH):
                P.op("vector", "tensor_tensor", xw[:, ki * 512:(ki + 1) * 512],
                     xw[:, ki * 512:(ki + 1) * 512],
                     xaf[:, ki * 512:(ki + 1) * 512], op=ALU.add)
            P.selfedge("vector", ve)
            for ki in range(KCH):
                P.op("vector", "tensor_tensor", xw[:, ki * 512:(ki + 1) * 512],
                     xw[:, ki * 512:(ki + 1) * 512],
                     acc[:, ki * G + 512 * wi: ki * G + 512 * wi + 512],
                     op=ALU.add, inc=[(ve, 1)])
            ve_out = ve.n
            P.wait("gpsimd", ve, ve_out)
            P.wait("gpsimd", st, st.n)
            P.op("gpsimd", "dma_start",
                 out=out_h.rearrange("(k p) n -> p k n", p=128)[
                     :, :, 512 * wi:512 * wi + 512],
                 in_=xw[:].rearrange("p (k n) -> p k n", n=512),
                 inc=[(st, 16)])
        P.wait("gpsimd", st, st.n)
        assert rr.sb_bytes <= 222 * 1024 * 128, rr.sb_bytes
        if fx is None:
            P.build()
        else:
            quiesce(P, sems)
            rr.pop()
    return nc


RG = [[0, 1, 2, 3], [4, 5, 6, 7]]


def _fused_io(nc):
    def mk(gname, shape, dt, kind="Internal"):
        return nc.dram_tensor(gname, list(shape), dt, kind=kind).ap()

    io = {"A": {}, "B0": {}, "B1": {}, "D": {}, "G": {}}
    A = io["A"]
    A["xh"] = mk("xh", [C, G + 8], F32, "ExternalInput")
    for n in ("wkT", "wvT", "wrT"):
        A[n] = mk(n, [C, C], F32, "ExternalInput")
    for n in ("tmP", "tdP", "tfP"):
        A[n] = mk(n, [128, KCH], F32, "ExternalInput")
    A["wkvl"] = mk("wkvl", [C, G], F32)
    A["wkl"] = mk("wkl", [C, G], F32)
    A["rbf"] = mk("rbf", [C, G], BF16)
    A["carr"] = mk("carr", [C, 2], F32)

    shared = {
        "identbf": mk("identbf", [128, 128], BF16, "ExternalInput"),
        "blockones": mk("blockones", [128, 128], BF16, "ExternalInput"),
        "triuA": mk("triuA", [128, 128], F32, "ExternalInput"),
        "triuB": mk("triuB", [128, 128], F32, "ExternalInput"),
    }
    w1scr = mk("w1scr", [44, 128, 1024], BF16)
    w2scr = mk("w2scr", [22, 128, 1024], BF16)
    for l in (0, 1):
        Bx = io[f"B{l}"]
        Bx.update(shared)
        Bx["w1scr"] = w1scr
        Bx["w2scr"] = w2scr
        pf = f"l{l}_"
        QLO2, XOFF = (520, 1544) if l == 0 else (512, 1024)
        QHI2 = XOFF + G
        Tq = QHI2 - QLO2
        Tk = QHI2 - (QLO2 - 512)
        KML = (XOFF // 128) * 128 + 128
        Bx["wqkv"] = mk(pf + "wqkv", [C, 1536], F32, "ExternalInput")
        Bx["wo"] = mk(pf + "wo", [512, C], F32, "ExternalInput")
        Bx["w1lm"] = mk(pf + "w1lm", [44, 128, 1024], F32, "ExternalInput")
        Bx["w2lm"] = mk(pf + "w2lm", [22, 128, 1024], F32, "ExternalInput")
        Bx["b1aP"] = mk(pf + "b1aP", [128, 22], F32, "ExternalInput")
        Bx["b1gP"] = mk(pf + "b1gP", [128, 22], F32, "ExternalInput")
        Bx["b2P"] = mk(pf + "b2P", [128, KCH], F32, "ExternalInput")
        Bx["lgP"] = mk(pf + "lgP", [128, KCH], F32, "ExternalInput")
        Bx["lbP"] = mk(pf + "lbP", [128, KCH], F32, "ExternalInput")
        Bx["scal4"] = mk(pf + "scal4", [128, 4], F32, "ExternalInput")
        Bx["cq"] = mk(pf + "cq", [128, Tq], F32, "ExternalInput")
        Bx["sq"] = mk(pf + "sq", [128, Tq], F32, "ExternalInput")
        Bx["ck"] = mk(pf + "ck", [128, Tk], F32, "ExternalInput")
        Bx["sk"] = mk(pf + "sk", [128, Tk], F32, "ExternalInput")
        Bx["kmask"] = mk(pf + "kmask", [1, KML], BF16, "ExternalInput")
    io["B0"]["xh2"] = mk("xh2", [C, 3600], F32, "ExternalInput")
    io["B0"]["xout0"] = mk("xout0", [C, 3072], BF16)
    B1 = io["B1"]
    B1["xin0"] = io["B0"]["xout0"]
    B1["af0"] = mk("af0", [C, G], F32)
    B1["f0scr"] = mk("f0scr", [C, G], BF16)
    B1["rwscr"] = mk("rwscr", [C, G], BF16)
    for n in ("wkvl", "wkl", "rbf", "tdP", "tfP"):
        B1[n] = A[n]
    B1["iota2d"] = mk("iota2d", [128, G], F32, "ExternalInput")
    B1["woTr"] = mk("woTr", [C, C], F32, "ExternalInput")
    carrX = mk("carrX", [C, 14], F32)
    B1["carrH"] = carrX[:, 6:14]

    D = io["D"]
    D["xTf"] = mk("xTf", [C, T], F32, "ExternalInput")
    afrot = mk("afrot", [C, 7 * G], F32)
    D["afT"] = afrot[:, 0:T]
    for v in range(4):
        D[f"wq{v}"] = mk(f"wq{v}", [2048, 512], F32, "ExternalInput")
        D[f"wk{v}"] = mk(f"wk{v}", [1024, 512], F32, "ExternalInput")
    D["wvT1"] = mk("wvT1", [C, C], F32, "ExternalInput")
    D["bv1"] = mk("bv1", [1, C], F32, "ExternalInput")
    D["woT1"] = mk("woT1", [C, C], F32, "ExternalInput")
    D["boP1"] = mk("boP1", [128, KCH], F32, "ExternalInput")
    D["sel128"] = mk("sel128", [128, 128], BF16, "ExternalInput")
    D["wflm"] = mk("wflm", [64, 128, 1024], F32, "ExternalInput")
    D["bf1"] = mk("bf1", [8, C], F32, "ExternalInput")
    D["sigsT"] = mk("sigsT", [32, 8], F32, "ExternalInput")
    D["peT"] = mk("peT", [32, G], F32, "ExternalInput")
    D["identf"] = mk("identf", [128, 128], F32, "ExternalInput")
    D["ident8"] = mk("ident8", [8, 8], BF16, "ExternalInput")
    D["triuD"] = mk("triuD", [128, 128], F32, "ExternalInput")
    D["kmaskD"] = mk("kmaskD", [1, T], BF16, "ExternalInput")
    D["ones1r"] = mk("ones1r", [1, 4 * G], BF16, "ExternalInput")
    D["outT"] = mk("outT", [C, G], F32, "ExternalOutput")
    D["xn1scr"] = mk("xn1scr", [C, T], BF16)
    D["jk4scr"] = mk("jk4scr", [4, 128, T], BF16)
    D["vscrD"] = mk("vscrD", [T, 1040], BF16)
    D["w8scr"] = mk("w8scr", [64, 128, 1024], BF16)

    Gd = io["G"]
    Gd["carrG"] = mk("carrG", [4 * C, 2], F32)
    Gd["carrX"] = carrX
    Gd["afG"] = mk("afG", [4 * C, G], F32)
    Gd["afrot"] = afrot
    Gd["zeroscr"] = mk("zeroscr", [C, 1024], BF16)
    return io


def _dyn(P, e, eng):
    et = {"sync": mybir.EngineType.SP, "scalar": mybir.EngineType.Activation,
          "gpsimd": mybir.EngineType.Pool}[eng]
    if not hasattr(P, "_dynst"):
        P._dynst = {}
    if eng not in P._dynst:
        pid = P.nc.partition_id((et,))
        jm = pid - (pid >= 4) * 4
        P._dynst[eng] = {"jm": jm, "e": e}
    return P._dynst[eng]


def dyn_dma(P, eng, sem, out_fn, in_ap):
    sem.bump(16)

    def fn(e, _of=out_fn, _i=in_ap, _s=sem, _eng=eng):
        d = _dyn(P, e, _eng)
        return e.dma_start(out=_of(e, d), in_=_i).then_inc(_s.h, 16)

    P.ops[eng].append(fn)
    P.kinds[eng].append("dma")
    P.meta[eng].append([(sem, sem.n)])


def _emit_glue1(P, rr, gio, A_io):
    cc = P.sem("cc1")
    gl = P.sem("gl1")
    gv = P.sem("gv1")
    carrG, carrX = gio["carrG"], gio["carrX"]
    rr.push("G1_")
    zc = rr.sb("z", (128, 16), F32)
    P.op("vector", "memset", zc[:], 0.0, inc=[(gv, 1)])
    P.wait("sync", gv, gv.n)
    for k in range(KCH):
        P.op("sync", "dma_start", out=carrX[k * 128:(k + 1) * 128, :],
             in_=zc[:, 0:14], inc=[(gl, 16)])
    P.op("gpsimd", "collective_compute", "AllGather", ALU.bypass,
         replica_groups=RG, ins=[A_io["carr"].opt()], outs=[carrG.opt()],
         inc=[(cc, 1)])
    P.wait("sync", cc, cc.n)
    P.wait("sync", gl, gl.n)
    for p in range(3):
        dyn_dma(P, "sync", gl,
                lambda e, d, _p=p: carrX[:, bass.ds(
                    _cv(e, d, "carb", lambda jm: (jm + 2) * 2) - _p * 2, 2)],
                carrG[p * C:(p + 1) * C, :])
    for e in ENGINES:
        P.wait(e, gl, gl.n)
        P.wait(e, cc, cc.n)
    rr.pop()


def _cv(e, d, key, expr):
    if key not in d:
        d[key] = e.compute_val(expr(d["jm"]))
    return d[key]


def _emit_glue2(P, rr, gio, B0_io):
    gl = P.sem("gl2")
    gv = P.sem("gv2")
    zeroscr = gio["zeroscr"]
    xout0 = B0_io["xout0"]
    rr.push("G2_")
    zb = rr.sb("z", (128, 1024), BF16)
    P.op("vector", "memset", zb[:], 0.0, inc=[(gv, 1)])
    P.wait("sync", gv, gv.n)
    for k in range(KCH):
        P.op("sync", "dma_start", out=zeroscr[k * 128:(k + 1) * 128, :],
             in_=zb[:], inc=[(gl, 16)])
    P.wait("sync", gl, gl.n)
    P.cond_dma("sync", out=xout0[:, 0:1024], in_=zeroscr, jval=0,
               inc=[(gl, 16)])
    for e in ENGINES:
        P.wait(e, gl, gl.n)
    rr.pop()


def _emit_glue3(P, rr, gio, B1_io):
    cc = P.sem("cc3")
    gl = P.sem("gl3")
    gv = P.sem("gv3")
    afG, afrot = gio["afG"], gio["afrot"]
    rr.push("G3_")
    zf = rr.sb("z", (128, 2048), F32)
    P.op("vector", "memset", zf[:], 0.0, inc=[(gv, 1)])
    P.wait("scalar", gv, gv.n)
    for b in range(4):
        for k in range(KCH):
            P.op("scalar", "dma_start",
                 out=afrot[k * 128:(k + 1) * 128, b * G:(b + 1) * G],
                 in_=zf[:], inc=[(gl, 16)])
    P.op("gpsimd", "collective_compute", "AllGather", ALU.bypass,
         replica_groups=RG, ins=[B1_io["af0"].opt()], outs=[afG.opt()],
         inc=[(cc, 1)])
    P.wait("scalar", cc, cc.n)
    P.wait("scalar", gl, gl.n)
    for p in range(4):
        dyn_dma(P, "scalar", gl,
                lambda e, d, _p=p: afrot[:, bass.ds(
                    _cv(e, d, "rotb", lambda jm: (3 - jm) * G) + _p * G, G)],
                afG[p * C:(p + 1) * C, :])
    for e in ENGINES:
        P.wait(e, gl, gl.n)
        P.wait(e, cc, cc.n)
    rr.pop()


def build_fused(upto="D"):
    order = ("A", "g1", "B0", "g2", "B1", "g3", "D")
    keep = order[:order.index(upto) + 1]
    nc = bass.Bass("TRN2", target_bir_lowering=False)
    io = _fused_io(nc)
    with ExitStack() as stack:
        P = Prog(nc, stack)
        rr = R(nc, stack)
        cv = P.sem("cv")
        consts = make_consts(P, rr, cv)
        fx = {"nc": nc, "P": P, "rr": rr, "consts": consts}
        if "A" in keep:
            fx["io"] = io["A"]
            build_launch_A(fx=fx)
        if "g1" in keep:
            _emit_glue1(P, rr, io["G"], io["A"])
        if "B0" in keep:
            fx["io"] = io["B0"]
            build_launch_B(0, fx=fx)
        if "g2" in keep:
            _emit_glue2(P, rr, io["G"], io["B0"])
        if "B1" in keep:
            fx["io"] = io["B1"]
            build_launch_B(1, fx=fx)
        if "g3" in keep:
            _emit_glue3(P, rr, io["G"], io["B1"])
        if "D" in keep:
            fx["io"] = io["D"]
            build_launch_D(fx=fx)
        assert len(P.all_sems) <= 95, len(P.all_sems)
        P.build()
    return nc


def fmpack(v):
    return np.ascontiguousarray(v.reshape(-1, 128).T)


def ct(a):
    return np.ascontiguousarray(a)


def host_prep_A(ins, xT):
    wkT = ct(ins["L0_attn_Wk"].T)
    wvT = ct(ins["L0_attn_Wv"].T)
    wrT = ct(ins["L0_attn_Wr"].T)
    tmP = fmpack(ins["L0_attn_time_mix"])
    tdP = fmpack(ins["L0_attn_time_decay"])
    tfP = fmpack(ins["L0_attn_time_first"])
    in_maps = []
    for c in range(NCORE):
        s, j = divmod(c, 4)
        o0 = j * G
        xhc = np.zeros((C, G + 8), np.float32)
        lo = max(0, o0 - 1)
        xhc[:, 1 - (o0 - lo):1 + G] = xT[s][:, lo:o0 + G]
        in_maps.append({
            "xh": xhc, "wkT": wkT, "wvT": wvT, "wrT": wrT,
            "tmP": tmP, "tdP": tdP, "tfP": tfP,
        })
    return in_maps


def _rotary_tables():
    half = np.arange(0, 64, 2, dtype=np.float64)
    inv_freq = 1.0 / (10000.0 ** (half / 64))
    t_arr = np.arange(T, dtype=np.float64)
    freqs = t_arr[:, None] * inv_freq[None, :]
    freqs2 = np.concatenate([freqs, freqs], 1)
    base = (half + 0.4 * 64) / (1.4 * 64)
    power = (t_arr - T // 2) / 256.0
    scale = base[None, :] ** power[:, None]
    scale2 = np.concatenate([scale, scale], 1)
    cos = np.cos(freqs2)
    sin = np.sin(freqs2)
    CQ = (cos * scale2).T.astype(np.float32)
    sq0 = (sin * scale2).T.astype(np.float32)[:32]
    SQ = np.concatenate([sq0, -sq0], 0)
    CK = (cos / scale2).T.astype(np.float32)
    sk0 = (sin / scale2).T.astype(np.float32)[:32]
    SK = np.concatenate([sk0, -sk0], 0)

    def dup(a):
        return ct(np.concatenate([a, a], 0))

    return dup(CQ), dup(SQ), dup(CK), dup(SK)


def _tab_slice(tab, start, n):
    out = np.zeros((128, n), np.float32)
    lo = max(0, start)
    hi = min(T, start + n)
    if hi > lo:
        out[:, lo - start:hi - start] = tab[:, lo:hi]
    return ct(out)


def _consts_maps():
    identbf = np.eye(128).astype(BF)
    blockones = np.zeros((128, 128), np.float32)
    blockones[:64, :64] = 1.0
    blockones[64:, 64:] = 1.0
    triuA = np.triu(np.full((128, 128), -1e9, np.float32), 1)
    triuB = np.tril(np.full((128, 128), -1e9, np.float32), -1)
    return {"identbf": identbf, "blockones": blockones.astype(BF),
            "triuA": triuA, "triuB": triuB}


def host_prep_B(l, ins, xT, prev, fused=False):
    wqkv = ct(ins["L0_ffn_wqkv"][l].astype(np.float32))
    qs = ins["L0_ffn_q_scale"][l]
    ksc = ins["L0_ffn_k_scale"][l]
    wo = ct(ins["L0_ffn_wo"][l])
    lg = ins["L0_ffn_ln_g"][l]
    lb = ins["L0_ffn_ln_b"][l]
    w1 = ins["L0_ffn_w1"][l]
    b1 = ins["L0_ffn_b1"][l]
    w2 = ins["L0_ffn_w2"][l]
    b2 = ins["L0_ffn_b2"][l]
    w1p = np.zeros((1024, 5632), np.float32)
    w1p[:, :2730] = w1[:, :2730]
    w1p[:, 2816:2816 + 2730] = w1[:, 2730:]
    b1p = np.zeros(5632, np.float32)
    b1p[:2730] = b1[:2730]
    b1p[2816:2816 + 2730] = b1[2730:]
    w2p = np.zeros((2816, 1024), np.float32)
    w2p[:2730] = w2
    w1lm = ct(w1p.reshape(8, 128, 44, 128).transpose(2, 1, 0, 3).reshape(44, 128, 1024))
    w2lm = ct(w2p.reshape(22, 128, 1024))
    b1aP = fmpack(b1p[:2816])
    b1gP = fmpack(b1p[2816:])
    b2P = fmpack(b2)
    lgP = fmpack(lg)
    lbP = fmpack(lb)
    qs2 = np.concatenate([qs, qs]).astype(np.float32)
    qss2 = np.concatenate([qs[32:], qs[:32], qs[32:], qs[:32]]).astype(np.float32)
    ks2 = np.concatenate([ksc, ksc]).astype(np.float32)
    kss2 = np.concatenate([ksc[32:], ksc[:32], ksc[32:], ksc[:32]]).astype(np.float32)
    scal4 = ct(np.stack([qs2, qss2, ks2, kss2], 1))
    CQ, SQ, CK, SK = _rotary_tables()
    cm = _consts_maps()
    if l == 0:
        NIN, XOFF = 3600, 1544
        QLO = XOFF - 1024
    else:
        NIN, XOFF = 3072, 1024
        QLO = XOFF - 512
    QHI = XOFF + G
    Tq = QHI - QLO
    Tk = QHI - (QLO - 512)
    maps = []
    for c in range(NCORE):
        sq, j = divmod(c, 4)
        o0 = j * G
        m = {
            "wqkv": wqkv, "wo": wo, "w1lm": w1lm, "w2lm": w2lm,
            "b1aP": b1aP, "b1gP": b1gP, "b2P": b2P, "lgP": lgP, "lbP": lbP,
            "scal4": scal4,
            "cq": _tab_slice(CQ, o0 + QLO - XOFF, Tq),
            "sq": _tab_slice(SQ, o0 + QLO - XOFF, Tq),
            "ck": _tab_slice(CK, o0 + QLO - 512 - XOFF, Tk),
            "sk": _tab_slice(SK, o0 + QLO - 512 - XOFF, Tk),
            "identbf": cm["identbf"], "blockones": cm["blockones"],
            "triuA": cm["triuA"], "triuB": cm["triuB"],
        }
        KML = (XOFF // 128) * 128 + 128
        tabs = o0 + np.arange(KML) - XOFF
        m["kmask"] = ct(np.where(tabs < 0, -1e9, 0.0)
                        .astype(BF).reshape(1, KML))
        if l == 0:
            xhc = np.zeros((C, NIN), np.float32)
            lo = max(0, o0 - XOFF)
            hi = min(T, o0 - XOFF + NIN)
            xhc[:, lo - (o0 - XOFF):hi - (o0 - XOFF)] = xT[sq][:, lo:hi]
            m["xh2"] = xhc
        elif fused:
            m["iota2d"] = ct(np.tile(np.arange(G, dtype=np.float32), (128, 1)))
            m["woTr"] = ct(ins["L0_attn_Wo"].T)
        else:
            xo = np.asarray(prev["xout0"][c])
            xo = xo.copy()
            tcols = o0 + np.arange(3072) - XOFF
            xo[:, tcols < 0] = np.zeros((C, 1), dtype=xo.dtype)
            m["xin0"] = xo
            m["wkvl"] = prev["wkvl"][c]
            m["wkl"] = prev["wkl"][c]
            m["rbf"] = prev["rbf"][c]
            carrH = np.zeros((C, 8), np.float32)
            for mi in range(4):
                p = j - 1 - mi
                if p >= 0:
                    carrH[:, 2 * mi:2 * mi + 2] = np.asarray(
                        prev["carr"][sq * 4 + p])
            m["carrH"] = ct(carrH)
            m["tdP"] = fmpack(ins["L0_attn_time_decay"])
            m["tfP"] = fmpack(ins["L0_attn_time_first"])
            m["iota2d"] = ct(np.tile(np.arange(G, dtype=np.float32), (128, 1)))
            m["woTr"] = ct(ins["L0_attn_Wo"].T)
        if fused:
            for nmm in ("wqkv", "wo", "w1lm", "w2lm", "b1aP", "b1gP", "b2P",
                        "lgP", "lbP", "scal4", "cq", "sq", "ck", "sk", "kmask"):
                m[f"l{l}_{nmm}"] = m.pop(nmm)
        maps.append(m)
    return maps


_PAIRS = [(0, 1), (0, 2), (0, 3), (1, 2), (1, 3), (2, 3)]


def _make_pe():
    import math as _m
    pos = np.arange(T, dtype=np.float32)[:, None]
    div = np.exp(np.arange(0, 32, 2, dtype=np.float32) *
                 np.float32(-_m.log(10000.0) / 32))
    ang = (pos * div).astype(np.float32)
    pe = np.zeros((T, 32), np.float32)
    pe[:, 0::2] = np.sin(ang)
    pe[:, 1::2] = np.cos(ang)
    return pe


def _qk_variant(WT, heads, comp, sign=None):
    din = WT.shape[0]
    out = np.zeros((din, 512), np.float32)
    for h in range(16):
        t, jj = divmod(h, 4)
        for p in range(6):
            c = WT[:, 4 * h + comp(p)]
            if sign is not None:
                c = c * sign[p]
            out[:, 128 * t + 32 * jj + p] = c
    return ct(out)


def host_prep_D(ins, xT, prev, fused=False):
    W1qT = ins["L1_attn_W1_q"].T.astype(np.float32)
    W2qT = ins["L1_attn_W2_q"].T.astype(np.float32)
    W1kT = ins["L1_attn_W1_k"].T.astype(np.float32)
    W2kT = ins["L1_attn_W2_k"].T.astype(np.float32)
    sig6 = np.array([1, -1, 1, 1, -1, 1], np.float32)
    perm = lambda p: _PAIRS[5 - p]
    wq = [_qk_variant(W1qT, None, lambda p: _PAIRS[p][0]),
          _qk_variant(W1qT, None, lambda p: _PAIRS[p][1]),
          _qk_variant(W2qT, None, lambda p: _PAIRS[p][0]),
          _qk_variant(W2qT, None, lambda p: _PAIRS[p][1])]
    wk = [_qk_variant(W1kT, None, lambda p: perm(p)[0], sig6),
          _qk_variant(W1kT, None, lambda p: perm(p)[1], sig6),
          _qk_variant(W2kT, None, lambda p: perm(p)[0]),
          _qk_variant(W2kT, None, lambda p: perm(p)[1])]
    wvT = ct(ins["L1_attn_Wv"].T)
    bv = ct(ins["L1_attn_bv"].reshape(1, C))
    woT = ct(ins["L1_attn_Wo"].T)
    boP = fmpack(ins["L1_attn_bo"])
    sel128 = np.zeros((128, 128), np.float32)
    for r in range(128):
        for m in range(128):
            if r // 32 == m // 32 and r % 32 < 6:
                sel128[r, m] = 1.0
    wflm = ct(np.stack([ins["L1_ffn_W"][t].T.reshape(8, 128, 1024)
                        for t in range(8)]).reshape(64, 128, 1024))
    bf1 = ct(ins["L1_ffn_b"].astype(np.float32))
    sigsT = ct(ins["L1_ffn_tile_sigs"].T.astype(np.float32))
    peT_full = ct(_make_pe().T)
    identf = np.eye(128, dtype=np.float32)
    ident8 = np.eye(8, dtype=BF)
    triuD = np.tril(np.full((128, 128), -1e9, np.float32), -1)
    af_seq = None if fused else [
        np.concatenate([np.asarray(prev["af0"][4 * sq + jj])
                        for jj in range(4)], axis=1) for sq in range(2)]
    maps = []
    for c in range(NCORE):
        sq, j = divmod(c, 4)
        o0 = j * G
        Pfx = o0 + G
        xrot = np.zeros((C, T), np.float32)
        xrot[:, T - Pfx:] = xT[sq][:, :Pfx]
        km = np.zeros((1, T), np.float32)
        km[0, :T - Pfx] = -1e9
        m = {
            "xTf": xrot,
            "wvT1": wvT, "bv1": bv, "woT1": woT, "boP1": boP,
            "sel128": sel128.astype(BF), "wflm": wflm, "bf1": bf1,
            "sigsT": sigsT, "peT": ct(peT_full[:, o0:o0 + G]),
            "identf": identf, "ident8": ident8, "triuD": triuD,
            "kmaskD": km.astype(BF),
            "ones1r": np.ones((1, 4 * G), BF),
        }
        for v in range(4):
            m[f"wq{v}"] = wq[v]
            m[f"wk{v}"] = wk[v]
        if not fused:
            afrot = np.zeros((C, T), np.float32)
            afrot[:, T - Pfx:] = af_seq[sq][:, :Pfx]
            m["afT"] = afrot
        maps.append(m)
    return maps


def host_prep_fused(ins, xT):
    mA = host_prep_A(ins, xT)
    mB0 = host_prep_B(0, ins, xT, {}, fused=True)
    mB1 = host_prep_B(1, ins, xT, {}, fused=True)
    mD = host_prep_D(ins, xT, {}, fused=True)
    maps = []
    for c in range(NCORE):
        m = {}
        for part in (mA[c], mB0[c], mB1[c], mD[c]):
            m.update(part)
        maps.append(m)
    return maps


_NC_FUSED = None


def kernel(**inputs):
    global _NC_FUSED
    ins = {k: np.asarray(v) for k, v in inputs.items()}
    xT = ct(ins["L0_attn_x"].transpose(0, 2, 1))
    if _NC_FUSED is None:
        _NC_FUSED = build_fused()
    res = run_bass_kernel_spmd(_NC_FUSED, host_prep_fused(ins, xT),
                               core_ids=list(range(NCORE)))
    out = np.empty((B, T, C), np.float32)
    for c in range(NCORE):
        sq, j = divmod(c, 4)
        out[sq, j * G:(j + 1) * G, :] = np.asarray(res.results[c]["outT"]).T
    return out




# version 06
# speedup vs baseline: 1.6486x; 1.0700x over previous
import math
from contextlib import ExitStack

import numpy as np
import ml_dtypes

import concourse.bass as bass
import concourse.mybir as mybir
from concourse.bass_utils import run_bass_kernel_spmd

F32 = mybir.dt.float32
BF16 = mybir.dt.bfloat16
AF = mybir.ActivationFunctionType
ALU = mybir.AluOpType
AX = mybir.AxisListType.X
BF = ml_dtypes.bfloat16

NCORE = 8
B, T, C = 2, 8192, 1024
G = T * B // NCORE
KCH = C // 128

ENGINES = ("sync", "tensor", "vector", "scalar", "gpsimd")


class Sem:
    __slots__ = ("h", "n", "name")

    def __init__(self, h, name):
        self.h = h
        self.n = 0
        self.name = name

    def bump(self, k):
        self.n += k
        return self.n


class Prog:
    def __init__(self, nc, stack):
        self.nc = nc
        self.stack = stack
        self.ops = {e: [] for e in ENGINES}
        self.kinds = {e: [] for e in ENGINES}
        self.meta = {e: [] for e in ENGINES}
        self.all_sems = []
        self._semcnt = 0

    def sem(self, name=None):
        name = f"{name or 's'}_{self._semcnt}"
        self._semcnt += 1
        h = self.stack.enter_context(self.nc.semaphore(name))
        s = Sem(h, name)
        self.all_sems.append(s)
        return s

    def op(self, eng, method, *args, inc=(), lowprec=False, **kw):
        inc = tuple(inc)

        def fn(e, _m=method, _a=args, _k=kw, _inc=inc, _lp=lowprec):
            if _lp:
                with self.nc.allow_low_precision(reason="bf16 out ok"):
                    ins = getattr(e, _m)(*_a, **_k)
            else:
                ins = getattr(e, _m)(*_a, **_k)
            for s, k in _inc:
                ins = ins.then_inc(s.h, k)
            return ins

        kind = "dma" if method in ("dma_start", "collective_compute") else "op"
        for s, k in inc:
            s.bump(k)
        self.ops[eng].append(fn)
        self.kinds[eng].append(kind)
        self.meta[eng].append([(s, s.n) for s, k in inc])

    def cond_dma(self, eng, *, out, in_, jval, inc=()):
        inc = tuple(inc)
        for s, k in inc:
            s.bump(k)
        et = {"sync": mybir.EngineType.SP, "gpsimd": mybir.EngineType.Pool,
              "scalar": mybir.EngineType.Activation}[eng]
        if not hasattr(self, "_pid"):
            self._pid = {}

        def fn(e, _o=out, _i=in_, _j=jval, _inc=inc, _et=et):
            if _et not in self._pid:
                self._pid[_et] = self.nc.partition_id((_et,))
            pid = self._pid[_et]
            from concourse.expressions import s_logical_or
            cond = s_logical_or(pid == _j, pid == _j + 4)
            ins = e.dma_start(out=_o, in_=_i, cond=cond)
            for s, k in _inc:
                ins = ins.then_inc(s.h, k)
            return ins

        self.ops[eng].append(fn)
        self.kinds[eng].append("dma")
        self.meta[eng].append([(s, s.n) for s, k in inc])

    def wait(self, eng, sem, val=None):
        v = sem.n if val is None else val
        self.ops[eng].append(lambda e, _s=sem.h, _v=v: e.wait_ge(_s, _v))
        self.kinds[eng].append("wait")
        self.meta[eng].append(None)

    def last_op_inc(self, eng):
        for i in range(len(self.ops[eng]) - 1, -1, -1):
            if self.kinds[eng][i] == "wait":
                continue
            if self.kinds[eng][i] != "op":
                return None
            m = self.meta[eng][i]
            return m[-1] if m else None
        return None

    def mark(self, eng, sem, k=1):
        stack = []
        while self.kinds[eng] and self.kinds[eng][-1] != "op":
            stack.append((self.ops[eng].pop(), self.kinds[eng].pop(),
                          self.meta[eng].pop()))
        assert self.kinds[eng], f"no compute op to mark on {eng}"
        fn = self.ops[eng].pop()
        meta = self.meta[eng].pop()
        self.kinds[eng].pop()

        def wrapped(e, _fn=fn, _s=sem, _k=k):
            ins = _fn(e)
            return ins.then_inc(_s.h, _k)

        sem.bump(k)
        self.ops[eng].append(wrapped)
        self.kinds[eng].append("op")
        self.meta[eng].append((meta or []) + [(sem, sem.n)])
        for f, kk, mmx in reversed(stack):
            self.ops[eng].append(f)
            self.kinds[eng].append(kk)
            self.meta[eng].append(mmx)

    def selfedge(self, eng, sem):
        last = self.last_op_inc(eng)
        if last is None:
            self.mark(eng, sem, 1)
            last = (sem, sem.n)
        self.wait(eng, last[0], last[1])

    def build(self):
        with self.nc.Block() as block:
            for e in ENGINES:
                if not self.ops[e]:
                    continue

                def section(eng, _e=e):
                    for fn in self.ops[_e]:
                        fn(eng)

                getattr(block, e)(section)


class R:

    def __init__(self, nc, stack):
        self.nc = nc
        self.stacks = [stack]
        self.tags = [""]
        self.sb_bytes = 0
        self.scope_bytes = [0]
        self.names = set()

    def push(self, tag=""):
        st = ExitStack()
        self.stacks.append(st)
        self.tags.append(tag)
        self.scope_bytes.append(0)
        return st

    def pop(self):
        self.tags.pop()
        self.sb_bytes -= self.scope_bytes.pop()
        self.stacks.pop().close()

    def sb(self, name, shape, dtype):
        name = self.tags[-1] + name
        assert name not in self.names, name
        self.names.add(name)
        nbytes = shape[1] * (2 if dtype == BF16 else 4)
        self.sb_bytes += nbytes * 128
        self.scope_bytes[-1] += nbytes * 128
        t = self.stacks[-1].enter_context(
            self.nc.sbuf_tensor(name, list(shape), dtype))
        return t

    def ps(self, name, shape, dtype=F32):
        name = self.tags[-1] + name
        assert name not in self.names, name
        self.names.add(name)
        return self.stacks[-1].enter_context(
            self.nc.psum_tensor(name, list(shape), dtype))


class PsumRing:

    def __init__(self, P, rr, name, n, shape=(128, 512), dtype=F32):
        self.P = P
        self.tiles = [rr.ps(f"{name}{i}", list(shape), dtype) for i in range(n)]
        self.free_at = [None] * n
        self.i = 0

    def grab(self):
        i = self.i
        self.i = (self.i + 1) % len(self.tiles)
        if self.free_at[i] is not None:
            s, v = self.free_at[i]
            self.P.wait("tensor", s, v)
        return i, self.tiles[i]

    def release(self, i, sem, val):
        self.free_at[i] = (sem, val)


def load_weight_fm(P, rr, name, hbm_ap, Ktot, Mtot, ldsem, queue="gpsimd"):
    kch = Ktot // 128
    sb = rr.sb(name, (128, kch * Mtot), BF16)
    for ki in range(kch):
        P.op(queue, "dma_start", out=sb[:, ki * Mtot:(ki + 1) * Mtot],
             in_=hbm_ap[ki * 128:(ki + 1) * 128, :], inc=[(ldsem, 16)])

    def w(ki, m0, mrows):
        return sb[:, ki * Mtot + m0: ki * Mtot + m0 + mrows]

    return w, ldsem.n


def fm_tiles(sb, N):

    def x(ki, n0, nw):
        return sb[:, ki * N + n0: ki * N + n0 + nw]

    return x


def mm_proj(P, pr, *, wt, xt, kch, Mtot, N, mm_sem, evac, nwin=512, rank1=(),
            wt_ready=None, xt_ready=None, mlist=None):
    first = True
    for m0 in (mlist if mlist is not None else range(0, Mtot, 128)):
        mr = min(128, Mtot - m0)
        for n0 in range(0, N, nwin):
            nw = min(nwin, N - n0)
            i, tile = pr.grab()
            if first:
                if wt_ready is not None:
                    P.wait("tensor", *wt_ready)
                if xt_ready is not None:
                    P.wait("tensor", *xt_ready)
                first = False
            nterms = kch + len(rank1)
            for ki in range(kch):
                last = ki == nterms - 1
                P.op("tensor", "matmul", tile[0:mr, 0:nw], wt(ki, m0, mr),
                     xt(ki, n0, nw), start=(ki == 0), stop=last,
                     inc=([(mm_sem, 1)] if last else []))
            for rj, (la, ra) in enumerate(rank1):
                last = kch + rj == nterms - 1
                P.op("tensor", "matmul", tile[0:mr, 0:nw], la(m0, mr), ra(n0, nw),
                     start=False, stop=last, inc=([(mm_sem, 1)] if last else []))
            free = evac(m0, mr, n0, nw, tile[0:mr, 0:nw], mm_sem.n)
            pr.release(i, *free)


def make_ln_bufs(rr, tag, x2=None):
    return {
        "x2": (x2 if x2 is not None else rr.sb(f"{tag}_x2", (128, KCH * 512), BF16)),
        "zsc": rr.sb(f"{tag}_z", (128, 512), BF16),
        "rowbuf": rr.sb(f"{tag}_rows", (1, 3 * 512), F32),
        "rowbf": rr.sb(f"{tag}_rowbf", (1, 2 * 512), BF16),
        "bc": rr.sb(f"{tag}_bc", (128, 2 * 512), BF16),
    }


class LNState:
    def __init__(self):
        self.mm_stats_prev = None
        self.ve_rows_prev = None
        self.ve_apply_prev = None


def emit_ln_win(P, pr, bufs, st8, xt, ot, nw, consts, sems, gain=None, bias=None,
                ready=None):
    mm, ve, se = sems["mm"], sems["ve"], sems["se"]
    x2, zsc = bufs["x2"], bufs["zsc"]
    rowbuf, rowbf, bc = bufs["rowbuf"], bufs["rowbf"], bufs["bc"]
    if ready is not None:
        P.wait("scalar", *ready)
    if st8.mm_stats_prev is not None:
        P.wait("scalar", mm, st8.mm_stats_prev)
    if st8.ve_apply_prev is not None:
        P.wait("scalar", ve, st8.ve_apply_prev)
    for ki in range(KCH):
        P.op("scalar", "activation", x2[:, ki * 512: ki * 512 + nw],
             xt(ki), AF.Square, inc=[(se, 1)])
    se_sq = se.n
    i1, t1 = pr.grab()
    i2, t2 = pr.grab()
    if ready is not None:
        P.wait("tensor", *ready)
    P.wait("tensor", se, se_sq)
    for ki in range(KCH):
        P.op("tensor", "matmul", t1[0:1, 0:nw], consts["ones_col"], xt(ki),
             start=(ki == 0), stop=(ki == KCH - 1),
             inc=([(mm, 1)] if ki == KCH - 1 else []))
    for ki in range(KCH):
        P.op("tensor", "matmul", t2[0:1, 0:nw], consts["ones_col"],
             x2[:, ki * 512: ki * 512 + nw],
             start=(ki == 0), stop=(ki == KCH - 1),
             inc=([(mm, 1)] if ki == KCH - 1 else []))
    mm_stats = mm.n
    st8.mm_stats_prev = mm_stats
    mrow = rowbuf[:, 0:nw]
    e2row = rowbuf[:, 512:512 + nw]
    m2row = rowbuf[:, 1024:1024 + nw]
    varrow = m2row
    lnrow = m2row
    rstd = m2row
    P.wait("scalar", mm, mm_stats)
    if st8.ve_rows_prev is not None:
        P.wait("scalar", ve, st8.ve_rows_prev)
    P.op("scalar", "activation", mrow, t1[0:1, 0:nw], AF.Copy, scale=1.0 / C)
    P.op("scalar", "activation", e2row, t2[0:1, 0:nw], AF.Copy, scale=1.0 / C,
         inc=[(se, 1)])
    se_rows = se.n
    P.wait("vector", se, se_rows)
    P.op("vector", "tensor_tensor", m2row, mrow, mrow, op=ALU.mult)
    P.selfedge("vector", ve)
    P.op("vector", "tensor_tensor", varrow, e2row, m2row, op=ALU.subtract,
         inc=[(ve, 1)])
    ve_var = ve.n
    P.wait("scalar", ve, ve_var)
    P.op("scalar", "activation", lnrow, varrow, AF.Ln,
         bias=consts["cvals"][0:1, 0:1])
    P.selfedge("scalar", se)
    P.op("scalar", "activation", rstd, lnrow, AF.Exp, scale=-0.5, inc=[(se, 1)])
    se_rstd = se.n
    P.wait("vector", se, se_rstd)
    P.op("vector", "tensor_copy", rowbf[:, 0:nw], mrow)
    P.op("vector", "tensor_copy", rowbf[:, 512:512 + nw], rstd, inc=[(ve, 1)])
    ve_rows = ve.n
    st8.ve_rows_prev = ve_rows
    i3, t3 = pr.grab()
    i4, t4 = pr.grab()
    P.wait("tensor", ve, ve_rows)
    P.op("tensor", "matmul", t3[:, 0:nw], consts["ones_row"], rowbf[:, 0:nw],
         start=True, stop=True, inc=[(mm, 1)])
    P.op("tensor", "matmul", t4[:, 0:nw], consts["ones_row"],
         rowbf[:, 512:512 + nw], start=True, stop=True, inc=[(mm, 1)])
    mm_bc = mm.n
    P.wait("vector", mm, mm_bc)
    P.op("vector", "tensor_copy", bc[:, 0:nw], t3[:, 0:nw])
    P.op("vector", "tensor_copy", bc[:, 512:512 + nw], t4[:, 0:nw], inc=[(ve, 1)])
    P.selfedge("vector", ve)
    pr.release(i1, ve, ve.n)
    pr.release(i2, ve, ve.n)
    pr.release(i3, ve, ve.n)
    pr.release(i4, ve, ve.n)
    for ki in range(KCH):
        P.op("vector", "tensor_tensor", x2[:, ki * 512: ki * 512 + nw],
             xt(ki), bc[:, 0:nw], op=ALU.subtract)
    P.selfedge("vector", ve)
    if gain is None:
        for ki in range(KCH):
            P.op("vector", "tensor_tensor", ot(ki),
                 x2[:, ki * 512: ki * 512 + nw],
                 bc[:, 512:512 + nw], op=ALU.mult, inc=[(ve, 1)])
    else:
        for ki in range(KCH):
            P.op("vector", "tensor_tensor", x2[:, ki * 512: ki * 512 + nw],
                 x2[:, ki * 512: ki * 512 + nw],
                 bc[:, 512:512 + nw], op=ALU.mult)
        P.selfedge("vector", ve)
        for ki in range(KCH):
            P.op("vector", "tensor_scalar", ot(ki),
                 x2[:, ki * 512: ki * 512 + nw],
                 gain(ki), bias(ki), op0=ALU.mult, op1=ALU.add, inc=[(ve, 1)])
    st8.ve_apply_prev = ve.n
    return ve.n


def emit_ln(P, rr, pr, tag, xb, N, consts, sems, gain=None, bias=None, out=None,
            ready=None):
    out = out if out is not None else xb
    bufs = make_ln_bufs(rr, tag)
    st8 = LNState()
    xt0 = fm_tiles(xb, N)
    ot0 = fm_tiles(out, N)
    r = None
    first = True
    for n0 in range(0, N, 512):
        nw = min(512, N - n0)
        xt = lambda ki, _n0=n0, _nw=nw: xt0(ki, _n0, _nw)
        ot = lambda ki, _n0=n0, _nw=nw: ot0(ki, _n0, _nw)
        r = emit_ln_win(P, pr, bufs, st8, xt, ot, nw, consts, sems,
                        gain=gain, bias=bias, ready=(ready if first else None))
        first = False
    return r


def make_consts(P, rr, ve, tag=""):
    ones_col = rr.sb(f"cons_col{tag}", (128, 1), BF16)
    ones_row = rr.sb(f"cons_row{tag}", (1, 128), BF16)
    cvals = rr.sb(f"cons_vals{tag}", (128, 4), F32)
    P.op("vector", "memset", ones_col[:], 1.0)
    P.op("vector", "memset", ones_row[:], 1.0)
    P.op("vector", "memset", cvals[:, 0:1], 1e-5)
    P.op("vector", "memset", cvals[:, 1:2], 1e-24)
    P.op("vector", "memset", cvals[:, 2:3], 0.0)
    P.op("vector", "memset", cvals[:, 3:4], 1.0, inc=[(ve, 1)])
    d = {"ones_col": ones_col[:], "ones_row": ones_row[:], "cvals": cvals,
         "ready": ve.n}
    for e in ("tensor", "scalar", "gpsimd"):
        P.wait(e, ve, d["ready"])
    return d


class PsumPool:

    def __init__(self, P, rr, tag):
        self.P = P
        self.tA = [rr.ps(f"{tag}A{i}", [128, 1024], F32) for i in range(2)]
        self.tB = [rr.ps(f"{tag}B{i}", [128, 512], F32) for i in range(4)]
        self.free_at = [None] * 8
        self.i = 0

    def view(self, idx):
        if idx < 4:
            return self.tA[idx // 2][:, 512 * (idx % 2):512 * (idx % 2) + 512]
        return self.tB[idx - 4][:, :]

    def wide(self, j):
        return self.tA[j][:, :]

    def gate(self, idx, eng="tensor"):
        f = self.free_at[idx]
        if f is not None:
            self.P.wait(eng, f[0], f[1])
            self.free_at[idx] = None

    def release(self, idx, sem, val):
        self.free_at[idx] = (sem, val)

    def grab(self):
        i = self.i
        self.i = (self.i + 1) % 6
        self.gate(i)
        return i, self.view(i)


def quiesce(P, sems):
    barrier(P, sems, engines=("tensor", "vector", "scalar", "gpsimd"))
    targets = [(sm, sm.n) for sm in P.all_sems if sm.n > 0]
    for e in ENGINES:
        if not any(k == "op" for k in P.kinds[e]):
            continue
        for (sm, v) in targets:
            P.wait(e, sm, v)


def barrier(P, sems, engines=("tensor", "vector", "scalar")):
    names = {"tensor": "mm", "vector": "ve", "scalar": "se", "gpsimd": "ve"}
    marks = []
    for e in engines:
        if not any(k == "op" for k in P.kinds[e]):
            continue
        last = P.last_op_inc(e)
        if last is None:
            s = sems[names[e]]
            P.mark(e, s)
            last = (s, s.n)
        marks.append((e,) + last)
    for e in engines:
        if not any(k == "op" for k in P.kinds[e]):
            continue
        for (e2, s, v) in marks:
            if e2 != e:
                P.wait(e, s, v)


def emit_ffn_inner(P, rr, pool, tag, *, xin, QLO, QHI, wq_sb, tabs, wts, consts,
                   sems, out_cb, kmask_bf, xoff, lnb=None, xin_load=None,
                   wq_load=None, xo_stage=None, OUT0=0):
    mm, ve, se = sems["mm"], sems["ve"], sems["se"]
    tab = sems["tab"]
    wst2 = sems["wst2"]
    NW = (QHI - QLO) // 512
    if DEBUG_NW is not None:
        NW = min(NW, DEBUG_NW)
    KLO = QLO - 512

    kn = rr.sb(f"{tag}_kn", (128, 2 * 4 * 512), BF16)
    vtok = rr.sb(f"{tag}_v", (128, 2 * 4 * 520), BF16)
    qn = rr.sb(f"{tag}_qn", (128, 4 * 512), BF16)
    q2s = rr.sb(f"{tag}_q2", (128, 512), BF16)
    rstdb = rr.sb(f"{tag}_rstd", (128, 512), BF16)
    tmp1 = rr.sb(f"{tag}_t1", (128, 512), BF16)
    tmp2 = rr.sb(f"{tag}_t2", (128, 512), BF16)
    tmp3 = rr.sb(f"{tag}_t3", (128, 512), BF16)
    tabst = rr.sb(f"{tag}_tabs", (128, 2 * 512), BF16)
    mf32 = rr.sb(f"{tag}_mf", (128, 8), F32)
    mbuf = rr.sb(f"{tag}_mb", (128, 8), BF16)
    mrows = rr.sb(f"{tag}_mrows", (1, 8 * 128), BF16)
    probsT = rr.sb(f"{tag}_pT", (128, 8 * 128), BF16)
    sumsr = rr.sb(f"{tag}_sums", (1, 1024), F32)
    recipr = rr.sb(f"{tag}_recip", (1, 1024), BF16)
    o_fm = rr.sb(f"{tag}_ofm", (128, 4 * 128), BF16)
    bcn = rr.sb(f"{tag}_bcn", (128, 128), BF16)
    xa = rr.sb(f"{tag}_xa", (128, 2 * KCH * 520), BF16)
    hln = rr.sb(f"{tag}_hln", (128, KCH * 512), BF16)
    ffnh = rr.sb(f"{tag}_ffnh", (128, 6 * 512), BF16)
    gts = rr.sb(f"{tag}_gt", (128, 512), BF16)
    w1st = rr.sb(f"{tag}_w1st", (128, 4 * 1024), BF16)
    w2st = rr.sb(f"{tag}_w2st", (128, 1024), BF16)
    if xo_stage is None:
        xo_stage = rr.sb(f"{tag}_xo", (128, KCH * 512), BF16)
    if lnb is None:
        lnb = make_ln_bufs(rr, f"{tag}_iln")
    lnst = LNState()
    state = {"store": None}

    P.op("vector", "memset", vtok[:], 1.0)
    P.op("vector", "memset", xa[:], 0.0, inc=[(ve, 1)])
    P.selfedge("vector", ve)

    def xsh(ki, c0, nw):
        return xin(ki, c0 - (1 if ki >= 4 else 0), nw)

    def kslot(c0):
        return ((c0 - KLO) // 512) % 2

    def kn_ap(c0, base_par, j0, nj):
        slot = kslot(c0)
        hp = None
        return kn[base_par:base_par + 64,
                  slot * 2048 + j0: slot * 2048 + j0 + nj]

    def project_qk(c0, is_q):
        if DEBUG_PROJ < 2:
            return 0
        if is_q:
            t0 = c0 - QLO
            P.wait("gpsimd", ve, ve.n)
            P.op("gpsimd", "dma_start", out=tabst[:, 0:512],
                 in_=tabs["cq"][:, t0:t0 + 512], inc=[(tab, 16)])
            P.op("gpsimd", "dma_start", out=tabst[:, 512:1024],
                 in_=tabs["sq"][:, t0:t0 + 512], inc=[(tab, 16)])
            ctab, stab = tabst[:, 0:512], tabst[:, 512:1024]
            scl, scls = wts["qs"], wts["qss"]
            mlo = 0
        else:
            t0 = c0 - KLO
            P.wait("gpsimd", ve, ve.n)
            P.op("gpsimd", "dma_start", out=tabst[:, 0:512],
                 in_=tabs["ck"][:, t0:t0 + 512], inc=[(tab, 16)])
            P.op("gpsimd", "dma_start", out=tabst[:, 512:1024],
                 in_=tabs["sk"][:, t0:t0 + 512], inc=[(tab, 16)])
            ctab, stab = tabst[:, 0:512], tabst[:, 512:1024]
            scl, scls = wts["ks"], wts["kss"]
            mlo = 512
        tab_val = tab.n
        if wq_load is not None:
            wq_load(mlo // 512)
        for hp in range(4):
            m0 = mlo + hp * 128
            ib, bank = pool.grab()
            for ki in range(KCH):
                P.op("tensor", "matmul", bank, wq_sb(ki, m0, 128),
                     xsh(ki, c0, 512), start=(ki == 0), stop=(ki == KCH - 1),
                     inc=([(mm, 1)] if ki == KCH - 1 else []))
            mmv = mm.n
            if is_q:
                raw = qn[:, hp * 512:(hp + 1) * 512]
            else:
                slot = kslot(c0)
                raw = kn[:, slot * 2048 + hp * 512: slot * 2048 + (hp + 1) * 512]
            if DEBUG_PROJ == 21:
                pool.release(ib, mm, mmv)
                continue
            P.wait("scalar", mm, mmv)
            P.op("scalar", "activation", q2s[:], bank, AF.Square, inc=[(se, 1)])
            se_sq = se.n
            if DEBUG_PROJ == 22:
                pool.release(ib, se, se_sq)
                continue
            P.wait("vector", mm, mmv)
            P.wait("vector", se, se_sq)
            P.op("vector", "tensor_copy", raw, bank, inc=[(ve, 1)])
            pool.release(ib, ve, ve.n)
            if DEBUG_PROJ < 3:
                continue
            ib2, bank2 = pool.grab()
            P.wait("tensor", se, se_sq)
            P.op("tensor", "matmul", bank2, consts["blockones"], q2s[:],
                 start=True, stop=True, inc=[(mm, 1)])
            mm_n2 = mm.n
            P.wait("scalar", mm, mm_n2)
            P.wait("scalar", ve, ve.n)
            P.op("scalar", "activation", rstdb[:], bank2, AF.Ln,
                 bias=consts["cvals"][:, 1:2])
            P.selfedge("scalar", se)
            P.op("scalar", "activation", rstdb[:], rstdb[:], AF.Exp, scale=-0.5,
                 inc=[(se, 1)])
            se_rstd = se.n
            pool.release(ib2, se, se_rstd)
            P.wait("vector", se, se_rstd)
            P.wait("vector", tab, tab_val)
            P.selfedge("vector", ve)
            P.op("vector", "tensor_tensor", tmp1[:], raw, rstdb[:], op=ALU.mult)
            P.selfedge("vector", ve)
            P.op("vector", "scalar_tensor_tensor", tmp2[:], tmp1[:], scl, ctab,
                 op0=ALU.mult, op1=ALU.mult)
            for (a, b) in ((0, 32), (32, 0), (64, 96), (96, 64)):
                P.op("vector", "scalar_tensor_tensor", tmp3[a:a + 32, :],
                     tmp1[b:b + 32, :], scl[b:b + 32, 0:1], stab[b:b + 32, :],
                     op0=ALU.mult, op1=ALU.mult)
            P.selfedge("vector", ve)
            P.op("vector", "tensor_tensor", raw, tmp2[:], tmp3[:],
                 op=ALU.add, inc=[(ve, 1)])
        return ve.n

    def project_v(c0):
        if DEBUG_PROJ < 1:
            return
        slot = kslot(c0)
        if wq_load is not None:
            wq_load(2)
        for tt in range(4):
            tc = c0 + tt * 128
            ib, bank = pool.grab()
            for ki in range(KCH):
                P.op("tensor", "matmul", bank, xsh(ki, tc, 128),
                     wq_sb(ki, 1024, 512), start=(ki == 0),
                     stop=(ki == KCH - 1),
                     inc=([(mm, 1)] if ki == KCH - 1 else []))
            mmv = mm.n
            base = slot * 4 * 520 + tt * 520
            dstv = vtok[:, base:base + 520].rearrange(
                "p (h d) -> p h d", d=65)[:, :, 0:64]
            P.wait("vector", mm, mmv)
            P.op("vector", "tensor_copy", dstv, bank, inc=[(ve, 1)])
            pool.release(ib, ve, ve.n)

    def attention(c0):
        wpar = ((c0 - QLO) // 512) % 2
        for qt in range(4):
            qc = c0 + 128 * qt
            for par in range(2):
                pool.gate(2 * par)
                pool.gate(2 * par + 1)
            for hp in range(4):
                for kh in range(2):
                    khc = c0 - 512 + 512 * kh
                    slot = kslot(khc)
                    for par in range(2):
                        wide = pool.wide(par)
                        P.op("tensor", "matmul", wide[:, 512 * kh:512 * kh + 512],
                             qn[64 * par:64 * par + 64,
                                hp * 512 + 128 * qt: hp * 512 + 128 * qt + 128],
                             kn[64 * par:64 * par + 64,
                                slot * 2048 + hp * 512: slot * 2048 + hp * 512 + 512],
                             start=True, stop=True, tile_position=(64 * par, 0),
                             inc=[(mm, 1)])
                mm_simA = mm.n
                P.wait("vector", mm, mm_simA)
                for par in range(2):
                    wide = pool.wide(par)
                    P.op("vector", "tensor_tensor",
                         wide[:, 512 + 128 * qt:512 + 128 * qt + 128],
                         wide[:, 512 + 128 * qt:512 + 128 * qt + 128],
                         consts["triuA"], op=ALU.add)
                P.selfedge("vector", ve)
                for par in range(2):
                    wide = pool.wide(par)
                    h = 2 * hp + par
                    P.op("vector", "reduce_max", mf32[:, h:h + 1],
                         wide[:, 0:512 + 128 * (qt + 1)],
                         axis=AX, inc=[(ve, 1)])
                ve_max = ve.n
                for par in range(2):
                    pool.release(2 * par, ve, ve_max)
                    pool.release(2 * par + 1, ve, ve_max)
                if hp < 3:
                    for par in range(2):
                        pool.gate(2 * par)
                        pool.gate(2 * par + 1)
            ve_maxall = ve.n
            P.wait("scalar", ve, ve_maxall)
            P.op("scalar", "activation", mbuf[:], mf32[:], AF.Copy, scale=-1.0,
                 inc=[(se, 1)])
            se_negm = se.n
            pool.gate(6)
            P.wait("tensor", se, se_negm)
            for h in range(8):
                P.op("tensor", "matmul", pool.view(6)[0:1, 0:128],
                     mbuf[:, h:h + 1], consts["identbf"], start=True, stop=True,
                     inc=[(mm, 1)])
                mmv = mm.n
                P.wait("vector", mm, mmv)
                P.op("vector", "tensor_copy", mrows[:, 128 * h:128 * h + 128],
                     pool.view(6)[0:1, 0:128], inc=[(ve, 1)])
                P.wait("tensor", ve, ve.n)
            pool.release(6, ve, ve.n)

            kts = [kt for kt in range(8) if c0 - 512 + 128 * kt <= qc]
            first_kt = kts[0]
            ve_rec = None
            for hpass in range(2):
                heads = [4 * hpass + i for i in range(4)]
                se_exp_prev = None
                mm_av_prev = None
                for kt in kts:
                    ktc = c0 - 512 + 128 * kt
                    slot = kslot(ktc)
                    j0 = (ktc - (c0 - 512)) % 512
                    if se_exp_prev is not None:
                        P.wait("tensor", se, se_exp_prev)
                    else:
                        pool.gate(0)
                        pool.gate(1)
                        if hpass == 0:
                            pool.gate(2)
                            pool.gate(3)
                        if kt == first_kt:
                            for i in range(4):
                                pool.gate(4 + i)
                    for i, h in enumerate(heads):
                        par = h % 2
                        hp = h // 2
                        sv = pool.wide(par)[:, 128 * (i // 2):128 * (i // 2) + 128]
                        P.op("tensor", "matmul", sv,
                             kn[64 * par:64 * par + 64,
                                slot * 2048 + hp * 512 + j0:
                                slot * 2048 + hp * 512 + j0 + 128],
                             qn[64 * par:64 * par + 64,
                                hp * 512 + 128 * qt: hp * 512 + 128 * qt + 128],
                             start=True, stop=False, tile_position=(64 * par, 0))
                        P.op("tensor", "matmul", sv,
                             consts["ones_row"][0:1, 0:128],
                             mrows[0:1, 128 * h:128 * h + 128], start=False,
                             stop=(ktc >= xoff or kmask_bf is None))
                        if ktc < xoff and kmask_bf is not None:
                            P.op("tensor", "matmul", sv,
                                 kmask_bf[0:1, ktc:ktc + 128],
                                 consts["ones_row"][0:1, 0:128], start=False,
                                 stop=True)
                    P.mark("tensor", mm)
                    mm_simB = mm.n
                    if ktc == qc:
                        P.wait("vector", mm, mm_simB)
                        for i in range(4):
                            sv = pool.wide(i % 2)[:, 128 * (i // 2):
                                                  128 * (i // 2) + 128]
                            P.op("vector", "tensor_tensor", sv, sv,
                                 consts["triuB"], op=ALU.add)
                        P.mark("vector", ve)
                        ve_diag = ve.n
                        P.wait("scalar", ve, ve_diag)
                    P.wait("scalar", mm, mm_simB)
                    if mm_av_prev is not None:
                        P.wait("scalar", mm, mm_av_prev)
                    P.op("scalar", "activation", probsT[:, 0:256],
                         pool.wide(0)[:, 0:256], AF.Exp)
                    P.op("scalar", "activation", probsT[:, 256:512],
                         pool.wide(1)[:, 0:256], AF.Exp, inc=[(se, 1)])
                    se_exp_prev = se.n
                    P.wait("tensor", se, se_exp_prev)
                    for i, h in enumerate(heads):
                        ov = pool.view(4 + i)[0:64, 0:128]
                        P.op("tensor", "matmul", ov,
                             vtok[:, slot * 4 * 520 + (j0 // 128) * 520 + 65 * h:
                                  slot * 4 * 520 + (j0 // 128) * 520 + 65 * h + 64],
                             probsT[:, 256 * (i % 2) + 128 * (i // 2):
                                    256 * (i % 2) + 128 * (i // 2) + 128],
                             start=(kt == first_kt), stop=(kt == kts[-1]))
                    P.op("tensor", "matmul", pool.wide(0)[0:1, 512:1024],
                         consts["ones_col"], probsT[:, 0:512],
                         start=(kt == first_kt), stop=(kt == kts[-1]))
                    P.mark("tensor", mm)
                    mm_av_prev = mm.n
                mm_av = mm_av_prev
                P.wait("vector", mm, mm_av)
                P.op("vector", "tensor_scalar", sumsr[:, 512 * hpass:512 * hpass + 512],
                     pool.wide(0)[0:1, 512:1024], 1e-30, None, op0=ALU.add)
                P.selfedge("vector", ve)
                P.op("vector", "reciprocal", recipr[:, 512 * hpass:512 * hpass + 512],
                     sumsr[:, 512 * hpass:512 * hpass + 512], inc=[(ve, 1)],
                     lowprec=True)
                ve_rec = ve.n
                P.wait("tensor", ve, ve_rec)
                for i, h in enumerate(heads):
                    P.op("tensor", "matmul", pool.view(3)[0:64, 0:128],
                         consts["ones_row"][0:1, 0:64],
                         recipr[0:1, 512 * hpass + 256 * (i % 2) + 128 * (i // 2):
                                512 * hpass + 256 * (i % 2) + 128 * (i // 2) + 128],
                         start=True, stop=True, inc=[(mm, 1)])
                    mmv = mm.n
                    P.wait("vector", mm, mmv)
                    P.op("vector", "tensor_copy", bcn[0:64, :],
                         pool.view(3)[0:64, 0:128])
                    P.selfedge("vector", ve)
                    P.op("vector", "tensor_tensor",
                         o_fm[64 * (h % 2):64 * (h % 2) + 64,
                              128 * (h // 2):128 * (h // 2) + 128],
                         pool.view(4 + i)[0:64, 0:128],
                         bcn[0:64, :], op=ALU.mult, inc=[(ve, 1)])
                    P.wait("tensor", ve, ve.n)
                ve_ofm = ve.n
                for i in range(8):
                    pool.release(i, ve, ve_ofm)
            for oc in range(KCH):
                ib, bank = pool.grab()
                for ki in range(4):
                    P.op("tensor", "matmul", bank[0:128, 0:128],
                         wts["wo"](ki, oc * 128, 128),
                         o_fm[:, 128 * ki:128 * ki + 128],
                         start=(ki == 0), stop=(ki == 3),
                         inc=([(mm, 1)] if ki == 3 else []))
                mmv = mm.n
                P.wait("vector", mm, mmv)
                dst = xa[:, wpar * KCH * 520 + oc * 520 + 8 + 128 * qt:
                         wpar * KCH * 520 + oc * 520 + 8 + 128 * qt + 128]
                P.op("vector", "tensor_tensor", dst, bank[0:128, 0:128],
                     xin(oc, qc, 128), op=ALU.add, inc=[(ve, 1)])
                pool.release(ib, ve, ve.n)
        return ve.n

    def ffn(c0, wi):
        wpar = ((c0 - QLO) // 512) % 2

        def xa_dat(ki, off=0, nw=512):
            return xa[:, wpar * KCH * 520 + ki * 520 + 8 + off:
                      wpar * KCH * 520 + ki * 520 + 8 + off + nw]

        def xt(ki):
            if ki < 4:
                return xa_dat(ki)
            return xa[:, wpar * KCH * 520 + ki * 520 + 7:
                      wpar * KCH * 520 + ki * 520 + 7 + 512]

        def ot(ki):
            return hln[:, ki * 512:(ki + 1) * 512]

        if state["store"] is not None:
            P.wait("vector", *state["store"])
            P.wait("scalar", *state["store"])
        emit_ln_win(P, pool, lnb, lnst, xt, ot, 512, consts, sems,
                    gain=wts["lg"], bias=wts["lb"])
        ve_hln = ve.n
        mm_hc = {}
        mm_w2 = {}
        groups = [(0, 6), (6, 12), (12, 18), (18, 22)]
        for gi, (h0, h1) in enumerate(groups):
            nhc = h1 - h0
            for j in range(nhc):
                hc = h0 + j
                p = hc % 2
                if hc >= 2:
                    P.wait("sync", mm, mm_hc[hc - 2])
                P.op("sync", "dma_start", out=w1st[:, p * 2048:p * 2048 + 1024],
                     in_=wts["w1scr"][hc], inc=[(wst2[p], 16)])
                P.op("sync", "dma_start",
                     out=w1st[:, p * 2048 + 1024:p * 2048 + 2048],
                     in_=wts["w1scr"][22 + hc], inc=[(wst2[p], 16)])
                wv = wst2[p].n
                ia, banka = pool.grab()
                ig, bankg = pool.grab()
                P.wait("tensor", wst2[p], wv)
                if hc == 0:
                    P.wait("tensor", ve, ve_hln)
                for ki in range(KCH):
                    P.op("tensor", "matmul", banka,
                         w1st[:, p * 2048 + ki * 128:p * 2048 + ki * 128 + 128],
                         hln[:, ki * 512:(ki + 1) * 512], start=(ki == 0),
                         stop=(ki == KCH - 1),
                         inc=([(mm, 1)] if ki == KCH - 1 else []))
                for ki in range(KCH):
                    P.op("tensor", "matmul", bankg,
                         w1st[:, p * 2048 + 1024 + ki * 128:
                              p * 2048 + 1024 + ki * 128 + 128],
                         hln[:, ki * 512:(ki + 1) * 512], start=(ki == 0),
                         stop=(ki == KCH - 1),
                         inc=([(mm, 1)] if ki == KCH - 1 else []))
                mm_hc[hc] = mm.n
                P.wait("scalar", mm, mm_hc[hc])
                P.wait("scalar", ve, ve.n)
                P.op("scalar", "activation", gts[:], bankg, AF.Gelu,
                     bias=wts["b1g"](hc), inc=[(se, 1)])
                se_g = se.n
                P.wait("vector", se, se_g)
                P.op("vector", "scalar_tensor_tensor",
                     ffnh[:, j * 512:(j + 1) * 512], banka, wts["b1a"](hc),
                     gts[:], op0=ALU.add, op1=ALU.mult, inc=[(ve, 1)])
                pool.release(ia, ve, ve.n)
                pool.release(ig, ve, ve.n)
            ve_ffnh = ve.n
            for i in range(8):
                pool.gate(i)
            for j in range(nhc):
                hc = h0 + j
                if hc >= 1:
                    P.wait("sync", mm, mm_w2.get(hc - 1, 0))
                P.op("sync", "dma_start", out=w2st[:, 0:1024],
                     in_=wts["w2scr"][hc], inc=[(wst2[2], 16)])
                wv = wst2[2].n
                P.wait("tensor", wst2[2], wv)
                if j == 0:
                    P.wait("tensor", ve, ve_ffnh)
                for oc in range(8):
                    P.op("tensor", "matmul", pool.view(oc),
                         w2st[:, oc * 128:oc * 128 + 128],
                         ffnh[:, j * 512:(j + 1) * 512], start=(j == 0),
                         stop=(j == nhc - 1))
                P.mark("tensor", mm)
                mm_w2[hc] = mm.n
            mm_w2f = mm.n
            P.wait("vector", mm, mm_w2f)
            for oc in range(8):
                if gi == 0:
                    P.op("vector", "scalar_tensor_tensor",
                         xo_stage[:, oc * 512:(oc + 1) * 512], pool.view(oc),
                         wts["b2"](oc), xa_dat(oc), op0=ALU.add, op1=ALU.add,
                         inc=[(ve, 1)])
                else:
                    P.op("vector", "tensor_tensor",
                         xo_stage[:, oc * 512:(oc + 1) * 512], pool.view(oc),
                         xo_stage[:, oc * 512:(oc + 1) * 512], op=ALU.add,
                         inc=[(ve, 1)])
            ve_out = ve.n
            for i in range(8):
                pool.release(i, ve, ve_out)
        state["store"] = out_cb(wi, xo_stage, ve_out)

    if DEBUG_PHASE < 1:
        return
    if xin_load is not None:
        xin_load(KLO)
    project_v(KLO)
    project_qk(KLO, False)
    for wi in range(NW):
        c0 = QLO + 512 * wi
        if xin_load is not None:
            xin_load(c0)
        project_v(c0)
        project_qk(c0, False)
        project_qk(c0, True)
        barrier(P, sems)
        if DEBUG_PHASE < 2:
            continue
        attention(c0)
        P.selfedge("vector", ve)
        wpar = ((c0 - QLO) // 512) % 2
        for ki in range(KCH):
            P.op("vector", "tensor_copy",
                 xa[:, (1 - wpar) * KCH * 520 + ki * 520:
                    (1 - wpar) * KCH * 520 + ki * 520 + 8],
                 xa[:, wpar * KCH * 520 + ki * 520 + 512:
                    wpar * KCH * 520 + ki * 520 + 520], inc=[(ve, 1)])
        P.selfedge("vector", ve)
        barrier(P, sems)
        if DEBUG_PHASE < 3 or wi < OUT0:
            continue
        ffn(c0, wi - OUT0)
        barrier(P, sems)


def build_launch_B(l, fx=None):
    if fx is None:
        nc = bass.Bass("TRN2", target_bir_lowering=False)
        dram = lambda name, shape, dt, kind="Internal": nc.dram_tensor(
            name, shape, dt, kind=kind).ap()
    else:
        nc = fx["nc"]
        dram = lambda name, shape, dt, kind="Internal": fx["io"][name]
    okind = "ExternalOutput" if fx is None else "Internal"
    if l == 0:
        NIN, XOFF = 3600, 1544
        QLO = XOFF - 1024
    else:
        NIN, XOFF = 3072, 1024
        QLO = XOFF - 512
    QHI = XOFF + G
    Tq = QHI - QLO
    Tk = QHI - (QLO - 512)

    if l == 0:
        xh = dram("xh2", [C, NIN], F32, "ExternalInput")
        xout_h = dram("xout0", [C, 3072], BF16, okind)
    else:
        xh = dram("xin0", [C, NIN], BF16, "ExternalInput")
        af0_h = dram("af0", [C, G], F32, okind)
        f0scr = dram("f0scr", [C, G], BF16)
        rwscr = dram("rwscr", [C, G], BF16)
        wkvl_h = dram("wkvl", [C, G], F32, "ExternalInput")
        wkl_h = dram("wkl", [C, G], F32, "ExternalInput")
        rbf_h = dram("rbf", [C, G], BF16, "ExternalInput")
        carrH_h = dram("carrH", [C, 8], F32, "ExternalInput")
        tdP_h = dram("tdP", [128, KCH], F32, "ExternalInput")
        tfP_h = dram("tfP", [128, KCH], F32, "ExternalInput")
        iota_h = dram("iota2d", [128, G], F32, "ExternalInput")
        woTr_h = dram("woTr", [C, C], F32, "ExternalInput")

    wqkv_h = dram("wqkv", [C, 1536], F32, "ExternalInput")
    wo_h = dram("wo", [512, C], F32, "ExternalInput")
    w1lm_h = dram("w1lm", [44, 128, 1024], F32, "ExternalInput")
    w2lm_h = dram("w2lm", [22, 128, 1024], F32, "ExternalInput")
    b1aP_h = dram("b1aP", [128, 22], F32, "ExternalInput")
    b1gP_h = dram("b1gP", [128, 22], F32, "ExternalInput")
    b2P_h = dram("b2P", [128, KCH], F32, "ExternalInput")
    lgP_h = dram("lgP", [128, KCH], F32, "ExternalInput")
    lbP_h = dram("lbP", [128, KCH], F32, "ExternalInput")
    scal4_h = dram("scal4", [128, 4], F32, "ExternalInput")
    cq_h = dram("cq", [128, Tq], F32, "ExternalInput")
    sq_h = dram("sq", [128, Tq], F32, "ExternalInput")
    ck_h = dram("ck", [128, Tk], F32, "ExternalInput")
    sk_h = dram("sk", [128, Tk], F32, "ExternalInput")
    identbf_h = dram("identbf", [128, 128], BF16, "ExternalInput")
    blockones_h = dram("blockones", [128, 128], BF16, "ExternalInput")
    triuA_h = dram("triuA", [128, 128], F32, "ExternalInput")
    triuB_h = dram("triuB", [128, 128], F32, "ExternalInput")
    KML = (XOFF // 128) * 128 + 128
    kmask_h = dram("kmask", [1, KML], BF16, "ExternalInput")
    w1scr = dram("w1scr", [44, 128, 1024], BF16)
    w2scr = dram("w2scr", [22, 128, 1024], BF16)

    with ExitStack() as stack:
        if fx is None:
            P = Prog(nc, stack)
            rr = R(nc, stack)
        else:
            P, rr = fx["P"], fx["rr"]
            rr.push(f"B{l}_")
        ld = P.sem("ld")
        lc = P.sem("lc")
        wl = P.sem("wl")
        pc_in = P.sem("pci")
        pc_out = P.sem("pco")
        mm = P.sem("mm")
        ve = P.sem("ve")
        se = P.sem("se")
        st = P.sem("st")
        tabsem = P.sem("tab")
        wst2 = (P.sem("wsta"), P.sem("wstb"), P.sem("wstc"))
        sems = {"mm": mm, "ve": ve, "se": se, "tab": tabsem, "wst2": wst2}

        if fx is None:
            consts = make_consts(P, rr, ve)
        else:
            consts = dict(fx["consts"])
        pool = PsumPool(P, rr, "pp")
        xo_stage = rr.sb("xo_stage", (128, KCH * 512), BF16)
        lnb = make_ln_bufs(rr, "lnb", x2=xo_stage[:, :])

        cst = rr.sb("cstb", (128, 2 * 128), BF16)
        cstf = rr.sb("cstf", (128, 2 * 128), F32)
        kmask_sb = rr.sb("kmask_sb", (1, KML), BF16)
        P.op("sync", "dma_start", out=cst[:, 0:128], in_=identbf_h, inc=[(lc, 16)])
        P.op("sync", "dma_start", out=cst[:, 128:256], in_=blockones_h, inc=[(lc, 16)])
        P.op("sync", "dma_start", out=cstf[:, 0:128], in_=triuA_h, inc=[(lc, 16)])
        P.op("sync", "dma_start", out=cstf[:, 128:256], in_=triuB_h, inc=[(lc, 16)])
        P.op("sync", "dma_start", out=kmask_sb[:], in_=kmask_h, inc=[(lc, 16)])

        pars = rr.sb("parsB", (128, 22 * 2 + KCH * 3 + 4 + (2 * KCH if l else 0)), F32)
        p_b1a = pars[:, 0:22]
        p_b1g = pars[:, 22:44]
        p_b2 = pars[:, 44:52]
        p_lg = pars[:, 52:60]
        p_lb = pars[:, 60:68]
        p_sc = pars[:, 68:72]
        P.op("sync", "dma_start", out=p_b1a, in_=b1aP_h, inc=[(lc, 16)])
        P.op("sync", "dma_start", out=p_b1g, in_=b1gP_h, inc=[(lc, 16)])
        P.op("sync", "dma_start", out=p_b2, in_=b2P_h, inc=[(lc, 16)])
        P.op("sync", "dma_start", out=p_lg, in_=lgP_h, inc=[(lc, 16)])
        P.op("sync", "dma_start", out=p_lb, in_=lbP_h, inc=[(lc, 16)])
        P.op("sync", "dma_start", out=p_sc, in_=scal4_h, inc=[(lc, 16)])
        if l == 1:
            p_td = pars[:, 72:80]
            p_tf = pars[:, 80:88]
            P.op("sync", "dma_start", out=p_td, in_=tdP_h, inc=[(lc, 16)])
            P.op("sync", "dma_start", out=p_tf, in_=tfP_h, inc=[(lc, 16)])
        lc_all = lc.n
        for e in ("tensor", "vector", "scalar"):
            P.wait(e, lc, lc_all)
        consts["identbf"] = cst[:, 0:128]
        consts["blockones"] = cst[:, 128:256]
        consts["triuA"] = cstf[:, 0:128]
        consts["triuB"] = cstf[:, 128:256]

        wo_sb = rr.sb("wo_sb", (128, 4 * 1024), BF16)
        for ki in range(4):
            P.op("gpsimd", "dma_start", out=wo_sb[:, ki * 1024:(ki + 1) * 1024],
                 in_=wo_h[ki * 128:(ki + 1) * 128, :], inc=[(wl, 16)])
        wl_all = wl.n
        P.wait("tensor", wl, wl_all)

        wqst = rr.sb("wqst", (128, KCH * 512), BF16)
        wq_state = {"blk": -1}

        def wq_load(blk):
            if wq_state["blk"] == blk:
                return
            wq_state["blk"] = blk
            P.wait("gpsimd", mm, mm.n)
            P.wait("gpsimd", wl, wl.n)
            for ki in range(KCH):
                P.op("gpsimd", "dma_start", out=wqst[:, ki * 512:(ki + 1) * 512],
                     in_=wqkv_h[ki * 128:(ki + 1) * 128,
                                512 * blk:512 * blk + 512], inc=[(wl, 16)])
            for e in ("tensor",):
                P.wait(e, wl, wl.n)

        def wq_sb(ki, m0, mr):
            blk = m0 // 512
            assert (m0 + mr - 1) // 512 == blk
            return wqst[:, ki * 512 + m0 - 512 * blk: ki * 512 + m0 - 512 * blk + mr]

        def wo_sl(ki, m0, mr):
            return wo_sb[:, ki * 1024 + m0: ki * 1024 + m0 + mr]

        pstage = rr.sb("pstage", (128, 1024), BF16)
        npc = 0
        for (src, dst, n) in ((w1lm_h, w1scr, 44), (w2lm_h, w2scr, 22)):
            for i in range(n):
                if npc >= 1:
                    P.wait("gpsimd", pc_out, 16 * npc)
                P.op("gpsimd", "dma_start", out=pstage[:],
                     in_=src[i], inc=[(pc_in, 16)])
                P.wait("scalar", pc_in, pc_in.n)
                P.op("scalar", "dma_start", out=dst[i],
                     in_=pstage[:], inc=[(pc_out, 16)])
                npc += 1
        pc_done = pc_out.n
        P.wait("sync", pc_out, pc_done)

        if l == 0:
            xb = rr.sb("xb2", (128, KCH * NIN), BF16)
            for ki in range(KCH):
                P.op("gpsimd", "dma_start", out=xb[:, ki * NIN:(ki + 1) * NIN],
                     in_=xh[ki * 128:(ki + 1) * 128, :], inc=[(ld, 16)])
            ld_x = ld.n
            emit_ln(P, rr, pool, "ln0", xb, NIN, consts, sems, ready=(ld, ld_x))
            xin = fm_tiles(xb, NIN)
            xin_load = None
        else:
            xinr = rr.sb("xinr", (128, 2 * KCH * 520), BF16)
            P.op("vector", "memset", xinr[:], 0.0, inc=[(ve, 1)])
            xl = P.sem("xl")
            def xin_load(c0):
                w = c0 // 512
                slot = w % 2
                P.wait("sync", ve, ve.n)
                for ki in range(KCH):
                    lo = max(0, c0 - 8)
                    P.op("sync", "dma_start",
                         out=xinr[:, slot * KCH * 520 + ki * 520 + (8 - (c0 - lo)):
                                  slot * KCH * 520 + (ki + 1) * 520],
                         in_=xh[ki * 128:(ki + 1) * 128, lo:c0 + 512],
                         inc=[(xl, 16)])
                P.wait("tensor", xl, xl.n)
                P.wait("vector", xl, xl.n)
                P.wait("scalar", xl, xl.n)

            def xin(ki, c, nw):
                wstar = (c + nw - 1) // 512
                slot = wstar % 2
                off = 8 + c - 512 * wstar
                assert off >= 0, (c, nw)
                return xinr[:, slot * KCH * 520 + ki * 520 + off:
                            slot * KCH * 520 + ki * 520 + off + nw]

        wts = {
            "w1scr": [w1scr[i] for i in range(44)],
            "w2scr": [w2scr[i] for i in range(22)],
            "b1a": lambda hc: p_b1a[:, hc:hc + 1],
            "b1g": lambda hc: p_b1g[:, hc:hc + 1],
            "b2": lambda oc: p_b2[:, oc:oc + 1],
            "lg": lambda ki: p_lg[:, ki:ki + 1],
            "lb": lambda ki: p_lb[:, ki:ki + 1],
            "qs": p_sc[:, 0:1],
            "qss": p_sc[:, 1:2],
            "ks": p_sc[:, 2:3],
            "kss": p_sc[:, 3:4],
            "wo": wo_sl,
        }
        tabs = {"cq": cq_h, "sq": sq_h, "ck": ck_h, "sk": sk_h}

        tgt_h = xout_h if l == 0 else f0scr
        NTG = 3072 if l == 0 else G
        tgt3 = tgt_h.rearrange("(k p) n -> p k n", p=128)

        def out_cb(wi, stage, vev):
            P.wait("gpsimd", ve, vev)
            P.wait("gpsimd", st, st.n)
            P.op("gpsimd", "dma_start",
                 out=tgt3[:, :, 512 * wi:512 * wi + 512],
                 in_=stage[:].rearrange("p (k n) -> p k n", n=512),
                 inc=[(st, 16)])
            return (st, st.n)

        emit_ffn_inner(P, rr, pool, "il", xin=xin, QLO=QLO, QHI=QHI,
                       wq_sb=wq_sb, tabs=tabs, wts=wts, consts=consts,
                       sems=sems, out_cb=out_cb, kmask_bf=kmask_sb[:],
                       xoff=XOFF, lnb=lnb, wq_load=wq_load,
                       xo_stage=xo_stage[:, :], OUT0=(0 if l == 0 else 1),
                       xin_load=(None if l == 0 else xin_load))

        if l == 1:
            emit_mixer_post(P, rr, pool, sems, consts,
                            wkvl_h=wkvl_h, wkl_h=wkl_h, rbf_h=rbf_h,
                            carrH_h=carrH_h, p_td=p_td, p_tf=p_tf,
                            iota_h=iota_h, woTr_h=woTr_h, f0scr=f0scr,
                            rwscr=rwscr, af0_h=af0_h, st=st)

        P.wait("gpsimd", st, st.n)
        assert rr.sb_bytes <= 222 * 1024 * 128, rr.sb_bytes
        if fx is None:
            P.build()
        else:
            quiesce(P, sems)
            rr.pop()
    return nc


def emit_mixer_post(P, rr, pool, sems, consts, *, wkvl_h, wkl_h, rbf_h, carrH_h,
                    p_td, p_tf, iota_h, woTr_h, f0scr, rwscr, af0_h, st):
    mm, ve, se = sems["mm"], sems["ve"], sems["se"]
    barrier(P, sems)
    lm = P.sem("lm")
    lg2 = P.sem("lg2")
    GH = G // 2
    wkv_t = rr.sb("mp_wkv", (128, GH), F32)
    wk_t = rr.sb("mp_wk", (128, GH), F32)
    fac_t = rr.sb("mp_fac", (128, GH), F32)
    rec_t = rr.sb("mp_rec", (128, GH), F32)
    sg_t = rr.sb("mp_sg", (128, GH), BF16)
    rb_t = rr.sb("mp_rb", (128, GH), BF16)
    rw_t = rr.sb("mp_rw", (128, GH), BF16)
    iota_t = rr.sb("mp_iota", (128, GH), F32)
    carr_t = rr.sb("mp_carr", (128, 8 * KCH), F32)
    chp = rr.sb("mp_chp", (128, 4 * KCH), F32)
    pw = rr.sb("mp_pw", (128, 4), F32)
    woT_sb = rr.sb("mp_woT", (128, KCH * 1024), BF16)
    for ki in range(KCH):
        P.op("gpsimd", "dma_start", out=woT_sb[:, ki * 1024:(ki + 1) * 1024],
             in_=woTr_h[ki * 128:(ki + 1) * 128, :], inc=[(lg2, 16)])
    lmc = P.sem("lmc")
    for ci in range(KCH):
        P.op("sync", "dma_start", out=carr_t[:, 8 * ci:8 * ci + 8],
             in_=carrH_h[128 * ci:128 * ci + 128, :], inc=[(lmc, 16)])
    lm_all = lmc.n
    lg_all = lg2.n
    st_prev = None
    for ci in range(KCH):
        cs = 128 * ci
        P.wait("scalar", lmc, lm_all)
        P.op("scalar", "activation", chp[:, 4 * ci:4 * ci + 1],
             p_td[:, ci:ci + 1], AF.Exp, inc=[(se, 1)])
        P.selfedge("scalar", se)
        for mI in range(1, 4):
            P.op("scalar", "activation", pw[:, mI:mI + 1],
                 chp[:, 4 * ci:4 * ci + 1], AF.Exp, scale=-float(G * mI),
                 inc=[(se, 1)])
        P.op("scalar", "activation", pw[:, 0:1],
             chp[:, 4 * ci:4 * ci + 1], AF.Copy, scale=-1.0, inc=[(se, 1)])
        se_pw = se.n
        P.wait("vector", se, se_pw)
        P.wait("vector", lmc, lm_all)
        for sc in range(2):
            dst = chp[:, 4 * ci + 1 + sc:4 * ci + 2 + sc]
            P.op("vector", "tensor_copy", dst,
                 carr_t[:, 8 * ci + sc:8 * ci + sc + 1])
            for mI in range(1, 4):
                P.selfedge("vector", ve)
                P.op("vector", "scalar_tensor_tensor", dst,
                     carr_t[:, 8 * ci + 2 * mI + sc:8 * ci + 2 * mI + sc + 1],
                     pw[:, mI:mI + 1], dst, op0=ALU.mult, op1=ALU.add)
            P.selfedge("vector", ve)
        ve_cin = ve.n
        for hf in range(2):
            h0 = hf * GH
            if st_prev is not None:
                P.wait("sync", ve, st_prev)
            P.op("sync", "dma_start", out=wkv_t[:],
                 in_=wkvl_h[cs:cs + 128, h0:h0 + GH], inc=[(lm, 16)])
            P.op("sync", "dma_start", out=wk_t[:],
                 in_=wkl_h[cs:cs + 128, h0:h0 + GH], inc=[(lm, 16)])
            P.op("sync", "dma_start", out=rb_t[:],
                 in_=rbf_h[cs:cs + 128, h0:h0 + GH], inc=[(lm, 16)])
            P.op("sync", "dma_start", out=iota_t[:],
                 in_=iota_h[:, h0:h0 + GH], inc=[(lm, 16)])
            lm_c = lm.n
            P.wait("scalar", ve, ve_cin)
            P.wait("scalar", lm, lm_c)
            if st_prev is not None:
                P.wait("scalar", ve, st_prev)
            P.op("scalar", "activation", fac_t[:], iota_t[:], AF.Exp,
                 scale=pw[:, 0:1], inc=[(se, 1)])
            P.op("scalar", "activation", sg_t[:], rb_t[:], AF.Sigmoid,
                 inc=[(se, 1)])
            se_fac = se.n
            P.wait("vector", se, se_fac)
            P.wait("vector", lm, lm_c)
            P.op("vector", "scalar_tensor_tensor", wkv_t[:], fac_t[:],
                 chp[:, 4 * ci + 1:4 * ci + 2], wkv_t[:], op0=ALU.mult,
                 op1=ALU.add)
            P.op("vector", "scalar_tensor_tensor", wk_t[:], fac_t[:],
                 chp[:, 4 * ci + 2:4 * ci + 3], wk_t[:], op0=ALU.mult,
                 op1=ALU.add)
            P.selfedge("vector", ve)
            P.op("vector", "reciprocal", rec_t[:], wk_t[:])
            P.selfedge("vector", ve)
            P.op("vector", "tensor_tensor", wkv_t[:], wkv_t[:], rec_t[:],
                 op=ALU.mult)
            P.selfedge("vector", ve)
            P.wait("vector", st, st.n)
            P.op("vector", "tensor_tensor", rw_t[:], wkv_t[:], sg_t[:],
                 op=ALU.mult, inc=[(ve, 1)])
            ve_rw = ve.n
            st_prev = ve_rw
            P.wait("gpsimd", ve, ve_rw)
            P.op("gpsimd", "dma_start", out=rwscr[cs:cs + 128, h0:h0 + GH],
                 in_=rw_t[:], inc=[(st, 16)])
    st_rw = st.n
    rwst = rr.sb("mp_rwst", (128, KCH * 512), BF16)
    f0st = rr.sb("mp_f0st", (128, 2 * 512), BF16)
    afst = rr.sb("mp_afst", (128, 2 * 512), F32)
    lrw = P.sem("lrw")
    lrf2 = (P.sem("lrf0"), P.sem("lrf1"))
    P.wait("sync", st, st_rw)
    P.wait("tensor", lg2, lg_all)
    af_done = None
    for win in range(G // 512):
        P.wait("sync", mm, mm.n)
        for ki in range(KCH):
            P.op("sync", "dma_start", out=rwst[:, ki * 512:(ki + 1) * 512],
                 in_=rwscr[ki * 128:(ki + 1) * 128, 512 * win:512 * win + 512],
                 inc=[(lrw, 16)])
        lr_w = lrw.n
        P.wait("tensor", lrw, lr_w)
        for oc in range(KCH):
            pp = (win * KCH + oc) % 2
            if af_done is not None and (win * KCH + oc) >= 2:
                P.wait("sync", st, af_done.get((win * KCH + oc) - 2, 0))
            P.op("sync", "dma_start", out=f0st[:, pp * 512:(pp + 1) * 512],
                 in_=f0scr[oc * 128:(oc + 1) * 128, 512 * win:512 * win + 512],
                 inc=[(lrf2[pp], 16)])
            lr_c = lrf2[pp].n
            ib, bank = pool.grab()
            for ki in range(KCH):
                P.op("tensor", "matmul", bank,
                     woT_sb[:, ki * 1024 + oc * 128: ki * 1024 + oc * 128 + 128],
                     rwst[:, ki * 512:(ki + 1) * 512], start=(ki == 0),
                     stop=(ki == KCH - 1),
                     inc=([(mm, 1)] if ki == KCH - 1 else []))
            mmv = mm.n
            P.wait("vector", mm, mmv)
            P.wait("vector", lrf2[pp], lr_c)
            P.op("vector", "tensor_tensor", afst[:, pp * 512:(pp + 1) * 512],
                 bank, f0st[:, pp * 512:(pp + 1) * 512], op=ALU.add,
                 inc=[(ve, 1)])
            pool.release(ib, ve, ve.n)
            P.wait("gpsimd", ve, ve.n)
            P.wait("gpsimd", st, st.n)
            P.op("gpsimd", "dma_start",
                 out=af0_h[oc * 128:(oc + 1) * 128, 512 * win:512 * win + 512],
                 in_=afst[:, pp * 512:(pp + 1) * 512], inc=[(st, 16)])
            if af_done is None:
                af_done = {}
            af_done[win * KCH + oc] = st.n


DEBUG_A = False
DEBUG_NW = None
DEBUG_PHASE = 3
DEBUG_PROJ = 3


def build_launch_A(fx=None):
    NA = G + 8
    if fx is None:
        nc = bass.Bass("TRN2", target_bir_lowering=False)
        dram = lambda name, shape, dt, kind: nc.dram_tensor(
            name, shape, dt, kind=kind).ap()
    else:
        nc = fx["nc"]
        dram = lambda name, shape, dt, kind: fx["io"][name]
    xh = dram("xh", [C, NA], F32, "ExternalInput")
    wkT = dram("wkT", [C, C], F32, "ExternalInput")
    wvT = dram("wvT", [C, C], F32, "ExternalInput")
    wrT = dram("wrT", [C, C], F32, "ExternalInput")
    tmP = dram("tmP", [128, KCH], F32, "ExternalInput")
    tdP = dram("tdP", [128, KCH], F32, "ExternalInput")
    tfP = dram("tfP", [128, KCH], F32, "ExternalInput")
    if DEBUG_A:
        xn_dbg = dram("xn_dbg", [C, NA], BF16, "ExternalOutput")
        xm_dbg = dram("xm_dbg", [C, G], BF16, "ExternalOutput")
        k_dbg = dram("k_dbg", [128, G], F32, "ExternalOutput")
        kv_dbg = dram("kv_dbg", [128, G], F32, "ExternalOutput")
    okind = "ExternalOutput" if fx is None else "Internal"
    wkvl_h = dram("wkvl", [C, G], F32, okind)
    wkl_h = dram("wkl", [C, G], F32, okind)
    rbf_h = dram("rbf", [C, G], BF16, okind)
    carr_h = dram("carr", [C, 2], F32, okind)

    with ExitStack() as stack:
        if fx is None:
            P = Prog(nc, stack)
            rr = R(nc, stack)
        else:
            P, rr = fx["P"], fx["rr"]
            rr.push("A_")
        ld = P.sem("ld")
        wl = P.sem("wl")
        mm = P.sem("mm")
        ve = P.sem("ve")
        se = P.sem("se")
        st = P.sem("st")
        sems = {"mm": mm, "ve": ve, "se": se}

        if fx is None:
            consts = make_consts(P, rr, ve)
        else:
            consts = dict(fx["consts"])
        pr = PsumRing(P, rr, "pr", 6)

        xb = rr.sb("xb", (128, KCH * NA), BF16)
        for ki in range(KCH):
            P.op("gpsimd", "dma_start", out=xb[:, ki * NA:(ki + 1) * NA],
                 in_=xh[ki * 128:(ki + 1) * 128, :], inc=[(ld, 16)])
        ld_x = ld.n
        lp = P.sem("lp")
        pars = rr.sb("pars", (128, 3 * KCH), F32)
        P.op("sync", "dma_start", out=pars[:, 0:KCH], in_=tmP, inc=[(lp, 16)])
        P.op("sync", "dma_start", out=pars[:, KCH:2 * KCH], in_=tdP, inc=[(lp, 16)])
        P.op("sync", "dma_start", out=pars[:, 2 * KCH:3 * KCH], in_=tfP, inc=[(lp, 16)])
        ld_pars = lp.n
        wk_w, _ = load_weight_fm(P, rr, "wk", wkT, C, C, wl)
        wv_w, _ = load_weight_fm(P, rr, "wv", wvT, C, C, wl)
        wr_w, _ = load_weight_fm(P, rr, "wr", wrT, C, C, wl)
        wl_all = wl.n

        emit_ln(P, rr, pr, "ln0", xb, NA, consts, sems, ready=(ld, ld_x))
        ve_ln = ve.n

        xm = rr.sb("xm", (128, KCH * G), BF16)
        dsc = rr.sb("dsc", (128, G), BF16)
        P.wait("vector", lp, ld_pars)
        P.wait("vector", ve, ve_ln)
        for ki in range(KCH):
            xnk = xb[:, ki * NA: (ki + 1) * NA]
            if ki:
                P.wait("vector", ve, ve.n)
            P.op("vector", "tensor_tensor", dsc[:], xnk[:, 1:G + 1], xnk[:, 0:G],
                 op=ALU.subtract)
            P.selfedge("vector", ve)
            P.op("vector", "scalar_tensor_tensor", xm[:, ki * G:(ki + 1) * G],
                 dsc[:], pars[:, ki:ki + 1], xnk[:, 0:G], op0=ALU.mult, op1=ALU.add,
                 inc=[(ve, 1)])
        ve_xm = ve.n
        xmt = fm_tiles(xm, G)

        k_sb = rr.sb("k_sb", (128, G), F32)
        kv_sb = rr.sb("kv_sb", (128, G), F32)
        s1 = rr.sb("s1", (128, G + 1), F32)
        s2 = rr.sb("s2", (128, G + 1), F32)
        wkv_o = rr.sb("wkv_o", (128, G), F32)
        wk_o = rr.sb("wk_o", (128, G), F32)
        r_o = rr.sb("r_o", (128, G), BF16)
        chp = rr.sb("chp", (128, 3 * KCH), F32)

        for ci in range(KCH):
            col = lambda j, _c=ci: chp[:, 3 * _c + j:3 * _c + j + 1]
            P.wait("scalar", lp, ld_pars)
            P.op("scalar", "activation", col(0), pars[:, KCH + ci:KCH + ci + 1], AF.Exp)
            P.op("scalar", "activation", col(2), pars[:, 2 * KCH + ci:2 * KCH + ci + 1],
                 AF.Exp, inc=[(se, 1)])
            se_p1 = se.n
            P.wait("scalar", se, se_p1)
            P.op("scalar", "activation", col(1), col(0), AF.Exp, scale=-1.0,
                 inc=[(se, 1)])
            se_par = se.n

            def evac_k(m0, mr, n0, nw, ps, mmv):
                P.wait("vector", mm, mmv)
                P.op("vector", "tensor_scalar", k_sb[:, n0:n0 + nw], ps, 60.0, None,
                     op0=ALU.min, inc=[(ve, 1)])
                return (ve, ve.n)

            mm_proj(P, pr, wt=wk_w, xt=xmt, kch=KCH, Mtot=C, N=G, mm_sem=mm,
                    evac=evac_k, mlist=[ci * 128],
                    wt_ready=(wl, wl_all), xt_ready=(ve, ve_xm))
            ve_kmin = ve.n
            P.wait("scalar", ve, ve_kmin)
            P.op("scalar", "activation", k_sb[:], k_sb[:], AF.Exp, inc=[(se, 1)])
            se_k = se.n

            def evac_v(m0, mr, n0, nw, ps, mmv):
                P.wait("vector", mm, mmv)
                P.wait("vector", se, se_k)
                P.op("vector", "tensor_tensor", kv_sb[:, n0:n0 + nw], ps,
                     k_sb[:, n0:n0 + nw], op=ALU.mult, inc=[(ve, 1)])
                return (ve, ve.n)

            mm_proj(P, pr, wt=wv_w, xt=xmt, kch=KCH, Mtot=C, N=G, mm_sem=mm,
                    evac=evac_v, mlist=[ci * 128])
            ve_kv = ve.n

            def evac_r(m0, mr, n0, nw, ps, mmv):
                P.wait("scalar", mm, mmv)
                P.op("scalar", "activation", r_o[:, n0:n0 + nw], ps, AF.Copy,
                     inc=[(se, 1)])
                return (se, se.n)

            mm_proj(P, pr, wt=wr_w, xt=xmt, kch=KCH, Mtot=C, N=G, mm_sem=mm,
                    evac=evac_r, mlist=[ci * 128])
            se_r = se.n

            P.op("vector", "memset", s1[:, 0:1], 0.0)
            P.op("vector", "memset", s2[:, 0:1], 0.0)
            P.wait("vector", se, se_par)
            P.wait("vector", ve, ve_kv)
            P.op("vector", "tensor_tensor_scan", s1[:, 1:G + 1],
                 col(1).broadcast_to((128, G)), kv_sb[:], 0.0,
                 op0=ALU.mult, op1=ALU.add, inc=[(ve, 1)])
            P.op("vector", "tensor_tensor_scan", s2[:, 1:G + 1],
                 col(1).broadcast_to((128, G)), k_sb[:], 0.0,
                 op0=ALU.mult, op1=ALU.add, inc=[(ve, 1)])
            ve_scan = ve.n
            P.wait("vector", ve, ve_scan)
            P.op("vector", "scalar_tensor_tensor", wkv_o[:], kv_sb[:], col(2),
                 s1[:, 0:G], op0=ALU.mult, op1=ALU.add, inc=[(ve, 1)])
            P.op("vector", "scalar_tensor_tensor", wk_o[:], k_sb[:], col(2),
                 s2[:, 0:G], op0=ALU.mult, op1=ALU.add)
            P.selfedge("vector", ve)
            P.op("vector", "tensor_scalar", wk_o[:], wk_o[:], 1e-16, None,
                 op0=ALU.add, inc=[(ve, 1)])
            ve_out = ve.n

            P.wait("gpsimd", ve, ve_out)
            P.wait("gpsimd", se, se_r)
            cs = ci * 128
            if DEBUG_A and ci == 0:
                P.op("gpsimd", "dma_start", out=k_dbg, in_=k_sb[:], inc=[(st, 16)])
                P.op("gpsimd", "dma_start", out=kv_dbg, in_=kv_sb[:], inc=[(st, 16)])
                for kj in range(KCH):
                    P.op("gpsimd", "dma_start",
                         out=xn_dbg[kj * 128:(kj + 1) * 128, :],
                         in_=xb[:, kj * NA:(kj + 1) * NA], inc=[(st, 16)])
                    P.op("gpsimd", "dma_start",
                         out=xm_dbg[kj * 128:(kj + 1) * 128, :],
                         in_=xm[:, kj * G:(kj + 1) * G], inc=[(st, 16)])
            P.op("gpsimd", "dma_start", out=wkvl_h[cs:cs + 128, :], in_=wkv_o[:],
                 inc=[(st, 16)])
            P.op("gpsimd", "dma_start", out=wkl_h[cs:cs + 128, :], in_=wk_o[:],
                 inc=[(st, 16)])
            P.op("gpsimd", "dma_start", out=rbf_h[cs:cs + 128, :], in_=r_o[:],
                 inc=[(st, 16)])
            P.op("gpsimd", "dma_start", out=carr_h[cs:cs + 128, 0:1],
                 in_=s1[:, G:G + 1], inc=[(st, 16)],
                 allow_slow_non_contiguous=True)
            P.op("gpsimd", "dma_start", out=carr_h[cs:cs + 128, 1:2],
                 in_=s2[:, G:G + 1], inc=[(st, 16)],
                 allow_slow_non_contiguous=True)
            P.wait("vector", st, st.n)
            P.wait("scalar", st, st.n)

        P.wait("gpsimd", st, st.n)
        if fx is None:
            P.build()
        else:
            quiesce(P, sems)
            rr.pop()
    return nc


def build_launch_D(fx=None):
    TT_ = T
    QLO = TT_ - G
    if fx is None:
        nc = bass.Bass("TRN2", target_bir_lowering=False)
        dram = lambda name, shape, dt, kind="Internal": nc.dram_tensor(
            name, shape, dt, kind=kind).ap()
    else:
        nc = fx["nc"]
        dram = lambda name, shape, dt, kind="Internal": fx["io"][name]
    xT_h = dram("xTf", [C, TT_], F32, "ExternalInput")
    afT_h = dram("afT", [C, TT_], F32, "ExternalInput")
    wq_h = [dram(f"wq{v}", [2048, 512], F32, "ExternalInput")
            for v in range(4)]
    wk_h = [dram(f"wk{v}", [1024, 512], F32, "ExternalInput")
            for v in range(4)]
    wvT_h = dram("wvT1", [C, C], F32, "ExternalInput")
    bv_h = dram("bv1", [1, C], F32, "ExternalInput")
    woT_h = dram("woT1", [C, C], F32, "ExternalInput")
    boP_h = dram("boP1", [128, KCH], F32, "ExternalInput")
    sel128_h = dram("sel128", [128, 128], BF16, "ExternalInput")
    wflm_h = dram("wflm", [64, 128, 1024], F32, "ExternalInput")
    bf_h = dram("bf1", [8, C], F32, "ExternalInput")
    sigsT_h = dram("sigsT", [32, 8], F32, "ExternalInput")
    peT_h = dram("peT", [32, G], F32, "ExternalInput")
    identf_h = dram("identf", [128, 128], F32, "ExternalInput")
    ident8_h = dram("ident8", [8, 8], BF16, "ExternalInput")
    triuD_h = dram("triuD", [128, 128], F32, "ExternalInput")
    kmask_h = dram("kmaskD", [1, TT_], BF16, "ExternalInput")
    ones1r_h = dram("ones1r", [1, 4 * G], BF16, "ExternalInput")
    out_h = dram("outT", [C, G], F32, "ExternalOutput")
    xn1scr = dram("xn1scr", [C, TT_], BF16)
    jk4scr = dram("jk4scr", [4, 128, TT_], BF16)
    vscr = dram("vscrD", [TT_, 1040], BF16)
    w8scr = dram("w8scr", [64, 128, 1024], BF16)

    with ExitStack() as stack:
        if fx is None:
            P = Prog(nc, stack)
            rr = R(nc, stack)
        else:
            P, rr = fx["P"], fx["rr"]
            rr.push("D_")
        mm = P.sem("mm")
        ve = P.sem("ve")
        se = P.sem("se")
        st = P.sem("st")
        lx = P.sem("lx")
        la = P.sem("la")
        lc = P.sem("lc")
        wl = P.sem("wl")
        ws2 = (P.sem("ws0"), P.sem("ws1"))
        kv2 = (P.sem("kv0"), P.sem("kv1"))
        xr2 = (P.sem("xr0"), P.sem("xr1"))
        pc_in = P.sem("pci")
        pc_out = P.sem("pco")
        sems = {"mm": mm, "ve": ve, "se": se}
        if fx is None:
            consts = make_consts(P, rr, ve)
        else:
            consts = dict(fx["consts"])
        pool = PsumPool(P, rr, "pp")
        lnb = make_ln_bufs(rr, "lnd")
        lnst = LNState()

        xw = rr.sb("xw", (128, KCH * 512), F32)
        xaf = rr.sb("xaf", (128, KCH * 512), F32)
        xnw = rr.sb("xnw", (128, KCH * 520), BF16)
        wstage = rr.sb("wstage", (128, 16 * 512), BF16)
        lsq = rr.sb("lsq", (128, 512), BF16)
        linb = rr.sb("linb", (128, 512), BF16)
        rstdb = rr.sb("rstdD", (128, 512), BF16)
        vw = rr.sb("vw", (128, 4 * 1040), BF16)
        probsT2 = vw[:, 0:2048]
        lnf = xaf[:, 512:1024]
        q_pl = rr.sb("q_pl", (128, 4 * G), BF16)
        jkst = rr.sb("jkst", (128, 2 * 512), BF16)
        vst = rr.sb("vst", (128, 2 * 4 * 1040), BF16)
        probsT = rr.sb("probsTD", (128, 4 * 512), BF16)
        o_fm = rr.sb("o_fmD", (128, 16 * 1024), BF16)
        kvar = o_fm
        xbw = o_fm[:, 8192:8192 + KCH * 512]
        acc = rr.sb("accD", (128, KCH * G), BF16)
        wvo = acc
        sumsr = rr.sb("sumsD", (1, 512), F32)
        recipr = rr.sb("recipD", (1, 512), BF16)
        bcn = rr.sb("bcnD", (128, 512), BF16)
        peT = rr.sb("peTs", (32, 128), F32)
        signsT = rr.sb("signsT", (32, 8), F32)
        onehot = rr.sb("onehot", (128, 8), F32)
        onehotT = rr.sb("onehotT", (8, G), BF16)
        ohrow = rr.sb("ohrow", (1, 512), BF16)
        obc = rr.sb("obc", (128, 512), BF16)
        cstD = rr.sb("cstD", (128, 2 * 128 + 8), BF16)
        cstDf = rr.sb("cstDf", (128, 2 * 128), F32)
        b8 = rr.sb("b8", (8, C), BF16)
        sigs = rr.sb("sigs", (32, 8), F32)
        mwin = rr.sb("mwin", (128, 8), F32)

        P.op("sync", "dma_start", out=cstD[:, 0:128], in_=sel128_h, inc=[(lc, 16)])
        P.op("sync", "dma_start", out=cstD[0:8, 128:136], in_=ident8_h, inc=[(lc, 16)])
        P.op("sync", "dma_start", out=cstDf[:, 0:128], in_=identf_h, inc=[(lc, 16)])
        P.op("sync", "dma_start", out=cstDf[:, 128:256], in_=triuD_h, inc=[(lc, 16)])


        P.op("sync", "dma_start", out=sigs[:], in_=sigsT_h, inc=[(lc, 16)])
        P.op("gpsimd", "dma_start", out=b8[:], in_=bf_h, inc=[(wl, 16)])
        boP = rr.sb("boPD", (128, KCH), F32)
        bvr = rr.sb("bvrD", (1, C), BF16)
        P.op("sync", "dma_start", out=boP[:], in_=boP_h, inc=[(lc, 16)])
        P.op("gpsimd", "dma_start", out=bvr[:], in_=bv_h, inc=[(wl, 16)])
        for ki in range(KCH):
            P.op("gpsimd", "dma_start", out=wvo[:, ki * 1024:(ki + 1) * 1024],
                 in_=wvT_h[ki * 128:(ki + 1) * 128, :], inc=[(wl, 16)])
        lc_all = lc.n
        wl_all = wl.n
        for e in ("tensor", "vector", "scalar"):
            P.wait(e, lc, lc_all)
            P.wait(e, wl, wl_all)

        pst = wstage[:, 0:1024]
        for i in range(64):
            if i >= 1:
                P.wait("gpsimd", pc_out, 16 * i)
            P.op("gpsimd", "dma_start", out=pst, in_=wflm_h[i], inc=[(pc_in, 16)])
            P.wait("scalar", pc_in, pc_in.n)
            P.op("scalar", "dma_start", out=w8scr[i], in_=pst, inc=[(pc_out, 16)])
        pc_done = pc_out.n
        P.wait("sync", pc_out, pc_done)

        P.op("vector", "memset", vw[:], 1.0)
        P.op("vector", "memset", xnw[:], 0.0)
        P.op("vector", "memset", vst[:], 1.0, inc=[(ve, 1)])
        P.selfedge("vector", ve)

        def stream_wvar(src_h, nk, parity):
            wsx = ws2[parity]
            P.wait("gpsimd", mm, mm.n)
            P.wait("gpsimd", wsx, wsx.n)
            P.op("gpsimd", "dma_start",
                 out=wstage[:, 0:nk * 512].rearrange("p (k n) -> p k n", n=512),
                 in_=src_h.rearrange("(k p) n -> p k n", p=128),
                 inc=[(wsx, 16)])
            P.wait("tensor", wsx, wsx.n)

        def plucker(dst, dstoff, w_hs, nk, xt_fn, nw, tabcols):
            for v in range(4):
                stream_wvar(w_hs[v], nk, v % 2)
                for m in range(4):
                    ib, bank = pool.grab()
                    for ki in range(nk):
                        P.op("tensor", "matmul", bank[0:128, 0:nw],
                             wstage[:, ki * 512 + m * 128: ki * 512 + m * 128 + 128],
                             xt_fn(ki), start=(ki == 0), stop=(ki == nk - 1),
                             inc=([(mm, 1)] if ki == nk - 1 else []))
                    mmv = mm.n
                    P.wait("vector", mm, mmv)
                    P.op("vector", "tensor_copy",
                         kvar[:, (v * 4 + m) * 512:(v * 4 + m) * 512 + nw],
                         bank[0:128, 0:nw], inc=[(ve, 1)])
                    pool.release(ib, ve, ve.n)
            P.selfedge("vector", ve)
            for m in range(4):
                t1i = kvar[:, (0 * 4 + m) * 512:(0 * 4 + m) * 512 + nw]
                t1j = kvar[:, (1 * 4 + m) * 512:(1 * 4 + m) * 512 + nw]
                t2i = kvar[:, (2 * 4 + m) * 512:(2 * 4 + m) * 512 + nw]
                t2j = kvar[:, (3 * 4 + m) * 512:(3 * 4 + m) * 512 + nw]
                P.op("vector", "tensor_tensor", lnf[:, 0:nw], t1i, t2j, op=ALU.mult)
                P.op("vector", "tensor_tensor", xaf[:, 0:nw], t1j, t2i,
                     op=ALU.mult)
                P.selfedge("vector", ve)
                P.op("vector", "tensor_tensor", lnf[:, 0:nw], lnf[:, 0:nw],
                     xaf[:, 0:nw], op=ALU.subtract)
                P.selfedge("vector", ve)
                P.op("vector", "tensor_copy", linb[:, 0:nw], lnf[:, 0:nw],
                     inc=[(ve, 1)])
                ve_lin = ve.n
                P.wait("scalar", ve, ve_lin)
                P.op("scalar", "activation", lsq[:, 0:nw], lnf[:, 0:nw], AF.Square,
                     inc=[(se, 1)])
                se_sq = se.n
                ib, bank = pool.grab()
                P.wait("tensor", se, se_sq)
                P.op("tensor", "matmul", bank[0:128, 0:nw], cstD[:, 0:128],
                     lsq[:, 0:nw], start=True, stop=True, inc=[(mm, 1)])
                mm_n2 = mm.n
                P.wait("scalar", mm, mm_n2)
                P.wait("scalar", ve, ve.n)
                P.op("scalar", "activation", rstdb[:, 0:nw], bank[0:128, 0:nw],
                     AF.Ln, bias=consts["cvals"][:, 1:2])
                P.selfedge("scalar", se)
                P.op("scalar", "activation", rstdb[:, 0:nw], rstdb[:, 0:nw],
                     AF.Exp, scale=-0.5, inc=[(se, 1)])
                se_r = se.n
                pool.release(ib, se, se_r)
                P.wait("vector", se, se_r)
                P.op("vector", "tensor_tensor", dst[:, dstoff(m):dstoff(m) + nw],
                     linb[:, 0:nw], rstdb[:, 0:nw], op=ALU.mult, inc=[(ve, 1)])
            P.selfedge("vector", ve)

        st_xn = None
        for win in range(TT_ // 512):
            c0 = 512 * win
            P.wait("sync", ve, ve.n)
            P.wait("gpsimd", ve, ve.n)
            for ki in range(KCH):
                P.op("sync", "dma_start", out=xw[:, ki * 512:(ki + 1) * 512],
                     in_=xT_h[ki * 128:(ki + 1) * 128, c0:c0 + 512],
                     inc=[(lx, 16)])
                P.op("gpsimd", "dma_start", out=xaf[:, ki * 512:(ki + 1) * 512],
                     in_=afT_h[ki * 128:(ki + 1) * 128, c0:c0 + 512],
                     inc=[(la, 16)])
            lx_w = lx.n
            la_w = la.n
            P.wait("vector", lx, lx_w)
            P.wait("vector", la, la_w)
            for ki in range(KCH):
                P.op("vector", "tensor_tensor", xw[:, ki * 512:(ki + 1) * 512],
                     xw[:, ki * 512:(ki + 1) * 512],
                     xaf[:, ki * 512:(ki + 1) * 512], op=ALU.add, inc=[(ve, 1)])
            P.selfedge("vector", ve)
            for ki in range(KCH):
                P.op("vector", "tensor_copy", xbw[:, ki * 512:(ki + 1) * 512],
                     xw[:, ki * 512:(ki + 1) * 512], inc=[(ve, 1)])
            P.selfedge("vector", ve)
            ve_x1 = ve.n
            P.wait("vector", mm, mm.n)
            emit_ln_win(P, pool, lnb, lnst,
                        lambda ki: xbw[:, ki * 512:(ki + 1) * 512],
                        lambda ki: xnw[:, ki * 520 + 8:(ki + 1) * 520], 512,
                        consts, sems, ready=(ve, ve_x1))
            ve_ln = ve.n
            P.wait("gpsimd", ve, ve_ln)
            for ki in range(KCH):
                P.op("gpsimd", "dma_start",
                     out=xn1scr[ki * 128:(ki + 1) * 128, c0:c0 + 512],
                     in_=xnw[:, ki * 520 + 8:(ki + 1) * 520], inc=[(st, 16)])
            st_xn = st.n
            P.wait("tensor", ve, ve_ln)
            for tt in range(4):
                for half in range(2):
                    ib, bank = pool.grab()
                    for ki in range(KCH):
                        P.op("tensor", "matmul", bank,
                             xnw[:, ki * 520 + 8 + 128 * tt:
                                 ki * 520 + 8 + 128 * tt + 128],
                             wvo[:, ki * 1024 + half * 512:
                                 ki * 1024 + half * 512 + 512],
                             start=(ki == 0), stop=False)
                    P.op("tensor", "matmul", bank, consts["ones_row"][0:1, 0:128],
                         bvr[0:1, half * 512:half * 512 + 512], start=False,
                         stop=True, inc=[(mm, 1)])
                    mmv = mm.n
                    dstv = vw[:, tt * 1040:tt * 1040 + 1040].rearrange(
                        "p (h d) -> p h d", d=65)[:, 8 * half:8 * half + 8, 0:64]
                    P.wait("vector", mm, mmv)
                    P.op("vector", "tensor_copy", dstv,
                         bank.rearrange("p (h d) -> p h d", d=64),
                         inc=[(ve, 1)])
                    pool.release(ib, ve, ve.n)
            P.selfedge("vector", ve)
            ve_v = ve.n
            P.wait("gpsimd", ve, ve_v)
            for tt in range(4):
                P.op("gpsimd", "dma_start",
                     out=vscr[c0 + 128 * tt:c0 + 128 * tt + 128, :],
                     in_=vw[:, tt * 1040:(tt + 1) * 1040], inc=[(st, 16)])
            def xk(ki, _c0=c0):
                if ki < 4:
                    return xnw[:, ki * 520 + 8:(ki + 1) * 520]
                return xnw[:, ki * 520 + 7:ki * 520 + 519]
            plucker(probsT, lambda m: m * 512, wk_h, KCH, xk, 512, None)
            ve_jk = ve.n
            P.wait("gpsimd", ve, ve_jk)
            for m in range(4):
                P.op("gpsimd", "dma_start",
                     out=jk4scr[m][:, c0:c0 + 512],
                     in_=probsT[:, m * 512:(m + 1) * 512], inc=[(st, 16)])
            P.wait("vector", mm, mm.n)
            for ki in range(KCH):
                P.op("vector", "tensor_copy", xnw[:, ki * 520:ki * 520 + 8],
                     xnw[:, ki * 520 + 512:ki * 520 + 520], inc=[(ve, 1)])
            P.selfedge("vector", ve)
            P.wait("vector", st, st.n)
        barrier(P, sems)

        def load_xn_win(c0, parity, halo=8):
            xr = xr2[parity]
            P.wait("sync", mm, mm.n)
            P.wait("sync", ve, ve.n)
            for ki in range(KCH):
                P.op("sync", "dma_start", out=xnw[:, ki * 520 + 8 - halo:
                                                 (ki + 1) * 520],
                     in_=xn1scr[ki * 128:(ki + 1) * 128, c0 - halo:c0 + 512],
                     inc=[(xr, 16)])
            for e in ("tensor", "vector", "scalar"):
                P.wait(e, xr, xr.n)

        P.wait("sync", st, st.n)
        for wi in range(4):
            c0 = QLO + 512 * wi
            load_xn_win(c0, wi % 2)

            def xq2(ki):
                if ki < 8:
                    return xnw[:, (ki % 8) * 520 + 8:(ki % 8) * 520 + 520]
                return xnw[:, (ki - 8) * 520 + 7:(ki - 8) * 520 + 519]
            plucker(q_pl, lambda m, _w=wi: m * G + 512 * _w, wq_h, 16, xq2, 512,
                    None)
        barrier(P, sems)

        P.wait("sync", ve, ve.n)
        P.wait("sync", st, st.n)
        for m in range(4):
            for j in range(4):
                P.op("sync", "dma_start", out=jk4scr[m][32 * j + 6:32 * j + 7, :],
                     in_=kmask_h, inc=[(lc, 16)])
        for j in range(4):
            P.op("sync", "dma_start", out=q_pl[32 * j + 6:32 * j + 7, :],
                 in_=ones1r_h, inc=[(lc, 16)])
        P.wait("sync", lc, lc.n)
        for e in ("tensor",):
            P.wait(e, lc, lc.n)

        NQW = G // 128
        NQW2 = G // 512
        for t in range(4):
            for qw in range(NQW2):
                qc = QLO + 512 * qw
                nks = (qc + 512) // 512
                kv_par = 0
                se_e = [None, None]
                mm_pv = None
                par_free = [mm.n, mm.n]
                pt = 0
                for i in range(8):
                    pool.gate(i)
                for ks in range(nks):
                    kc = 512 * ks
                    kvs = kv2[kv_par]
                    P.wait("sync", mm, par_free[kv_par])
                    P.op("sync", "dma_start",
                         out=jkst[:, kv_par * 512:kv_par * 512 + 512],
                         in_=jk4scr[t][:, kc:kc + 512], inc=[(kvs, 16)])
                    P.op("sync", "dma_start",
                         out=vst[:, kv_par * 4160:(kv_par + 1) * 4160]
                         .rearrange("p (k n) -> p k n", n=1040),
                         in_=vscr[kc:kc + 512, :]
                         .rearrange("(k p) n -> p k n", p=128), inc=[(kvs, 16)])
                    P.wait("tensor", kvs, kvs.n)
                    for kt in range(4):
                        for pair in range(2):
                            if se_e[pair] is not None:
                                P.wait("tensor", se, se_e[pair])
                            for j in (2 * pair, 2 * pair + 1):
                                P.op("tensor", "matmul", pool.view(j),
                                     jkst[32 * j:32 * j + 7,
                                          kv_par * 512 + 128 * kt:
                                          kv_par * 512 + 128 * kt + 128],
                                     q_pl[32 * j:32 * j + 7,
                                          t * G + 512 * qw:
                                          t * G + 512 * qw + 512],
                                     start=True, stop=True,
                                     tile_position=(32 * j, 0))
                            P.mark("tensor", mm)
                        mm_s = [mm.n - 1, mm.n]
                        if ks == nks - 1:
                            P.wait("vector", mm, mm.n)
                            for j in range(4):
                                sv = pool.view(j)
                                if kt:
                                    P.op("vector", "tensor_scalar",
                                         sv[:, 0:128 * kt], sv[:, 0:128 * kt],
                                         -1e9, None, op0=ALU.add)
                                P.op("vector", "tensor_tensor",
                                     sv[:, 128 * kt:128 * kt + 128],
                                     sv[:, 128 * kt:128 * kt + 128],
                                     cstDf[:, 128:256], op=ALU.add)
                            P.mark("vector", ve)
                            P.wait("scalar", ve, ve.n)
                        pbuf = probsT if pt == 0 else probsT2
                        for pair in range(2):
                            P.wait("scalar", mm, mm_s[pair])
                            P.op("scalar", "activation",
                                 pbuf[:, 1024 * pair:1024 * pair + 1024],
                                 pool.wide(pair), AF.Exp, inc=[(se, 1)])
                            se_e[pair] = se.n
                        for pair in range(2):
                            P.wait("tensor", se, se_e[pair])
                            for j in (2 * pair, 2 * pair + 1):
                                h = 4 * t + j
                                P.op("tensor", "matmul",
                                     pool.view(4 + j)[0:65, 0:512],
                                     vst[:, kv_par * 4160 + kt * 1040 + 65 * h:
                                         kv_par * 4160 + kt * 1040 + 65 * h + 65],
                                     pbuf[:, 512 * j:512 * j + 512],
                                     start=(ks == 0 and kt == 0),
                                     stop=(ks == nks - 1 and kt == 3))
                        P.mark("tensor", mm)
                        mm_pv = mm.n
                        pt ^= 1
                    par_free[kv_par] = mm.n
                    kv_par ^= 1
                P.wait("vector", mm, mm_pv)
                for pair in range(2):
                    P.wait("tensor", se, se_e[pair])
                for j in range(4):
                    h = 4 * t + j
                    P.op("vector", "tensor_scalar", sumsr[:],
                         pool.view(4 + j)[64:65, 0:512], 1e-30, None,
                         op0=ALU.add)
                    P.selfedge("vector", ve)
                    P.op("vector", "reciprocal", recipr[:], sumsr[:],
                         inc=[(ve, 1)], lowprec=True)
                    ve_rec = ve.n
                    P.wait("tensor", ve, ve_rec)
                    P.op("tensor", "matmul", pool.view(j)[0:64, 0:512],
                         consts["ones_row"][0:1, 0:64], recipr[0:1, 0:512],
                         start=True, stop=True, inc=[(mm, 1)])
                    mm_bc = mm.n
                    P.wait("vector", mm, mm_bc)
                    P.op("vector", "tensor_copy", bcn[0:64, 0:512],
                         pool.view(j)[0:64, 0:512])
                    P.selfedge("vector", ve)
                    for sseg in range(4):
                        P.op("vector", "tensor_tensor",
                             o_fm[64 * (h % 2):64 * (h % 2) + 64,
                                  (4 * qw + sseg) * 1024 + (h // 2) * 128:
                                  (4 * qw + sseg) * 1024 + (h // 2) * 128 + 128],
                             pool.view(4 + j)[0:64, 128 * sseg:128 * sseg + 128],
                             bcn[0:64, 128 * sseg:128 * sseg + 128],
                             op=ALU.mult, inc=[(ve, 1)])
                P.wait("tensor", ve, ve.n)
                for i in range(8):
                    pool.release(i, ve, ve.n)
        barrier(P, sems)

        P.wait("gpsimd", mm, mm.n)
        P.op("gpsimd", "dma_start",
             out=wstage[:].rearrange("p (k n) -> p k n", n=1024),
             in_=woT_h.rearrange("(k p) n -> p k n", p=128), inc=[(wl, 16)])
        P.wait("tensor", wl, wl.n)
        for qw in range(NQW):
            for m in range(KCH):
                ib, bank = pool.grab()
                for ki in range(KCH):
                    P.op("tensor", "matmul", bank[0:128, 0:128],
                         wstage[:, ki * 1024 + m * 128: ki * 1024 + m * 128 + 128],
                         o_fm[:, qw * 1024 + ki * 128: qw * 1024 + ki * 128 + 128],
                         start=(ki == 0), stop=(ki == KCH - 1),
                         inc=([(mm, 1)] if ki == KCH - 1 else []))
                mmv = mm.n
                P.wait("vector", mm, mmv)
                P.op("vector", "tensor_scalar",
                     acc[:, m * G + 128 * qw: m * G + 128 * qw + 128],
                     bank[0:128, 0:128], boP[:, m:m + 1], None, op0=ALU.add,
                     inc=[(ve, 1)])
                pool.release(ib, ve, ve.n)
        barrier(P, sems)

        P.op("scalar", "activation", signsT[:], sigs[:], AF.Sign, inc=[(se, 1)])
        se_sg = se.n
        P.wait("tensor", se, se_sg)
        pe_sem = P.sem("pe")
        for pt in range(G // 128):
            P.wait("sync", mm, mm.n)
            P.op("sync", "dma_start", out=peT[:],
                 in_=peT_h[:, 128 * pt:128 * pt + 128], inc=[(pe_sem, 16)])
            P.wait("tensor", pe_sem, pe_sem.n)
            ib, bank = pool.grab()
            P.op("tensor", "matmul", bank[0:128, 0:8],
                 peT[:], signsT[:], start=True,
                 stop=True, inc=[(mm, 1)])
            mmv = mm.n
            P.wait("vector", mm, mmv)
            P.op("vector", "reduce_max", mwin[:, 0:1], bank[0:128, 0:8], axis=AX,
                 inc=[(ve, 1)])
            P.selfedge("vector", ve)
            P.op("vector", "tensor_scalar", onehot[:], bank[0:128, 0:8],
                 mwin[:, 0:1], None, op0=ALU.is_ge, inc=[(ve, 1)])
            ve_oh = ve.n
            pool.release(ib, ve, ve_oh)
            ib2, bank2 = pool.grab()
            P.wait("tensor", ve, ve_oh)
            P.op("tensor", "matmul", bank2[0:8, 0:128], onehot[:],
                 cstDf[:, 0:128], start=True, stop=True, inc=[(mm, 1)])
            mmv2 = mm.n
            P.wait("vector", mm, mmv2)
            P.op("vector", "tensor_copy", onehotT[:, 128 * pt:128 * pt + 128],
                 bank2[0:8, 0:128], inc=[(ve, 1)])
            pool.release(ib2, ve, ve.n)
        P.selfedge("vector", ve)
        barrier(P, sems)

        for wi in range(4):
            for m in range(KCH):
                ib, bank = pool.grab()
                P.op("tensor", "matmul", bank,
                     b8[:, m * 128:(m + 1) * 128],
                     onehotT[:, 512 * wi:512 * wi + 512], start=True, stop=True,
                     inc=[(mm, 1)])
                mmv = mm.n
                P.wait("vector", mm, mmv)
                P.op("vector", "tensor_tensor",
                     acc[:, m * G + 512 * wi: m * G + 512 * wi + 512], bank,
                     acc[:, m * G + 512 * wi: m * G + 512 * wi + 512],
                     op=ALU.add, inc=[(ve, 1)])
                pool.release(ib, ve, ve.n)
        P.selfedge("vector", ve)
        for tl in range(8):
            wsx = ws2[tl % 2]
            P.wait("gpsimd", mm, mm.n)
            P.wait("gpsimd", wsx, wsx.n)
            P.op("gpsimd", "dma_start",
                 out=wstage[:].rearrange("p (k n) -> p k n", n=1024),
                 in_=w8scr[tl * 8:tl * 8 + 8].rearrange("k p n -> p k n"),
                 inc=[(wsx, 16)])
            P.wait("tensor", wsx, wsx.n)
            for wi in range(4):
                c0 = QLO + 512 * wi
                load_xn_win(c0, wi % 2, halo=0)
                ib, bank = pool.grab()
                P.op("tensor", "matmul", bank[0:1, 0:512],
                     cstD[0:8, 128 + tl:128 + tl + 1],
                     onehotT[:, 512 * wi:512 * wi + 512], start=True, stop=True,
                     inc=[(mm, 1)])
                mmv = mm.n
                P.wait("vector", mm, mmv)
                P.op("vector", "tensor_copy", ohrow[:], bank[0:1, 0:512],
                     inc=[(ve, 1)])
                pool.release(ib, ve, ve.n)
                ib2, bank2 = pool.grab()
                P.wait("tensor", ve, ve.n)
                P.op("tensor", "matmul", bank2, consts["ones_row"],
                     ohrow[:], start=True, stop=True, inc=[(mm, 1)])
                mmv2 = mm.n
                P.wait("vector", mm, mmv2)
                P.op("vector", "tensor_copy", obc[:], bank2, inc=[(ve, 1)])
                pool.release(ib2, ve, ve.n)
                P.selfedge("vector", ve)
                for ki in range(KCH):
                    P.op("vector", "tensor_tensor",
                         xw[:].bitcast(BF16)[:, ki * 512:(ki + 1) * 512],
                         xnw[:, ki * 520 + 8:(ki + 1) * 520], obc[:],
                         op=ALU.mult, inc=[(ve, 1)])
                P.selfedge("vector", ve)
                ve_xm = ve.n
                P.wait("tensor", ve, ve_xm)
                for m in range(KCH):
                    ib3, bank3 = pool.grab()
                    for ki in range(KCH):
                        P.op("tensor", "matmul", bank3,
                             wstage[:, ki * 1024 + m * 128:
                                    ki * 1024 + m * 128 + 128],
                             xw[:].bitcast(BF16)[:, ki * 512:(ki + 1) * 512],
                             start=(ki == 0), stop=(ki == KCH - 1),
                             inc=([(mm, 1)] if ki == KCH - 1 else []))
                    mmv3 = mm.n
                    P.wait("vector", mm, mmv3)
                    P.op("vector", "tensor_tensor",
                         acc[:, m * G + 512 * wi: m * G + 512 * wi + 512],
                         bank3,
                         acc[:, m * G + 512 * wi: m * G + 512 * wi + 512],
                         op=ALU.add, inc=[(ve, 1)])
                    pool.release(ib3, ve, ve.n)
        barrier(P, sems)

        for wi in range(4):
            c0 = QLO + 512 * wi
            P.wait("sync", ve, ve.n)
            P.wait("sync", st, st.n)
            P.wait("gpsimd", ve, ve.n)
            for ki in range(KCH):
                P.op("sync", "dma_start", out=xw[:, ki * 512:(ki + 1) * 512],
                     in_=xT_h[ki * 128:(ki + 1) * 128, c0:c0 + 512],
                     inc=[(lx, 16)])
                P.op("gpsimd", "dma_start", out=xaf[:, ki * 512:(ki + 1) * 512],
                     in_=afT_h[ki * 128:(ki + 1) * 128, c0:c0 + 512],
                     inc=[(la, 16)])
            P.wait("vector", lx, lx.n)
            P.wait("vector", la, la.n)
            for ki in range(KCH):
                P.op("vector", "tensor_tensor", xw[:, ki * 512:(ki + 1) * 512],
                     xw[:, ki * 512:(ki + 1) * 512],
                     xaf[:, ki * 512:(ki + 1) * 512], op=ALU.add)
            P.selfedge("vector", ve)
            for ki in range(KCH):
                P.op("vector", "tensor_tensor", xw[:, ki * 512:(ki + 1) * 512],
                     xw[:, ki * 512:(ki + 1) * 512],
                     acc[:, ki * G + 512 * wi: ki * G + 512 * wi + 512],
                     op=ALU.add, inc=[(ve, 1)])
            ve_out = ve.n
            P.wait("gpsimd", ve, ve_out)
            P.wait("gpsimd", st, st.n)
            P.op("gpsimd", "dma_start",
                 out=out_h.rearrange("(k p) n -> p k n", p=128)[
                     :, :, 512 * wi:512 * wi + 512],
                 in_=xw[:].rearrange("p (k n) -> p k n", n=512),
                 inc=[(st, 16)])
        P.wait("gpsimd", st, st.n)
        assert rr.sb_bytes <= 222 * 1024 * 128, rr.sb_bytes
        if fx is None:
            P.build()
        else:
            quiesce(P, sems)
            rr.pop()
    return nc


RG = [[0, 1, 2, 3], [4, 5, 6, 7]]


def _fused_io(nc):
    def mk(gname, shape, dt, kind="Internal"):
        return nc.dram_tensor(gname, list(shape), dt, kind=kind).ap()

    io = {"A": {}, "B0": {}, "B1": {}, "D": {}, "G": {}}
    A = io["A"]
    A["xh"] = mk("xh", [C, G + 8], F32, "ExternalInput")
    for n in ("wkT", "wvT", "wrT"):
        A[n] = mk(n, [C, C], F32, "ExternalInput")
    for n in ("tmP", "tdP", "tfP"):
        A[n] = mk(n, [128, KCH], F32, "ExternalInput")
    A["wkvl"] = mk("wkvl", [C, G], F32)
    A["wkl"] = mk("wkl", [C, G], F32)
    A["rbf"] = mk("rbf", [C, G], BF16)
    A["carr"] = mk("carr", [C, 2], F32)

    shared = {
        "identbf": mk("identbf", [128, 128], BF16, "ExternalInput"),
        "blockones": mk("blockones", [128, 128], BF16, "ExternalInput"),
        "triuA": mk("triuA", [128, 128], F32, "ExternalInput"),
        "triuB": mk("triuB", [128, 128], F32, "ExternalInput"),
    }
    w1scr = mk("w1scr", [44, 128, 1024], BF16)
    w2scr = mk("w2scr", [22, 128, 1024], BF16)
    for l in (0, 1):
        Bx = io[f"B{l}"]
        Bx.update(shared)
        Bx["w1scr"] = w1scr
        Bx["w2scr"] = w2scr
        pf = f"l{l}_"
        QLO2, XOFF = (520, 1544) if l == 0 else (512, 1024)
        QHI2 = XOFF + G
        Tq = QHI2 - QLO2
        Tk = QHI2 - (QLO2 - 512)
        KML = (XOFF // 128) * 128 + 128
        Bx["wqkv"] = mk(pf + "wqkv", [C, 1536], F32, "ExternalInput")
        Bx["wo"] = mk(pf + "wo", [512, C], F32, "ExternalInput")
        Bx["w1lm"] = mk(pf + "w1lm", [44, 128, 1024], F32, "ExternalInput")
        Bx["w2lm"] = mk(pf + "w2lm", [22, 128, 1024], F32, "ExternalInput")
        Bx["b1aP"] = mk(pf + "b1aP", [128, 22], F32, "ExternalInput")
        Bx["b1gP"] = mk(pf + "b1gP", [128, 22], F32, "ExternalInput")
        Bx["b2P"] = mk(pf + "b2P", [128, KCH], F32, "ExternalInput")
        Bx["lgP"] = mk(pf + "lgP", [128, KCH], F32, "ExternalInput")
        Bx["lbP"] = mk(pf + "lbP", [128, KCH], F32, "ExternalInput")
        Bx["scal4"] = mk(pf + "scal4", [128, 4], F32, "ExternalInput")
        Bx["cq"] = mk(pf + "cq", [128, Tq], F32, "ExternalInput")
        Bx["sq"] = mk(pf + "sq", [128, Tq], F32, "ExternalInput")
        Bx["ck"] = mk(pf + "ck", [128, Tk], F32, "ExternalInput")
        Bx["sk"] = mk(pf + "sk", [128, Tk], F32, "ExternalInput")
        Bx["kmask"] = mk(pf + "kmask", [1, KML], BF16, "ExternalInput")
    io["B0"]["xh2"] = mk("xh2", [C, 3600], F32, "ExternalInput")
    io["B0"]["xout0"] = mk("xout0", [C, 3072], BF16)
    B1 = io["B1"]
    B1["xin0"] = io["B0"]["xout0"]
    B1["af0"] = mk("af0", [C, G], F32)
    B1["f0scr"] = mk("f0scr", [C, G], BF16)
    B1["rwscr"] = mk("rwscr", [C, G], BF16)
    for n in ("wkvl", "wkl", "rbf", "tdP", "tfP"):
        B1[n] = A[n]
    B1["iota2d"] = mk("iota2d", [128, G], F32, "ExternalInput")
    B1["woTr"] = mk("woTr", [C, C], F32, "ExternalInput")
    carrX = mk("carrX", [C, 14], F32)
    B1["carrH"] = carrX[:, 6:14]

    D = io["D"]
    D["xTf"] = mk("xTf", [C, T], F32, "ExternalInput")
    afrot = mk("afrot", [C, 7 * G], F32)
    D["afT"] = afrot[:, 0:T]
    for v in range(4):
        D[f"wq{v}"] = mk(f"wq{v}", [2048, 512], F32, "ExternalInput")
        D[f"wk{v}"] = mk(f"wk{v}", [1024, 512], F32, "ExternalInput")
    D["wvT1"] = mk("wvT1", [C, C], F32, "ExternalInput")
    D["bv1"] = mk("bv1", [1, C], F32, "ExternalInput")
    D["woT1"] = mk("woT1", [C, C], F32, "ExternalInput")
    D["boP1"] = mk("boP1", [128, KCH], F32, "ExternalInput")
    D["sel128"] = mk("sel128", [128, 128], BF16, "ExternalInput")
    D["wflm"] = mk("wflm", [64, 128, 1024], F32, "ExternalInput")
    D["bf1"] = mk("bf1", [8, C], F32, "ExternalInput")
    D["sigsT"] = mk("sigsT", [32, 8], F32, "ExternalInput")
    D["peT"] = mk("peT", [32, G], F32, "ExternalInput")
    D["identf"] = mk("identf", [128, 128], F32, "ExternalInput")
    D["ident8"] = mk("ident8", [8, 8], BF16, "ExternalInput")
    D["triuD"] = mk("triuD", [128, 128], F32, "ExternalInput")
    D["kmaskD"] = mk("kmaskD", [1, T], BF16, "ExternalInput")
    D["ones1r"] = mk("ones1r", [1, 4 * G], BF16, "ExternalInput")
    D["outT"] = mk("outT", [C, G], F32, "ExternalOutput")
    D["xn1scr"] = mk("xn1scr", [C, T], BF16)
    D["jk4scr"] = mk("jk4scr", [4, 128, T], BF16)
    D["vscrD"] = mk("vscrD", [T, 1040], BF16)
    D["w8scr"] = mk("w8scr", [64, 128, 1024], BF16)

    Gd = io["G"]
    Gd["carrG"] = mk("carrG", [4 * C, 2], F32)
    Gd["carrX"] = carrX
    Gd["afG"] = mk("afG", [4 * C, G], F32)
    Gd["afrot"] = afrot
    Gd["zeroscr"] = mk("zeroscr", [C, 1024], BF16)
    return io


def _dyn(P, e, eng):
    et = {"sync": mybir.EngineType.SP, "scalar": mybir.EngineType.Activation,
          "gpsimd": mybir.EngineType.Pool}[eng]
    if not hasattr(P, "_dynst"):
        P._dynst = {}
    if eng not in P._dynst:
        pid = P.nc.partition_id((et,))
        jm = pid - (pid >= 4) * 4
        P._dynst[eng] = {"jm": jm, "e": e}
    return P._dynst[eng]


def dyn_dma(P, eng, sem, out_fn, in_ap):
    sem.bump(16)

    def fn(e, _of=out_fn, _i=in_ap, _s=sem, _eng=eng):
        d = _dyn(P, e, _eng)
        return e.dma_start(out=_of(e, d), in_=_i).then_inc(_s.h, 16)

    P.ops[eng].append(fn)
    P.kinds[eng].append("dma")
    P.meta[eng].append([(sem, sem.n)])


def _emit_glue1(P, rr, gio, A_io):
    cc = P.sem("cc1")
    gl = P.sem("gl1")
    gv = P.sem("gv1")
    glz = P.sem("glz")
    gio["glz"] = glz
    carrG, carrX = gio["carrG"], gio["carrX"]
    afrot = gio["afrot"]
    rr.push("G1_")
    zc = rr.sb("z", (128, 16), F32)
    zf = rr.sb("zf", (128, 2048), F32)
    P.op("vector", "memset", zc[:], 0.0, inc=[(gv, 1)])
    P.op("vector", "memset", zf[:], 0.0, inc=[(gv, 1)])
    P.wait("scalar", gv, gv.n)
    for b in range(4):
        for k in range(KCH):
            P.op("scalar", "dma_start",
                 out=afrot[k * 128:(k + 1) * 128, b * G:(b + 1) * G],
                 in_=zf[:], inc=[(glz, 16)])
    P.wait("sync", gv, gv.n)
    for k in range(KCH):
        P.op("sync", "dma_start", out=carrX[k * 128:(k + 1) * 128, :],
             in_=zc[:, 0:14], inc=[(gl, 16)])
    P.op("gpsimd", "collective_compute", "AllGather", ALU.bypass,
         replica_groups=RG, ins=[A_io["carr"].opt()], outs=[carrG.opt()],
         inc=[(cc, 1)])
    P.wait("sync", cc, cc.n)
    P.wait("sync", gl, gl.n)
    for p in range(3):
        dyn_dma(P, "sync", gl,
                lambda e, d, _p=p: carrX[:, bass.ds(
                    _cv(e, d, "carb", lambda jm: (jm + 2) * 2) - _p * 2, 2)],
                carrG[p * C:(p + 1) * C, :])
    for e in ENGINES:
        P.wait(e, gl, gl.n)
        P.wait(e, cc, cc.n)
        P.wait(e, glz, glz.n)
    rr.pop()


def _cv(e, d, key, expr):
    if key not in d:
        d[key] = e.compute_val(expr(d["jm"]))
    return d[key]


def _emit_glue2(P, rr, gio, B0_io):
    gl = P.sem("gl2")
    gv = P.sem("gv2")
    zeroscr = gio["zeroscr"]
    xout0 = B0_io["xout0"]
    rr.push("G2_")
    zb = rr.sb("z", (128, 1024), BF16)
    P.op("vector", "memset", zb[:], 0.0, inc=[(gv, 1)])
    P.wait("sync", gv, gv.n)
    for k in range(KCH):
        P.op("sync", "dma_start", out=zeroscr[k * 128:(k + 1) * 128, :],
             in_=zb[:], inc=[(gl, 16)])
    P.wait("sync", gl, gl.n)
    P.cond_dma("sync", out=xout0[:, 0:1024], in_=zeroscr, jval=0,
               inc=[(gl, 16)])
    for e in ENGINES:
        P.wait(e, gl, gl.n)
    rr.pop()


def _emit_glue3(P, rr, gio, B1_io):
    cc = P.sem("cc3")
    gl = P.sem("gl3")
    afG, afrot = gio["afG"], gio["afrot"]
    rr.push("G3_")
    P.op("gpsimd", "collective_compute", "AllGather", ALU.bypass,
         replica_groups=RG, ins=[B1_io["af0"].opt()], outs=[afG.opt()],
         inc=[(cc, 1)])
    P.wait("scalar", cc, cc.n)
    glz = gio["glz"]
    P.wait("scalar", glz, glz.n)
    for p in range(4):
        dyn_dma(P, "scalar", gl,
                lambda e, d, _p=p: afrot[:, bass.ds(
                    _cv(e, d, "rotb", lambda jm: (3 - jm) * G) + _p * G, G)],
                afG[p * C:(p + 1) * C, :])
    for e in ENGINES:
        P.wait(e, gl, gl.n)
        P.wait(e, cc, cc.n)
    rr.pop()


def build_fused(upto="D"):
    order = ("A", "g1", "B0", "g2", "B1", "g3", "D")
    keep = order[:order.index(upto) + 1]
    nc = bass.Bass("TRN2", target_bir_lowering=False)
    io = _fused_io(nc)
    with ExitStack() as stack:
        P = Prog(nc, stack)
        rr = R(nc, stack)
        cv = P.sem("cv")
        consts = make_consts(P, rr, cv)
        fx = {"nc": nc, "P": P, "rr": rr, "consts": consts}
        if "A" in keep:
            fx["io"] = io["A"]
            build_launch_A(fx=fx)
        if "g1" in keep:
            _emit_glue1(P, rr, io["G"], io["A"])
        if "B0" in keep:
            fx["io"] = io["B0"]
            build_launch_B(0, fx=fx)
        if "g2" in keep:
            _emit_glue2(P, rr, io["G"], io["B0"])
        if "B1" in keep:
            fx["io"] = io["B1"]
            build_launch_B(1, fx=fx)
        if "g3" in keep:
            _emit_glue3(P, rr, io["G"], io["B1"])
        if "D" in keep:
            fx["io"] = io["D"]
            build_launch_D(fx=fx)
        assert len(P.all_sems) <= 95, len(P.all_sems)
        P.build()
    return nc


def fmpack(v):
    return np.ascontiguousarray(v.reshape(-1, 128).T)


def ct(a):
    return np.ascontiguousarray(a)


def host_prep_A(ins, xT):
    wkT = ct(ins["L0_attn_Wk"].T)
    wvT = ct(ins["L0_attn_Wv"].T)
    wrT = ct(ins["L0_attn_Wr"].T)
    tmP = fmpack(ins["L0_attn_time_mix"])
    tdP = fmpack(ins["L0_attn_time_decay"])
    tfP = fmpack(ins["L0_attn_time_first"])
    in_maps = []
    for c in range(NCORE):
        s, j = divmod(c, 4)
        o0 = j * G
        xhc = np.zeros((C, G + 8), np.float32)
        lo = max(0, o0 - 1)
        xhc[:, 1 - (o0 - lo):1 + G] = xT[s][:, lo:o0 + G]
        in_maps.append({
            "xh": xhc, "wkT": wkT, "wvT": wvT, "wrT": wrT,
            "tmP": tmP, "tdP": tdP, "tfP": tfP,
        })
    return in_maps


def _rotary_tables():
    half = np.arange(0, 64, 2, dtype=np.float64)
    inv_freq = 1.0 / (10000.0 ** (half / 64))
    t_arr = np.arange(T, dtype=np.float64)
    freqs = t_arr[:, None] * inv_freq[None, :]
    freqs2 = np.concatenate([freqs, freqs], 1)
    base = (half + 0.4 * 64) / (1.4 * 64)
    power = (t_arr - T // 2) / 256.0
    scale = base[None, :] ** power[:, None]
    scale2 = np.concatenate([scale, scale], 1)
    cos = np.cos(freqs2)
    sin = np.sin(freqs2)
    CQ = (cos * scale2).T.astype(np.float32)
    sq0 = (sin * scale2).T.astype(np.float32)[:32]
    SQ = np.concatenate([sq0, -sq0], 0)
    CK = (cos / scale2).T.astype(np.float32)
    sk0 = (sin / scale2).T.astype(np.float32)[:32]
    SK = np.concatenate([sk0, -sk0], 0)

    def dup(a):
        return ct(np.concatenate([a, a], 0))

    return dup(CQ), dup(SQ), dup(CK), dup(SK)


def _tab_slice(tab, start, n):
    out = np.zeros((128, n), np.float32)
    lo = max(0, start)
    hi = min(T, start + n)
    if hi > lo:
        out[:, lo - start:hi - start] = tab[:, lo:hi]
    return ct(out)


def _consts_maps():
    identbf = np.eye(128).astype(BF)
    blockones = np.zeros((128, 128), np.float32)
    blockones[:64, :64] = 1.0
    blockones[64:, 64:] = 1.0
    triuA = np.triu(np.full((128, 128), -1e9, np.float32), 1)
    triuB = np.tril(np.full((128, 128), -1e9, np.float32), -1)
    return {"identbf": identbf, "blockones": blockones.astype(BF),
            "triuA": triuA, "triuB": triuB}


def host_prep_B(l, ins, xT, prev, fused=False):
    wqkv = ct(ins["L0_ffn_wqkv"][l].astype(np.float32))
    qs = ins["L0_ffn_q_scale"][l]
    ksc = ins["L0_ffn_k_scale"][l]
    wo = ct(ins["L0_ffn_wo"][l])
    lg = ins["L0_ffn_ln_g"][l]
    lb = ins["L0_ffn_ln_b"][l]
    w1 = ins["L0_ffn_w1"][l]
    b1 = ins["L0_ffn_b1"][l]
    w2 = ins["L0_ffn_w2"][l]
    b2 = ins["L0_ffn_b2"][l]
    w1p = np.zeros((1024, 5632), np.float32)
    w1p[:, :2730] = w1[:, :2730]
    w1p[:, 2816:2816 + 2730] = w1[:, 2730:]
    b1p = np.zeros(5632, np.float32)
    b1p[:2730] = b1[:2730]
    b1p[2816:2816 + 2730] = b1[2730:]
    w2p = np.zeros((2816, 1024), np.float32)
    w2p[:2730] = w2
    w1lm = ct(w1p.reshape(8, 128, 44, 128).transpose(2, 1, 0, 3).reshape(44, 128, 1024))
    w2lm = ct(w2p.reshape(22, 128, 1024))
    b1aP = fmpack(b1p[:2816])
    b1gP = fmpack(b1p[2816:])
    b2P = fmpack(b2)
    lgP = fmpack(lg)
    lbP = fmpack(lb)
    qs2 = np.concatenate([qs, qs]).astype(np.float32)
    qss2 = np.concatenate([qs[32:], qs[:32], qs[32:], qs[:32]]).astype(np.float32)
    ks2 = np.concatenate([ksc, ksc]).astype(np.float32)
    kss2 = np.concatenate([ksc[32:], ksc[:32], ksc[32:], ksc[:32]]).astype(np.float32)
    scal4 = ct(np.stack([qs2, qss2, ks2, kss2], 1))
    CQ, SQ, CK, SK = _rotary_tables()
    cm = _consts_maps()
    if l == 0:
        NIN, XOFF = 3600, 1544
        QLO = XOFF - 1024
    else:
        NIN, XOFF = 3072, 1024
        QLO = XOFF - 512
    QHI = XOFF + G
    Tq = QHI - QLO
    Tk = QHI - (QLO - 512)
    maps = []
    for c in range(NCORE):
        sq, j = divmod(c, 4)
        o0 = j * G
        m = {
            "wqkv": wqkv, "wo": wo, "w1lm": w1lm, "w2lm": w2lm,
            "b1aP": b1aP, "b1gP": b1gP, "b2P": b2P, "lgP": lgP, "lbP": lbP,
            "scal4": scal4,
            "cq": _tab_slice(CQ, o0 + QLO - XOFF, Tq),
            "sq": _tab_slice(SQ, o0 + QLO - XOFF, Tq),
            "ck": _tab_slice(CK, o0 + QLO - 512 - XOFF, Tk),
            "sk": _tab_slice(SK, o0 + QLO - 512 - XOFF, Tk),
            "identbf": cm["identbf"], "blockones": cm["blockones"],
            "triuA": cm["triuA"], "triuB": cm["triuB"],
        }
        KML = (XOFF // 128) * 128 + 128
        tabs = o0 + np.arange(KML) - XOFF
        m["kmask"] = ct(np.where(tabs < 0, -1e9, 0.0)
                        .astype(BF).reshape(1, KML))
        if l == 0:
            xhc = np.zeros((C, NIN), np.float32)
            lo = max(0, o0 - XOFF)
            hi = min(T, o0 - XOFF + NIN)
            xhc[:, lo - (o0 - XOFF):hi - (o0 - XOFF)] = xT[sq][:, lo:hi]
            m["xh2"] = xhc
        elif fused:
            m["iota2d"] = ct(np.tile(np.arange(G, dtype=np.float32), (128, 1)))
            m["woTr"] = ct(ins["L0_attn_Wo"].T)
        else:
            xo = np.asarray(prev["xout0"][c])
            xo = xo.copy()
            tcols = o0 + np.arange(3072) - XOFF
            xo[:, tcols < 0] = np.zeros((C, 1), dtype=xo.dtype)
            m["xin0"] = xo
            m["wkvl"] = prev["wkvl"][c]
            m["wkl"] = prev["wkl"][c]
            m["rbf"] = prev["rbf"][c]
            carrH = np.zeros((C, 8), np.float32)
            for mi in range(4):
                p = j - 1 - mi
                if p >= 0:
                    carrH[:, 2 * mi:2 * mi + 2] = np.asarray(
                        prev["carr"][sq * 4 + p])
            m["carrH"] = ct(carrH)
            m["tdP"] = fmpack(ins["L0_attn_time_decay"])
            m["tfP"] = fmpack(ins["L0_attn_time_first"])
            m["iota2d"] = ct(np.tile(np.arange(G, dtype=np.float32), (128, 1)))
            m["woTr"] = ct(ins["L0_attn_Wo"].T)
        if fused:
            for nmm in ("wqkv", "wo", "w1lm", "w2lm", "b1aP", "b1gP", "b2P",
                        "lgP", "lbP", "scal4", "cq", "sq", "ck", "sk", "kmask"):
                m[f"l{l}_{nmm}"] = m.pop(nmm)
        maps.append(m)
    return maps


_PAIRS = [(0, 1), (0, 2), (0, 3), (1, 2), (1, 3), (2, 3)]


def _make_pe():
    import math as _m
    pos = np.arange(T, dtype=np.float32)[:, None]
    div = np.exp(np.arange(0, 32, 2, dtype=np.float32) *
                 np.float32(-_m.log(10000.0) / 32))
    ang = (pos * div).astype(np.float32)
    pe = np.zeros((T, 32), np.float32)
    pe[:, 0::2] = np.sin(ang)
    pe[:, 1::2] = np.cos(ang)
    return pe


def _qk_variant(WT, heads, comp, sign=None):
    din = WT.shape[0]
    out = np.zeros((din, 512), np.float32)
    for h in range(16):
        t, jj = divmod(h, 4)
        for p in range(6):
            c = WT[:, 4 * h + comp(p)]
            if sign is not None:
                c = c * sign[p]
            out[:, 128 * t + 32 * jj + p] = c
    return ct(out)


def host_prep_D(ins, xT, prev, fused=False):
    W1qT = ins["L1_attn_W1_q"].T.astype(np.float32)
    W2qT = ins["L1_attn_W2_q"].T.astype(np.float32)
    W1kT = ins["L1_attn_W1_k"].T.astype(np.float32)
    W2kT = ins["L1_attn_W2_k"].T.astype(np.float32)
    sig6 = np.array([1, -1, 1, 1, -1, 1], np.float32)
    perm = lambda p: _PAIRS[5 - p]
    wq = [_qk_variant(W1qT, None, lambda p: _PAIRS[p][0]),
          _qk_variant(W1qT, None, lambda p: _PAIRS[p][1]),
          _qk_variant(W2qT, None, lambda p: _PAIRS[p][0]),
          _qk_variant(W2qT, None, lambda p: _PAIRS[p][1])]
    wk = [_qk_variant(W1kT, None, lambda p: perm(p)[0], sig6),
          _qk_variant(W1kT, None, lambda p: perm(p)[1], sig6),
          _qk_variant(W2kT, None, lambda p: perm(p)[0]),
          _qk_variant(W2kT, None, lambda p: perm(p)[1])]
    wvT = ct(ins["L1_attn_Wv"].T)
    bv = ct(ins["L1_attn_bv"].reshape(1, C))
    woT = ct(ins["L1_attn_Wo"].T)
    boP = fmpack(ins["L1_attn_bo"])
    sel128 = np.zeros((128, 128), np.float32)
    for r in range(128):
        for m in range(128):
            if r // 32 == m // 32 and r % 32 < 6:
                sel128[r, m] = 1.0
    wflm = ct(np.stack([ins["L1_ffn_W"][t].T.reshape(8, 128, 1024)
                        for t in range(8)]).reshape(64, 128, 1024))
    bf1 = ct(ins["L1_ffn_b"].astype(np.float32))
    sigsT = ct(ins["L1_ffn_tile_sigs"].T.astype(np.float32))
    peT_full = ct(_make_pe().T)
    identf = np.eye(128, dtype=np.float32)
    ident8 = np.eye(8, dtype=BF)
    triuD = np.tril(np.full((128, 128), -1e9, np.float32), -1)
    af_seq = None if fused else [
        np.concatenate([np.asarray(prev["af0"][4 * sq + jj])
                        for jj in range(4)], axis=1) for sq in range(2)]
    maps = []
    for c in range(NCORE):
        sq, j = divmod(c, 4)
        o0 = j * G
        Pfx = o0 + G
        xrot = np.zeros((C, T), np.float32)
        xrot[:, T - Pfx:] = xT[sq][:, :Pfx]
        km = np.zeros((1, T), np.float32)
        km[0, :T - Pfx] = -1e9
        m = {
            "xTf": xrot,
            "wvT1": wvT, "bv1": bv, "woT1": woT, "boP1": boP,
            "sel128": sel128.astype(BF), "wflm": wflm, "bf1": bf1,
            "sigsT": sigsT, "peT": ct(peT_full[:, o0:o0 + G]),
            "identf": identf, "ident8": ident8, "triuD": triuD,
            "kmaskD": km.astype(BF),
            "ones1r": np.ones((1, 4 * G), BF),
        }
        for v in range(4):
            m[f"wq{v}"] = wq[v]
            m[f"wk{v}"] = wk[v]
        if not fused:
            afrot = np.zeros((C, T), np.float32)
            afrot[:, T - Pfx:] = af_seq[sq][:, :Pfx]
            m["afT"] = afrot
        maps.append(m)
    return maps


def host_prep_fused(ins, xT):
    mA = host_prep_A(ins, xT)
    mB0 = host_prep_B(0, ins, xT, {}, fused=True)
    mB1 = host_prep_B(1, ins, xT, {}, fused=True)
    mD = host_prep_D(ins, xT, {}, fused=True)
    maps = []
    for c in range(NCORE):
        m = {}
        for part in (mA[c], mB0[c], mB1[c], mD[c]):
            m.update(part)
        maps.append(m)
    return maps


_NC_FUSED = None


def kernel(**inputs):
    global _NC_FUSED
    ins = {k: np.asarray(v) for k, v in inputs.items()}
    xT = ct(ins["L0_attn_x"].transpose(0, 2, 1))
    if _NC_FUSED is None:
        _NC_FUSED = build_fused()
    res = run_bass_kernel_spmd(_NC_FUSED, host_prep_fused(ins, xT),
                               core_ids=list(range(NCORE)))
    out = np.empty((B, T, C), np.float32)
    for c in range(NCORE):
        sq, j = divmod(c, 4)
        out[sq, j * G:(j + 1) * G, :] = np.asarray(res.results[c]["outT"]).T
    return out


